# version 10
# baseline (speedup 1.0000x reference)
import os
from contextlib import ExitStack

import numpy as np
import ml_dtypes

import concourse.bass as bass
import concourse.mybir as mybir
from concourse.bass_types import DynSlice
from concourse.bass_utils import run_bass_kernel_spmd

F32 = mybir.dt.float32
BF16 = mybir.dt.bfloat16
I8 = mybir.dt.int8
I32 = mybir.dt.int32
AF = mybir.ActivationFunctionType
ALU = mybir.AluOpType
AX = mybir.AxisListType.X
SP = mybir.EngineType.SP

BF = ml_dtypes.bfloat16

_STAGE = "full"
_PARANOID = 0

D = 1024
INNER = 4096
SEQ = 4096
NSEQ = 4
NCORE = 8
NT = 16
NB = 4
TPB = 4
LN_EPS = 1e-5

ENGS = ("sync", "gpsimd", "tensor", "vector", "scalar")


def _rope_consts():
    inv = np.float32(10000.0) ** (-(np.arange(0, D, 2, dtype=np.float32) / np.float32(D)))
    emb = np.repeat(inv, 2)
    ang = np.float32(SEQ) * emb
    sin = np.sin(ang).astype(np.float32)
    cos = np.cos(ang).astype(np.float32)
    idx = np.arange(D)
    partner = np.where(idx < D // 2, 2 * idx + 1, 2 * idx - D)
    sin_eff = np.where(idx < D // 2, -sin, sin).astype(np.float32)
    return cos, sin_eff, partner


class Sem:
    def __init__(self, nc, name):
        self.h = nc.alloc_semaphore(name)
        self.n = 0


class Sched:
    def __init__(self, nc, block, sems=None):
        self.nc = nc
        self.block = block
        self.q = {e: [] for e in ENGS}
        self.sems = sems

    def op(self, eng, fn, sem=None, k=1, waits=(), chain=True):
        waits = list(waits)
        if chain and self.sems is not None and eng in ("vector", "scalar"):
            own = self.sems["dv" if eng == "vector" else "ac"]
            if own.n > 0 and (sem is not own or True):
                waits.append((own, own.n))
            if sem is None:
                sem = own
        waits = tuple((s.h, int(c)) for (s, c) in waits if c)
        if sem is not None:
            sem.n += k
        h = sem.h if sem is not None else None

        def run(e, fn=fn, waits=waits, h=h, kk=k):
            for (sh, c) in waits:
                e.wait_ge(sh, c)
            inst = fn(e)
            if h is not None:
                inst.then_inc(h, kk)

        self.q[eng].append(run)
        return sem.n if sem is not None else None

    def wait(self, eng, sem, n):
        self.q[eng].append(lambda e, sh=sem.h, n=int(n): e.wait_ge(sh, n))

    def dma(self, eng, fn, sem, waits=()):
        return self.op(eng, fn, sem=sem, k=16, waits=waits)

    def flush(self):
        for eng in ENGS:
            lst = self.q[eng]
            if not lst:
                continue

            def body(e, lst=lst):
                for f in lst:
                    f(e)

            getattr(self.block, eng)(body)
        self.q = {e: [] for e in ENGS}

    def barrier(self, bar):
        self.op("sync", lambda e: e.nop(), sem=bar, k=1)
        self.op("gpsimd", lambda e: e.engine_nop(), sem=bar, k=1)
        self.op("vector", lambda e: e.engine_nop(), sem=bar, k=1)
        self.op("scalar", lambda e: e.activation(
            self.scratch11[:, :], self.scratch11[:, :], AF.Identity), sem=bar, k=1)
        self.op("tensor", lambda e: e.nop(), sem=bar, k=1)
        t = bar.n
        for eng in ENGS:
            self.wait(eng, bar, t)

    def maybe_bar(self, bar, level):
        if _PARANOID >= level:
            self.barrier(bar)



def build_program():
    nc = bass.Bass("TRN2", target_bir_lowering=False, debug=False)

    x_in = nc.dram_tensor("x", [2048, D], F32, kind="ExternalInput")
    g_in = nc.dram_tensor("g_t", [128, D], F32, kind="ExternalInput")
    b_in = nc.dram_tensor("b_t", [128, D], F32, kind="ExternalInput")
    wqm_in = nc.dram_tensor("wq_m", [D, 128], F32, kind="ExternalInput")
    wqr_in = nc.dram_tensor("wq_r", [D, 128], F32, kind="ExternalInput")
    wkm_in = nc.dram_tensor("wk_m", [D, 128], F32, kind="ExternalInput")
    wkr_in = nc.dram_tensor("wk_r", [D, 128], F32, kind="ExternalInput")
    wvm_in = nc.dram_tensor("wv_m", [D, 128], F32, kind="ExternalInput")
    cosq_in = nc.dram_tensor("cos_q", [128, 128], F32, kind="ExternalInput")
    sinq_in = nc.dram_tensor("sin_q", [128, 128], F32, kind="ExternalInput")
    cosk_in = nc.dram_tensor("cos_k", [128, 128], F32, kind="ExternalInput")
    sink_in = nc.dram_tensor("sin_k", [128, 128], F32, kind="ExternalInput")
    wout_in = nc.dram_tensor("w_out", [D, D], F32, kind="ExternalInput")
    wgT_in = nc.dram_tensor("wgT", [D, INNER], F32, kind="ExternalInput")
    wuT_in = nc.dram_tensor("wuT", [D, INNER], F32, kind="ExternalInput")
    wdT_in = nc.dram_tensor("wdT", [INNER, D], F32, kind="ExternalInput")
    qks_in = nc.dram_tensor("qks", [1, 1], F32, kind="ExternalInput")
    rank_in = nc.dram_tensor("rank", [1, 1], I32, kind="ExternalInput")
    out = nc.dram_tensor("out", [2048, D], F32, kind="ExternalOutput")

    wgq_d = nc.dram_tensor("wgq_d", [8, 128, 8, 512], BF16)
    wuq_d = nc.dram_tensor("wuq_d", [8, 128, 8, 512], BF16)
    wdq_d = nc.dram_tensor("wdq_d", [INNER, D], BF16)
    cc1a_in = nc.dram_tensor("cc1a_in", [8, 128, 1024], BF16)
    cc1b_in = nc.dram_tensor("cc1b_in", [8, 128, 1024], BF16)
    cc1a_out = nc.dram_tensor("cc1a_out", [NCORE, 8, 128, 1024], BF16)
    cc1b_out = nc.dram_tensor("cc1b_out", [NCORE, 8, 128, 1024], BF16)
    cc2a_in = nc.dram_tensor("cc2a_in", [NCORE, 128, 8 * 128], BF16)
    cc2b_in = nc.dram_tensor("cc2b_in", [NCORE, 128, 8 * 128], BF16)
    cc2a_out = nc.dram_tensor("cc2a_out", [NCORE, NCORE, 128, 8 * 128], BF16)
    cc2b_out = nc.dram_tensor("cc2b_out", [NCORE, NCORE, 128, 8 * 128], BF16)

    I128 = nc.inline_tensor(np.eye(128, dtype=BF), name="I128")
    triu_np = (np.arange(128)[:, None] <= np.arange(128)[None, :]).astype(BF)
    TRIU = nc.inline_tensor(triu_np, name="TRIU")
    ind_sq_np = np.zeros((128, 2), dtype=BF)
    ind_sq_np[0:64, 0] = 1
    ind_sq_np[64:128, 1] = 1
    IND_SQ = nc.inline_tensor(ind_sq_np, name="IND_SQ")
    ind_bc_np = np.zeros((2, 128), dtype=BF)
    ind_bc_np[0, 0:64] = 1
    ind_bc_np[1, 64:128] = 1
    IND_BC = nc.inline_tensor(ind_bc_np, name="IND_BC")
    ONES_1x64 = nc.inline_tensor(np.ones((1, 64), dtype=np.float32), name="ONES_1x64")
    ONES_R_F = nc.inline_tensor(np.ones((1, 128), dtype=np.float32), name="ONES_R_F")
    ONES_C_F = nc.inline_tensor(np.ones((128, 1), dtype=np.float32), name="ONES_C_F")

    semnames = ("dS", "dG", "pe", "ac", "dv", "cc", "bar",
                "wq0", "wq1", "ts0", "ts1", "wg0", "wg1", "wu0", "wu1",
                "wd0", "wd1", "h2f0", "h2f1", "h2g0", "h2g1", "of0", "of1")
    sems = {n: Sem(nc, n) for n in semnames}
    dS, dG, pe, ac, dv, cc, bar = (sems[n] for n in
                                   ("dS", "dG", "pe", "ac", "dv", "cc", "bar"))

    with ExitStack() as stack:
        block = stack.enter_context(nc.Block())
        S = Sched(nc, block, sems=sems)
        S.scratch11 = stack.enter_context(nc.sbuf_tensor("bar_scr", [1, 1], F32))

        def sb(name, shape, dt=F32):
            return stack.enter_context(nc.sbuf_tensor(name, shape, dt))

        resid = sb("resid", [128, NT * D])
        stats = sb("stats", [128, 160])
        cols = sb("cols", [128, 16])
        sc11 = sb("sc11", [1, 8])
        i128_sb = sb("i128_sb", [128, 128], BF16)
        triu_sb = sb("triu_sb", [128, 128], BF16)
        ind_sq_sb = sb("ind_sq_sb", [128, 2], BF16)
        ind_bc_sb = sb("ind_bc_sb", [2, 128], BF16)
        ones64_sb = sb("ones64_sb", [1, 64], F32)
        onesrf_sb = sb("onesrf_sb", [1, 128], F32)
        onescf_sb = sb("onescf_sb", [128, 1], F32)
        rank_sb = sb("rank_sb", [1, 1], I32)
        qks_sb = sb("qks_sb", [1, 1], F32)

        consts = dict(resid=resid, stats=stats, cols=cols, sc11=sc11,
                      i128=i128_sb, triu=triu_sb, ind_sq=ind_sq_sb,
                      ind_bc=ind_bc_sb, ones64=ones64_sb, onesrf=onesrf_sb,
                      onescf=onescf_sb, rank=rank_sb, qks=qks_sb, out=out)

        S.op("vector", lambda e: e.memset(cols[:, 8:9], LN_EPS), sem=sems["dv"])
        for t in range(NT):
            S.dma("sync", lambda e, t=t: e.dma_start(
                out=resid[:, t * D:(t + 1) * D],
                in_=x_in.ap().rearrange("(t p) c -> t p c", p=128)[t]), dS)
        for dst, src in ((i128_sb, I128), (triu_sb, TRIU), (ind_sq_sb, IND_SQ),
                         (ind_bc_sb, IND_BC), (ones64_sb, ONES_1x64),
                         (onesrf_sb, ONES_R_F), (onescf_sb, ONES_C_F),
                         (rank_sb, rank_in), (qks_sb, qks_in)):
            S.dma("sync", lambda e, d=dst, s=src: e.dma_start(out=d[:, :], in_=s[:, :]), dS)
        n_x = dS.n
        n_consts = dS.n

        with nc.psum_tensor("ps_misc", [128, 8], F32) as ps_misc:
            def serial0(eng, fn, sem, extra=()):
                S.op(eng, fn, sem=sem,
                     waits=list(extra) + [(pe, pe.n), (ac, ac.n), (dv, dv.n)])

            serial0("scalar", lambda e: e.activation(sc11[:, 0:1], qks_sb[:, :],
                                                     AF.Abs), ac,
                    extra=[(dS, n_consts)])
            serial0("tensor", lambda e: e.matmul(ps_misc[:, 0:1], onesrf_sb[:, :],
                                                 qks_sb[:, :], start=True, stop=True),
                    pe, extra=[(dS, n_consts)])
            serial0("tensor", lambda e: e.matmul(ps_misc[:, 1:2], onesrf_sb[:, :],
                                                 sc11[:, 0:1], start=True, stop=True),
                    pe)
            serial0("scalar", lambda e: e.activation(cols[:, 6:7], ps_misc[:, 0:1],
                                                     AF.Copy), ac)
            serial0("scalar", lambda e: e.activation(cols[:, 7:8], ps_misc[:, 1:2],
                                                     AF.Copy, scale=-1.0), ac)

            mats = ((wgT_in, wgq_d, 8, INNER, 0, 3, 1.0 / 127.0),
                    (wuT_in, wuq_d, 8, INNER, 1, 4, 1.0 / 127.0),
                    (wdT_in, wdq_d, 32, D, 2, 5, 0.5 / 127.0))
            with ExitStack() as wst:
                s0dump = wst.enter_context(
                    nc.sbuf_tensor("s0_dump", [128, D], F32))
                B = _WQBufs(wst, nc, sems)
                absc = [wst.enter_context(
                    nc.sbuf_tensor(f"s0_ac{m}", [128, mats[m][2]], F32))
                    for m in range(3)]
                csum = [wst.enter_context(
                    nc.sbuf_tensor(f"s0_cs{m}", [128, 1], F32)) for m in range(3)]

                n_sacc_dv, n_sacc_ac = _ln_stats_acc(nc, S, sems, consts, s0dump,
                                                     (dS, n_x))
                n_sc = []
                for (w_in, wq_d, nck, wid, cs_, ci_, im_), am, cm in zip(
                        mats, absc, csum):
                    _wq_pass1(nc, S, sems, B, w_in, nck, wid, am)
                    n_sc.append(_wq_chain(nc, S, sems, am, cm, nck, wid, cols,
                                          cs_, ci_, im_, sc11, onesrf_sb,
                                          onescf_sb, ps_misc))
                n_stats_f1 = _ln_stats_tail(nc, S, sems, consts, n_sacc_ac)
                for (w_in, wq_d, nck, wid, cs_, ci_, im_), ns in zip(mats, n_sc):
                    _wq_pass2(nc, S, sems, B, w_in, wq_d, nck, wid, cols, cs_, ns)
                S.wait("sync", sems["ts0"], sems["ts0"].n)
                S.wait("sync", sems["ts1"], sems["ts1"].n)
                S.barrier(bar)
                S.flush()

        if _STAGE == "dbg_wq":
            S.dma("gpsimd", lambda e: e.dma_start(
                out=out.ap()[0:512, :].rearrange("(r s) c -> r s c", s=4),
                in_=wgq_d.ap()[0:128, :].rearrange("r (s c) -> r s c", c=1024)),
                dG)
            S.dma("sync", lambda e: e.dma_start(
                out=out.ap()[512:640, 0:16], in_=cols[:, :]), dS)
            S.dma("sync", lambda e: e.dma_start(
                out=out.ap()[640:641, 0:8], in_=sc11[:, :]), dS)
            S.wait("gpsimd", dG, dG.n)
            S.wait("gpsimd", dS, dS.n)
            S.flush()
            return nc
        if _STAGE in ("dbg_stats", "dbg_red"):
            with nc.sbuf_tensor("dbg_dump", [128, D], F32) as ddump, \
                 nc.sbuf_tensor("dbg_snap", [128, 112], F32) as dsnap:
                consts["snap"] = dsnap
                _ln_stats(nc, S, sems, consts, ddump, (dS, n_x),
                          chain=(_STAGE == "dbg_stats"))
                S.barrier(bar)
                S.dma("sync", lambda e: e.dma_start(
                    out=out.ap()[0:128, 0:160], in_=consts["stats"][:, :]), dS)
                S.dma("sync", lambda e: e.dma_start(
                    out=out.ap()[128:256, 0:112], in_=dsnap[:, :]), dS)
                S.wait("gpsimd", dS, dS.n)
                S.flush()
                consts.pop("snap")
            return nc

        if _STAGE == "dbg_xq":
            with nc.sbuf_tensor("dbg_dump", [128, D], F32) as ddump, \
                 nc.sbuf_tensor("dbg_x8", [128, D], I8) as x8, \
                 nc.sbuf_tensor("dbg_xb", [128, D], BF16) as xb:
                n_st = _ln_stats(nc, S, sems, consts, ddump, (dS, n_x))
                stats = consts["stats"]
                for t in range(NT):
                    S.op("vector", lambda e, t=t: e.tensor_scalar(
                        x8[:, :], resid[:, t * D:(t + 1) * D],
                        stats[:, 0 + t:1 + t], stats[:, 16 + t:17 + t],
                        ALU.subtract, ALU.mult),
                        sem=sems["dv"], waits=[(sems["dv"], n_st)] if t == 0 else [])
                    S.op("vector", lambda e: e.tensor_copy(xb[:, :], x8[:, :]),
                         sem=sems["dv"])
                    n_q = sems["dv"].n
                    S.dma("gpsimd", lambda e, t=t: e.dma_start(
                        out=out.ap().rearrange("(t p) c -> t p c", p=128)[t],
                        in_=xb[:, :]), dG, waits=[(sems["dv"], n_q)])
                    S.wait("vector", dG, dG.n)
                S.wait("gpsimd", dG, dG.n)
                S.flush()
            return nc

        with ExitStack() as p2st:
            p2_accs, p2_emit, p2_state = _p2_setup(nc, S, sems, consts,
                                                    p2st, g_in, b_in,
                                                    cc1a_in, cc1b_in,
                                                    cc1a_out, cc1b_out)

            def post_block(b, ffnctx):
                p2_accs(b, ffnctx)
                if b == 1:
                    p2_emit(0, ffnctx)
                elif b == 3:
                    p2_emit(1, ffnctx)

            _ffn_phase(nc, S, sems, consts, wgq_d, wuq_d, wdq_d,
                       wait_x=(dS, n_x), tag="f1", stats_n=n_stats_f1,
                       post_block=post_block)
        ccA_done, ccB_done = p2_state["cc_done"]
        if _STAGE == "dbg_raw":
            S.flush()
            return nc
        S.barrier(bar)
        S.flush()

        if _STAGE == "ffn1":
            _store_out(nc, S, sems, resid, out)
            S.flush()
            return nc

        with ExitStack() as astack:
            def asb(name, shape, dt=F32):
                return astack.enter_context(nc.sbuf_tensor(name, shape, dt))

            qnf = asb("qnf", [128, NSEQ * SEQ], BF16)
            knf = asb("knf", [128, NSEQ * SEQ], BF16)
            vres = asb("vres", [128, 128 * 130], BF16)

            _p3_qkv(nc, S, astack, sems, consts,
                    (wqm_in, wqr_in, wkm_in, wkr_in, wvm_in),
                    (cosq_in, sinq_in, cosk_in, sink_in),
                    cc1a_out, ccA_done, cc1b_out, ccB_done, qnf, knf, vres)
            if _STAGE in ("dbg_wro", "dbg_h2g"):
                S.flush()
                return nc
            S.barrier(bar)
            S.flush()

            if _STAGE in ("dbg_qnf", "dbg_knf", "dbg_vres"):
                src_t = {"dbg_qnf": qnf, "dbg_knf": knf, "dbg_vres": vres}[_STAGE]
                ncol = min(src_t.shape[1], 16384)
                S.dma("gpsimd", lambda e: e.dma_start(
                    out=out.ap().rearrange("(t p) c -> p t c", p=128)
                    [:, 0:ncol // D, :],
                    in_=src_t.ap()[:, 0:(ncol // D) * D]
                    .rearrange("p (t c) -> p t c", c=D)), sems["dG"])
                S.wait("gpsimd", sems["dG"], sems["dG"].n)
                S.flush()
                return nc

            cc2h = dict(a_in=cc2a_in, b_in=cc2b_in, a_out=cc2a_out,
                        ccA_done=None)
            if _STAGE == "dbg_p0":
                try:
                    _p4_attn(nc, S, astack, sems, consts, qnf, knf, vres, cc2h)
                except StopIteration:
                    pass
                return nc
            _p4_attn(nc, S, astack, sems, consts, qnf, knf, vres, cc2h)
            if _STAGE == "dbg_of":
                S.wait("gpsimd", sems["of0"], sems["of0"].n)
                S.wait("gpsimd", sems["of1"], sems["of1"].n)
                for r in range(NCORE):
                    for sl in range(2):
                        S.dma("gpsimd", lambda e, r=r, sl=sl: e.dma_start(
                            out=out.ap().rearrange("(a p) c -> a p c", p=128)
                            [r * 2 + sl],
                            in_=(cc2a_in if sl == 0 else cc2b_in).ap()[r]),
                            sems["dG"])
                S.wait("gpsimd", sems["dG"], sems["dG"].n)
                S.flush()
                return nc
            S.op("gpsimd", lambda e: e.collective_compute(
                "AllGather", ALU.bypass, replica_groups=[list(range(NCORE))],
                ins=[cc2b_in.ap().opt()], outs=[cc2b_out.ap().opt()]),
                sem=cc, waits=[(sems["of0"], sems["of0"].n),
                               (sems["of1"], sems["of1"].n)])
            ccB_done = cc.n
            S.flush()

        _p5_oproj(nc, S, sems, consts, cc2a_out, cc2h["ccA_done"],
                  cc2b_out, ccB_done, wout_in)
        S.barrier(bar)
        S.flush()

        if _STAGE == "attn":
            _store_out(nc, S, sems, resid, out)
            S.flush()
            return nc

        _ffn_phase(nc, S, sems, consts, wgq_d, wuq_d, wdq_d,
                   wait_x=None, tag="f2")
        S.barrier(bar)
        S.flush()

        _store_out(nc, S, sems, resid, out)
        S.flush()

    return nc



class _WQBufs:

    def __init__(self, st, nc, sems):
        self.wbuf = [st.enter_context(nc.sbuf_tensor(f"s0wb{i}", [128, INNER], F32))
                     for i in range(2)]
        self.t8 = st.enter_context(nc.sbuf_tensor("s0t8", [128, INNER], I8))
        self.tern = [st.enter_context(nc.sbuf_tensor(f"s0tn{i}", [128, INNER], BF16))
                     for i in range(2)]
        self.q = [("sync", sems["wq0"]), ("gpsimd", sems["wq1"])]
        self.sq = [("sync", sems["ts0"]), ("gpsimd", sems["ts1"])]
        self.last_read = [None, None]
        self.last_store = [None, None]


def _wq_pass1(nc, S, sems, B, w_in, nchunk, width, abscols):
    dv = sems["dv"]
    wv = w_in.ap().rearrange("(k p) w -> k p w", p=128)
    for k in range(nchunk):
        s = k % 2
        eng, sem = B.q[s]
        w = [B.last_read[s]] if B.last_read[s] else []
        S.dma(eng, lambda e, k=k, s=s: e.dma_start(
            out=B.wbuf[s][:, 0:width], in_=wv[k]), sem, waits=w)
        n_load = sem.n
        S.op("vector", lambda e, k=k, s=s: e.reduce_sum(
            abscols[:, k:k + 1], B.wbuf[s][:, 0:width], AX,
            apply_absolute_value=True), sem=dv, waits=[(sem, n_load)])
        B.last_read[s] = (dv, dv.n)


def _wq_chain(nc, S, sems, abscols, colsum, nchunk, width, cols, col_s, col_inv,
              inv_mult, sc11, onesrf, onescf, ps_misc):
    pe, ac, dv = sems["pe"], sems["ac"], sems["dv"]

    def serial(eng, fn, sem):
        S.op(eng, fn, sem=sem,
             waits=[(pe, pe.n), (ac, ac.n), (dv, dv.n)])

    serial("vector", lambda e: e.reduce_sum(colsum[:, :], abscols[:, 0:nchunk],
                                            AX), dv)
    serial("tensor", lambda e: e.matmul(ps_misc[0:1, 2:3], colsum[:, :],
                                        onescf[:, :], start=True, stop=True), pe)
    serial("scalar", lambda e: e.activation(sc11[:, 1:2], ps_misc[0:1, 2:3],
                                            AF.Copy,
                                            scale=1.0 / (128 * nchunk * width)),
           ac)
    serial("vector", lambda e: e.tensor_scalar_max(sc11[:, 2:3], sc11[:, 1:2],
                                                   1e-5), dv)
    serial("vector", lambda e: e.reciprocal(sc11[:, 3:4], sc11[:, 2:3]), dv)
    serial("vector", lambda e: e.tensor_single_scalar(sc11[:, 4:5], sc11[:, 2:3],
                                                      inv_mult, ALU.mult), dv)
    serial("tensor", lambda e: e.matmul(ps_misc[:, 3:4], onesrf[:, :],
                                        sc11[:, 3:4], start=True, stop=True), pe)
    serial("tensor", lambda e: e.matmul(ps_misc[:, 4:5], onesrf[:, :],
                                        sc11[:, 4:5], start=True, stop=True), pe)
    serial("scalar", lambda e: e.activation(cols[:, col_s:col_s + 1],
                                            ps_misc[:, 3:4], AF.Copy), ac)
    serial("scalar", lambda e: e.activation(cols[:, col_inv:col_inv + 1],
                                            ps_misc[:, 4:5], AF.Copy), ac)
    return ac.n


def _wq_pass2(nc, S, sems, B, w_in, wq_d, nchunk, width, cols, col_s, n_scale):
    ac, dv = sems["ac"], sems["dv"]
    wv = w_in.ap().rearrange("(k p) w -> k p w", p=128)
    if len(wq_d.shape) == 4:
        wqv = [wq_d.ap()[:, :, k, :].rearrange("ic p j -> p ic j")
               for k in range(nchunk)]
    else:
        w4 = wq_d.ap().rearrange("(k p) w -> k p w", p=128)
        wqv = [w4[k] for k in range(nchunk)]

    for k in range(nchunk):
        s = k % 2
        eng, sem = B.q[s]
        w = [B.last_read[s]] if B.last_read[s] else []
        S.dma(eng, lambda e, k=k, s=s: e.dma_start(
            out=B.wbuf[s][:, 0:width], in_=wv[k]), sem, waits=w)
        n_load = sem.n
        wts = [(sem, n_load)]
        if k < 2:
            wts.append((ac, n_scale))
        S.op("vector", lambda e, s=s: e.tensor_single_scalar(
            B.t8[:, 0:width], B.wbuf[s][:, 0:width], cols[:, col_s:col_s + 1],
            ALU.mult), sem=dv, waits=wts)
        B.last_read[s] = (dv, dv.n)
        w2 = [B.last_store[s]] if B.last_store[s] else []
        S.op("vector", lambda e, s=s: e.tensor_scalar(
            B.tern[s][:, 0:width], B.t8[:, 0:width], 1.0, -1.0, ALU.min, ALU.max),
            sem=dv, waits=w2)
        n_tern = dv.n
        seng, ssem = B.sq[s]
        if len(wq_d.shape) == 4:
            S.dma(seng, lambda e, k=k, s=s: e.dma_start(
                out=wqv[k],
                in_=B.tern[s].ap().rearrange("p (ic j) -> p ic j", j=512)),
                ssem, waits=[(dv, n_tern)])
        else:
            S.dma(seng, lambda e, k=k, s=s: e.dma_start(
                out=wqv[k], in_=B.tern[s][:, 0:width]), ssem,
                waits=[(dv, n_tern)])
        B.last_store[s] = (ssem, ssem.n)



def _ln_stats_acc(nc, S, sems, consts, dump, wait_x):
    dS, pe, ac, dv = (sems[n] for n in ("dS", "pe", "ac", "dv"))
    resid, stats, cols = consts["resid"], consts["stats"], consts["cols"]

    S.op("scalar", lambda e: e.preload_act_table(AF.Sqrt), sem=ac)

    sumc = stats[:, 96:112]
    sqc = stats[:, 112:128]
    rmx = stats[:, 128:144]
    rmn = stats[:, 144:160]
    for t in range(NT):
        w = [wait_x] if (wait_x and t == 0) else []
        S.op("scalar", lambda e, t=t: e.activation(
            dump[:, 0:D], resid[:, t * D:(t + 1) * D], AF.Copy,
            accum_out=sumc[:, t:t + 1]), sem=ac, waits=w)
        S.op("scalar", lambda e, t=t: e.activation(
            dump[:, 0:D], resid[:, t * D:(t + 1) * D], AF.Square,
            accum_out=sqc[:, t:t + 1]), sem=ac)
        S.op("vector", lambda e, t=t: e.reduce_max(
            rmx[:, t:t + 1], resid[:, t * D:(t + 1) * D], AX),
            sem=dv, waits=([wait_x] if (wait_x and t == 0) else []))
        S.op("vector", lambda e, t=t: e.tensor_reduce(
            rmn[:, t:t + 1], resid[:, t * D:(t + 1) * D], AX, ALU.min), sem=dv)
    return dv.n, ac.n


def _ln_stats(nc, S, sems, consts, dump, wait_x, chain=True):
    n_dv, n_acc = _ln_stats_acc(nc, S, sems, consts, dump, wait_x)
    if not chain:
        return n_dv
    return _ln_stats_tail(nc, S, sems, consts, n_acc)


def _ln_stats_tail(nc, S, sems, consts, n_acc):
    dS, pe, ac, dv = (sems[n] for n in ("dS", "pe", "ac", "dv"))
    resid, stats, cols = consts["resid"], consts["stats"], consts["cols"]
    sumc = stats[:, 96:112]
    sqc = stats[:, 112:128]
    rmx = stats[:, 128:144]
    rmn = stats[:, 144:160]

    mu = stats[:, 0:16]
    a_q = stats[:, 16:32]
    c_g = stats[:, 32:48]
    c_u = stats[:, 48:64]
    t1 = stats[:, 64:80]
    t2 = stats[:, 80:96]

    S.op("vector", lambda e: e.tensor_single_scalar(mu[:, :], sumc[:, :], 1.0 / D,
                                                    ALU.mult),
         sem=dv, waits=[(ac, n_acc)])
    S.op("vector", lambda e: e.tensor_single_scalar(sqc[:, :], sqc[:, :], 1.0 / D,
                                                    ALU.mult), sem=dv)
    S.op("vector", lambda e: e.tensor_mul(t1[:, :], mu[:, :], mu[:, :]), sem=dv)
    S.op("vector", lambda e: e.tensor_sub(sqc[:, :], sqc[:, :], t1[:, :]), sem=dv)
    n_var = dv.n
    S.op("scalar", lambda e: e.activation(t1[:, :], sqc[:, :], AF.Sqrt,
                                          bias=cols[:, 8:9]),
         sem=ac, waits=[(dv, n_var)])
    n_std = ac.n
    S.op("vector", lambda e: e.reciprocal(t1[:, :], t1[:, :]),
         sem=dv, waits=[(ac, n_std)])
    n_rstd = dv.n
    snap = consts.get("snap")

    def snp(j):
        if snap is not None:
            S.op("vector", lambda e, j=j: e.tensor_copy(
                snap[:, 16 * j:16 * j + 16],
                stats[:, 128:144] if j < 5 else (stats[:, 80:96] if j == 5
                                                 else stats[:, 16:32])), sem=dv)

    S.op("vector", lambda e: e.tensor_sub(rmx[:, :], rmx[:, :], mu[:, :]),
         sem=dv, waits=[(dv, n_rstd)])
    snp(0)
    S.op("vector", lambda e: e.tensor_sub(rmn[:, :], mu[:, :], rmn[:, :]), sem=dv)
    S.op("vector", lambda e: e.tensor_max(rmx[:, :], rmx[:, :], rmn[:, :]), sem=dv)
    snp(1)
    S.op("vector", lambda e: e.tensor_mul(rmx[:, :], rmx[:, :], t1[:, :]), sem=dv)
    snp(2)
    S.op("vector", lambda e: e.tensor_scalar_max(rmx[:, :], rmx[:, :], 1e-5), sem=dv)
    snp(3)
    snp(4)
    S.op("vector", lambda e: e.reciprocal(t2[:, :], rmx[:, :]), sem=dv)
    snp(5)
    S.op("vector", lambda e: e.scalar_tensor_tensor(
        a_q[:, :], t2[:, :], 127.0, t1[:, :], ALU.mult, ALU.mult), sem=dv)
    snp(6)
    S.op("vector", lambda e: e.tensor_single_scalar(
        c_g[:, :], rmx[:, :], cols[:, 3:4], ALU.mult), sem=dv)
    S.op("vector", lambda e: e.tensor_single_scalar(
        c_u[:, :], rmx[:, :], cols[:, 4:5], ALU.mult), sem=dv)
    return dv.n



def _ffn_phase(nc, S, sems, consts, wgq_d, wuq_d, wdq_d, wait_x, tag,
               stats_n=None, post_block=None):
    dS, pe, ac, dv = (sems[n] for n in ("dS", "pe", "ac", "dv"))
    resid, stats, cols, i128_sb = (consts["resid"], consts["stats"],
                                   consts["cols"], consts["i128"])
    mu = stats[:, 0:16]
    a_q = stats[:, 16:32]
    c_g = stats[:, 32:48]
    c_u = stats[:, 48:64]
    hcd = stats[:, 96:112]

    wgv = wgq_d.ap().rearrange("ic p kk j -> ic p (kk j)")
    wuv = wuq_d.ap().rearrange("ic p kk j -> ic p (kk j)")
    wdv = wdq_d.ap().rearrange("(jc p) (h w) -> jc h p w", p=128, w=512)

    with ExitStack() as fst:
        def fsb(name, shape, dt):
            return fst.enter_context(nc.sbuf_tensor(f"{tag}_{name}", shape, dt))

        wg_sb = [fsb(f"wg{i}", [128, 8 * 512], BF16) for i in range(2)]
        wu_sb = [fsb(f"wu{i}", [128, 8 * 512], BF16) for i in range(2)]
        wd_sb = [fsb(f"wd{i}", [128, 512], BF16) for i in range(2)]
        xq8 = fsb("xq8", [128, D], I8)
        xqb = fsb("xqb", [128, D], BF16)
        xqT = [fsb(f"xqT{i}", [128, D], BF16) for i in range(TPB)]
        silu_s = [fsb(f"silu{i}", [128, 512], F32) for i in range(2)]
        gq8 = fsb("gq8", [128, INNER], I8)
        mcol = fsb("mcol", [128, 8], F32)
        dump = fsb("dump", [128, D], F32)

        big = fsb("big", [128, TPB * 16384], mybir.dt.uint8)
        base = nc.lookup_mloc(big).addr
        raw_s = [nc.alloc_sbuf_tensor_at(f"{tag}_raw{i}", [128, INNER], F32,
                                         offset=base + i * 16384)
                 for i in range(TPB)]
        gqb_s = [nc.alloc_sbuf_tensor_at(f"{tag}_gqb{i}", [128, INNER], BF16,
                                         offset=base + i * 16384)
                 for i in range(TPB)]
        gqT_s = [nc.alloc_sbuf_tensor_at(f"{tag}_gqT{i}", [128, INNER], BF16,
                                         offset=base + i * 16384 + 8192)
                 for i in range(TPB)]

        if stats_n is None:
            n_stats = _ln_stats(nc, S, sems, consts, dump, wait_x)
        else:
            n_stats = stats_n
        S.maybe_bar(sems["bar"], 1)
        S.op("scalar", lambda e: e.preload_act_table(AF.Silu), sem=ac)

        with nc.psum_tensor(f"{tag}_ptr", [128, 1024], F32) as ptr, \
             nc.psum_tensor(f"{tag}_pg0", [128, 512], F32) as pg0, \
             nc.psum_tensor(f"{tag}_pg1", [128, 512], F32) as pg1, \
             nc.psum_tensor(f"{tag}_pu0", [128, 512], F32) as pu0, \
             nc.psum_tensor(f"{tag}_pu1", [128, 512], F32) as pu1, \
             nc.psum_tensor(f"{tag}_pd0", [128, 512], F32) as pd0, \
             nc.psum_tensor(f"{tag}_pd1", [128, 512], F32) as pd1:
            pg, pu = [pg0, pg1], [pu0, pu1]
            pd4 = [pg0, pg1, pd0, pd1]

            strip_loads = {}
            strip_free = {}
            tr_pe = {}
            ptr_evac = []
            n_gu_last = 0
            n_down_evac = 0
            n_down_mm = 0

            wg2 = [sems["wg0"], sems["wg1"]]
            wu2 = [sems["wu0"], sems["wu1"]]
            wd2 = [sems["wd0"], sems["wd1"]]

            def load_strips(b, ic):
                slot = ic % 2
                prevkey = (b, ic - 2) if ic >= 2 else (b - 1, ic + 6)
                prev = strip_free.get(prevkey)
                w = [(pe, prev)] if prev else []
                S.dma("sync", lambda e, ic=ic, slot=slot: e.dma_start(
                    out=wg_sb[slot][:, :], in_=wgv[ic]), wg2[slot], waits=w)
                S.dma("sync", lambda e, ic=ic, slot=slot: e.dma_start(
                    out=wu_sb[slot][:, :], in_=wuv[ic]), wu2[slot])
                strip_loads[(b, ic)] = (wg2[slot].n, wu2[slot].n)

            def emit_xq(b, n_gu_prev):
                xq_evac = {}
                for i, t in enumerate([b * TPB + j for j in range(TPB)]):
                    wq = [(dv, n_stats)] if (b == 0 and i == 0) else []
                    if i >= 1:
                        wq.append((pe, xq_evac[i - 1][1]))
                    S.op("vector", lambda e, t=t: e.tensor_scalar(
                        xq8[:, :], resid[:, t * D:(t + 1) * D],
                        mu[:, t:t + 1], a_q[:, t:t + 1], ALU.subtract, ALU.mult),
                        sem=dv, waits=wq)
                    S.op("vector", lambda e: e.tensor_copy(xqb[:, :], xq8[:, :]),
                         sem=dv)
                    n_xq = dv.n
                    for c in range(8):
                        w = []
                        if c == 0:
                            w = [(dv, n_xq)]
                            if ptr_evac:
                                w.append((ac, ptr_evac[-1]))
                        S.op("tensor", lambda e, c=c: e.matmul(
                            ptr[:, c * 128:(c + 1) * 128],
                            xqb[:, 128 * c:128 * c + 128], i128_sb[:, :],
                            start=True, stop=True),
                            sem=(pe if c == 7 else None), waits=w)
                    n_tr = pe.n
                    wev = [(pe, n_tr)]
                    if b > 0:
                        wev.append((pe, n_gu_prev))
                    S.op("scalar", lambda e, i=i: e.activation(
                        xqT[i][:, 0:512], ptr[:, 0:512], AF.Copy),
                        sem=ac, waits=wev)
                    S.op("scalar", lambda e, i=i: e.activation(
                        xqT[i][:, 512:1024], ptr[:, 512:1024], AF.Copy), sem=ac)
                    ptr_evac.append(ac.n)
                    xq_evac[i] = (ac.n, n_tr)
                return ac.n

            xq_all = {0: emit_xq(0, 0)}
            for b in range(NB):
                tiles = [b * TPB + i for i in range(TPB)]
                n_xq_all = xq_all[b]

                if (b, 0) not in strip_loads:
                    load_strips(b, 0)
                if (b, 1) not in strip_loads:
                    load_strips(b, 1)
                evac_g, evac_u = {}, {}
                for ic in range(8):
                    n_wg, n_wu = strip_loads[(b, ic)]
                    for i, t in enumerate(tiles):
                        ps = i % 2
                        w_mm = [(wg2[ic % 2], n_wg), (wu2[ic % 2], n_wu),
                                (ac, n_xq_all)]
                        if b > 0 and ic == 0 and i < 2:
                            w_mm.append((dv, n_down_evac))
                        prev = (ic, i - 2) if i >= 2 else ((ic - 1, i + 2) if ic >= 1 else None)
                        if prev in evac_g:
                            w_mm.append((ac, evac_g[prev]))
                        if prev in evac_u:
                            w_mm.append((dv, evac_u[prev]))
                        for kk in range(8):
                            S.op("tensor", lambda e, i=i, kk=kk, ps=ps, ic=ic: e.matmul(
                                pg[ps][:, :], xqT[i][:, 128 * kk:128 * kk + 128],
                                wg_sb[ic % 2][:, 512 * kk:512 * kk + 512],
                                start=(kk == 0), stop=(kk == 7)),
                                sem=(pe if kk == 7 else None),
                                waits=(w_mm if kk == 0 else []))
                        n_g = pe.n
                        for kk in range(8):
                            S.op("tensor", lambda e, i=i, kk=kk, ps=ps, ic=ic: e.matmul(
                                pu[ps][:, :], xqT[i][:, 128 * kk:128 * kk + 128],
                                wu_sb[ic % 2][:, 512 * kk:512 * kk + 512],
                                start=(kk == 0), stop=(kk == 7)),
                                sem=(pe if kk == 7 else None))
                        n_u = pe.n
                        strip_free[(b, ic)] = n_u
                        w_silu = [(pe, n_g)]
                        if prev in evac_u:
                            w_silu.append((dv, evac_u[prev]))
                        S.op("scalar", lambda e, t=t, ps=ps: e.activation(
                            silu_s[ps][:, :], pg[ps][:, :], AF.Silu,
                            scale=c_g[:, t:t + 1]),
                            sem=ac, waits=w_silu)
                        evac_g[(ic, i)] = ac.n
                        w_raw = [(ac, evac_g[(ic, i)]), (pe, n_u)]
                        if b > 0 and ic == 0:
                            w_raw.append((pe, n_down_mm))
                        S.op("vector", lambda e, i=i, ic=ic, ps=ps: e.tensor_mul(
                            raw_s[i][:, 512 * ic:512 * ic + 512], silu_s[ps][:, :],
                            pu[ps][:, :]),
                            sem=dv, waits=w_raw)
                        evac_u[(ic, i)] = dv.n
                    S.maybe_bar(sems["bar"], 2)
                    if ic + 2 < 8:
                        load_strips(b, ic + 2)
                    elif b + 1 < NB:
                        load_strips(b + 1, ic + 2 - 8)
                n_gu_last = pe.n

                if _STAGE == "dbg_raw" and b == 0:
                    for i in range(TPB):
                        S.dma("gpsimd", lambda e, i=i: e.dma_start(
                            out=consts["out"].ap().rearrange(
                                "(f p) c -> f p c", p=128)[4 * i:4 * i + 4],
                            in_=raw_s[i].ap().rearrange("p (f c) -> f p c", c=D)),
                            sems["dG"], waits=[(dv, evac_u[(7, i)])])
                    S.wait("gpsimd", sems["dG"], sems["dG"].n)
                    return

                for i, t in enumerate(tiles):
                    n_raw = evac_u[(7, i)]
                    S.op("vector", lambda e, i=i: e.reduce_max(
                        mcol[:, 0:1], raw_s[i][:, :], AX, apply_absolute_value=True),
                        sem=dv, waits=[(dv, n_raw)])
                    S.op("vector", lambda e, t=t: e.tensor_single_scalar(
                        mcol[:, 1:2], mcol[:, 0:1], c_u[:, t:t + 1], ALU.mult), sem=dv)
                    S.op("vector", lambda e: e.tensor_scalar_max(
                        mcol[:, 1:2], mcol[:, 1:2], 1e-5), sem=dv)
                    S.op("vector", lambda e: e.reciprocal(mcol[:, 2:3], mcol[:, 1:2]),
                         sem=dv)
                    S.op("vector", lambda e, t=t: e.scalar_tensor_tensor(
                        mcol[:, 3:4], mcol[:, 2:3], 127.0, c_u[:, t:t + 1],
                        ALU.mult, ALU.mult), sem=dv)
                    S.op("vector", lambda e, t=t: e.tensor_single_scalar(
                        hcd[:, t:t + 1], mcol[:, 1:2], cols[:, 5:6], ALU.mult), sem=dv)
                    wq8 = []
                    if i >= 1:
                        wq8.append((pe, tr_pe[i - 1]))
                    S.op("vector", lambda e, i=i: e.tensor_single_scalar(
                        gq8[:, :], raw_s[i][:, :], mcol[:, 3:4], ALU.mult),
                        sem=dv, waits=wq8)
                    S.op("vector", lambda e, i=i: e.tensor_copy(
                        gqb_s[i][:, :], gq8[:, :]), sem=dv)
                    n_gqb = dv.n
                    for grp in range(8):
                        half = (grp % 2) * 512
                        for c in range(4):
                            jc = grp * 4 + c
                            w = []
                            if c == 0:
                                if grp == 0:
                                    w.append((dv, n_gqb))
                                if ptr_evac:
                                    w.append((ac, ptr_evac[-1] if grp < 2 else ptr_evac[-2]))
                            S.op("tensor", lambda e, i=i, jc=jc, half=half, c=c: e.matmul(
                                ptr[:, half + c * 128: half + c * 128 + 128],
                                gqb_s[i][:, 128 * jc:128 * jc + 128], i128_sb[:, :],
                                start=True, stop=True),
                                sem=(pe if c == 3 else None), waits=w)
                        n_tr = pe.n
                        S.op("scalar", lambda e, i=i, grp=grp, half=half: e.activation(
                            gqT_s[i][:, grp * 512: grp * 512 + 512],
                            ptr[:, half: half + 512], AF.Copy),
                            sem=ac, waits=[(pe, n_tr)])
                        ptr_evac.append(ac.n)
                    tr_pe[i] = pe.n
                    S.maybe_bar(sems["bar"], 2)
                n_gqT_ready = ac.n

                ev_prev = n_down_evac
                wd_slot_free = getattr(_ffn_phase, "_wd_free", {0: 0, 1: 0})
                for half in range(2):
                    first_mm_wait = [(ac, n_gqT_ready)]
                    if half == 0:
                        first_mm_wait.append((ac, evac_g[(7, 3)]))
                        first_mm_wait.append((dv, evac_u[(7, 3)]))
                        if b > 0:
                            first_mm_wait.append((dv, ev_prev))
                    else:
                        first_mm_wait.append((dv, n_down_evac))
                    for jc in range(32):
                        slot = jc % 2
                        wl = [(pe, wd_slot_free[slot])] if wd_slot_free[slot] else []
                        S.dma("sync", lambda e, jc=jc, half=half, slot=slot: e.dma_start(
                            out=wd_sb[slot][:, :], in_=wdv[jc, half]), wd2[slot],
                            waits=wl)
                        n_wd = wd2[slot].n
                        for i, t in enumerate(tiles):
                            w = []
                            if jc == 0:
                                w = list(first_mm_wait) if i == 0 else []
                            if i == 0:
                                w.append((wd2[slot], n_wd))
                            S.op("tensor", lambda e, i=i, jc=jc, slot=slot: e.matmul(
                                pd4[i][:, :], gqT_s[i][:, 128 * jc:128 * jc + 128],
                                wd_sb[slot][:, :],
                                start=(jc == 0), stop=(jc == 31)),
                                sem=(pe if i == TPB - 1 else None), waits=w)
                        wd_slot_free[slot] = pe.n
                    n_dmm = pe.n
                    for i, t in enumerate(tiles):
                        S.op("vector", lambda e, t=t, i=i, half=half: e.scalar_tensor_tensor(
                            resid[:, t * D + half * 512: t * D + half * 512 + 512],
                            pd4[i][:, :], hcd[:, t:t + 1],
                            resid[:, t * D + half * 512: t * D + half * 512 + 512],
                            ALU.mult, ALU.add),
                            sem=dv, waits=[(pe, n_dmm)])
                    n_down_evac = dv.n
                    if half == 0 and b + 1 < NB:
                        xq_all[b + 1] = emit_xq(b + 1, n_gu_last)
                    S.maybe_bar(sems["bar"], 2)
                n_down_mm = pe.n
                if post_block is not None:
                    post_block(b, dict(ptr=ptr, ptr_evac=ptr_evac,
                                       n_down_evac=n_down_evac, dump=dump))
                S.maybe_bar(sems["bar"], 1)



def _p2_setup(nc, S, sems, consts, st, g_in, b_in, cc1a_in, cc1b_in,
              cc1a_out, cc1b_out):
    dS, pe, ac, dv, cc = (sems[n] for n in ("dS", "pe", "ac", "dv", "cc"))
    resid, stats, i128_sb = consts["resid"], consts["stats"], consts["i128"]

    def psb(name, shape, dt=F32):
        return st.enter_context(nc.sbuf_tensor(name, shape, dt))

    g_t = psb("p2_g", [128, D])
    b_t = psb("p2_b", [128, D])
    nbuf = psb("p2_n", [128, D])
    h2b = psb("p2_h2b", [128, D], BF16)
    h2f = [psb(f"p2_h2f{i}", [128, D], BF16) for i in range(2)]

    S.dma("sync", lambda e: e.dma_start(out=g_t[:, :], in_=g_in[:, :]), dS)
    S.dma("sync", lambda e: e.dma_start(out=b_t[:, :], in_=b_in[:, :]), dS)
    n_gb = dS.n

    sumc = stats[:, 96:112]
    sqc = stats[:, 112:128]
    rstd2 = stats[:, 64:80]
    b2 = stats[:, 80:96]
    state = dict(prev_tt=0, prev_ev={}, prev_dma={}, cc_done=[None, None])

    def emit_accs(b, ffnctx):
        dump = ffnctx["dump"]
        for k, t in enumerate(range(4 * b, 4 * b + 4)):
            w = [(dv, ffnctx["n_down_evac"])] if k == 0 else []
            S.op("scalar", lambda e, t=t: e.activation(
                dump[:, :], resid[:, t * D:(t + 1) * D], AF.Copy,
                accum_out=sumc[:, t:t + 1]), sem=ac, waits=w)
            S.op("scalar", lambda e, t=t: e.activation(
                dump[:, :], resid[:, t * D:(t + 1) * D], AF.Square,
                accum_out=sqc[:, t:t + 1]), sem=ac)
        state["acc_n"] = ac.n

    def emit_half(h, ffnctx):
        dump = ffnctx["dump"]
        t0 = 8 * h
        hs = slice(t0, t0 + 8)
        n_acc = state["acc_n"]
        S.op("vector", lambda e: e.tensor_single_scalar(
            sumc[:, hs], sumc[:, hs], 1.0 / D, ALU.mult),
            sem=dv, waits=[(ac, n_acc)])
        S.op("vector", lambda e: e.tensor_single_scalar(
            sqc[:, hs], sqc[:, hs], 1.0 / D, ALU.mult), sem=dv)
        S.op("vector", lambda e: e.tensor_mul(b2[:, hs], sumc[:, hs],
                                              sumc[:, hs]), sem=dv)
        S.op("vector", lambda e: e.tensor_sub(sqc[:, hs], sqc[:, hs],
                                              b2[:, hs]), sem=dv)
        n_var = dv.n
        S.op("scalar", lambda e: e.preload_act_table(AF.Sqrt), sem=ac)
        S.op("scalar", lambda e: e.activation(rstd2[:, hs], sqc[:, hs], AF.Sqrt,
                                              bias=consts["cols"][:, 8:9]),
             sem=ac, waits=[(dv, n_var)])
        n_std2 = ac.n
        if h == 0:
            S.op("scalar", lambda e: e.preload_act_table(AF.Silu), sem=ac)
        S.op("vector", lambda e: e.reciprocal(rstd2[:, hs], rstd2[:, hs]),
             sem=dv, waits=[(ac, n_std2)])
        n_rstd = dv.n

        ptr2 = ffnctx["ptr"]
        ptr_evac = ffnctx["ptr_evac"]
        for t in range(t0, t0 + 8):
            slot = t % 2
            w_n = [(dv, n_rstd), (dS, n_gb)]
            if state["prev_tt"]:
                w_n.append((dv, state["prev_tt"]))
            S.op("vector", lambda e, t=t: e.tensor_scalar(
                nbuf[:, :], resid[:, t * D:(t + 1) * D],
                sumc[:, t:t + 1], rstd2[:, t:t + 1], ALU.subtract, ALU.mult),
                sem=dv, waits=w_n)
            w_m = []
            if (t - 1) in state["prev_ev"]:
                w_m.append((pe, state["prev_ev"][(t - 1, "pe")]))
            S.op("vector", lambda e: e.tensor_mul(nbuf[:, :], nbuf[:, :],
                                                  g_t[:, :]),
                 sem=dv, waits=w_m)
            S.op("vector", lambda e: e.tensor_add(h2b[:, :], nbuf[:, :],
                                                  b_t[:, :]), sem=dv)
            state["prev_tt"] = dv.n
            n_h2b = dv.n
            for c in range(8):
                w2 = []
                if c == 0:
                    w2 = [(dv, n_h2b)]
                    if ptr_evac:
                        w2.append((ac, ptr_evac[-1]))
                S.op("tensor", lambda e, c=c: e.matmul(
                    ptr2[:, c * 128:(c + 1) * 128],
                    h2b[:, 128 * c:128 * c + 128], i128_sb[:, :],
                    start=True, stop=True),
                    sem=(pe if c == 7 else None), waits=w2)
            n_tr = pe.n
            wev = [(pe, n_tr)]
            if (t - 2) in state["prev_dma"]:
                wev.append(state["prev_dma"][t - 2])
            S.op("scalar", lambda e, slot=slot: e.activation(
                h2f[slot][:, 0:512], ptr2[:, 0:512], AF.Copy),
                sem=ac, waits=wev)
            S.op("scalar", lambda e, slot=slot: e.activation(
                h2f[slot][:, 512:1024], ptr2[:, 512:1024], AF.Copy), sem=ac)
            state["prev_ev"][t] = ac.n
            state["prev_ev"][(t, "pe")] = n_tr
            ptr_evac.append(ac.n)
            h2fs = sems["h2f0"] if slot == 0 else sems["h2f1"]
            cct = cc1a_in if h == 0 else cc1b_in
            S.dma("sync", lambda e, t=t, slot=slot, cct=cct: e.dma_start(
                out=cct.ap()[t % 8], in_=h2f[slot][:, :]),
                h2fs, waits=[(ac, state["prev_ev"][t])])
            state["prev_dma"][t] = (h2fs, h2fs.n)

        src_t, dst_t = ((cc1a_in, cc1a_out) if h == 0 else (cc1b_in, cc1b_out))
        S.op("gpsimd", lambda e, src_t=src_t, dst_t=dst_t: e.collective_compute(
            "AllGather", ALU.bypass, replica_groups=[list(range(NCORE))],
            ins=[src_t.ap().opt()], outs=[dst_t.ap().opt()]),
            sem=cc, waits=[(sems["h2f0"], sems["h2f0"].n),
                           (sems["h2f1"], sems["h2f1"].n)])
        state["cc_done"][h] = cc.n

    return emit_accs, emit_half, state



def _p3_qkv(nc, S, astack, sems, consts, wslices, ropes, cc1a_out, ccA_done,
            cc1b_out, ccB_done, qnf, knf, vres):
    dS, dG, pe, ac, dv, cc = (sems[n] for n in ("dS", "dG", "pe", "ac", "dv", "cc"))
    cols, i128_sb = consts["cols"], consts["i128"]
    ind_sq_sb, ind_bc_sb = consts["ind_sq"], consts["ind_bc"]
    wqm_in, wqr_in, wkm_in, wkr_in, wvm_in = wslices
    cosq_in, sinq_in, cosk_in, sink_in = ropes

    S.op("scalar", lambda e: e.add_instruction(mybir.InstLoadActFuncSet(
        name=nc.get_next_instruction_name(), act_func_set_id=6,
        ins=[], outs=[])), sem=sems["ac"])
    wq_ro = astack.enter_context(nc.sbuf_tensor("wq_ro", [128, D], BF16))
    wk_ro = astack.enter_context(nc.sbuf_tensor("wk_ro", [128, D], BF16))
    wv_bf = astack.enter_context(nc.sbuf_tensor("wv_bf", [128, D], BF16))

    def _rope_fold():
        with ExitStack() as ropest:
            wm = ropest.enter_context(nc.sbuf_tensor("p3_wm", [128, D], F32))
            wr = ropest.enter_context(nc.sbuf_tensor("p3_wr", [128, D], F32))
            cosb = ropest.enter_context(nc.sbuf_tensor("p3_cos", [128, 128], F32))
            sinb = ropest.enter_context(nc.sbuf_tensor("p3_sin", [128, 128], F32))
            tmp = ropest.enter_context(nc.sbuf_tensor("p3_tmp", [128, 128], F32))
            tmp2 = ropest.enter_context(nc.sbuf_tensor("p3_tmp2", [128, 128], F32))

            for (m_in, r_in, c_in, s_in, dst) in (
                    (wqm_in, wqr_in, cosq_in, sinq_in, wq_ro),
                    (wkm_in, wkr_in, cosk_in, sink_in, wk_ro)):
                n_prev_use = dv.n
                S.dma("sync", lambda e, m=m_in: e.dma_start(
                    out=wm.ap().rearrange("p (k c) -> p k c", c=128),
                    in_=m.ap().rearrange("(k p) c -> p k c", p=128)),
                    dS, waits=[(dv, n_prev_use)])
                S.dma("sync", lambda e, r=r_in: e.dma_start(
                    out=wr.ap().rearrange("p (k c) -> p k c", c=128),
                    in_=r.ap().rearrange("(k p) c -> p k c", p=128)), dS)
                S.dma("sync", lambda e, c=c_in: e.dma_start(
                    out=cosb[:, :], in_=c[:, :]), dS)
                S.dma("sync", lambda e, s=s_in: e.dma_start(
                    out=sinb[:, :], in_=s[:, :]), dS)
                n_l = dS.n
                for k in range(8):
                    S.op("vector", lambda e, k=k: e.tensor_mul(
                        tmp[:, :], wr[:, 128 * k:128 * k + 128], sinb[:, :]),
                        sem=dv, waits=[(dS, n_l)] if k == 0 else [])
                    S.op("vector", lambda e, k=k: e.tensor_mul(
                        tmp2[:, :], wm[:, 128 * k:128 * k + 128], cosb[:, :]),
                        sem=dv)
                    S.op("vector", lambda e, k=k, dst=dst: e.tensor_add(
                        dst[:, 128 * k:128 * k + 128], tmp2[:, :], tmp[:, :]),
                        sem=dv)
        return dv.n

    with ExitStack() as rst:
        n_ro = _rope_fold()
        if _STAGE == "dbg_wro":
            S.wait("gpsimd", dv, n_ro)
            S.dma("gpsimd", lambda e: e.dma_start(
                out=consts["out"].ap()[0:128, :], in_=wq_ro[:, :]), dG)
            S.dma("gpsimd", lambda e: e.dma_start(
                out=consts["out"].ap()[128:256, :], in_=wk_ro[:, :]), dG)
            S.wait("gpsimd", dG, dG.n)
            S.flush()
            return
        S.dma("gpsimd", lambda e: e.dma_start(
            out=wv_bf.ap().rearrange("p (k c) -> p k c", c=128),
            in_=wvm_in.ap().rearrange("(k p) c -> p k c", p=128)), dG)
        n_wv = dG.n

        h2g = [rst.enter_context(nc.sbuf_tensor(f"p3_h2g{i}", [128, 8 * 512], BF16))
               for i in range(2)]
        qraw = [rst.enter_context(nc.sbuf_tensor(f"p3_qraw{i}", [128, 1024], BF16))
                for i in range(3)]
        sqb = [rst.enter_context(nc.sbuf_tensor(f"p3_sqb{i}", [128, 1024], BF16))
               for i in range(2)]
        nrm = rst.enter_context(nc.sbuf_tensor("p3_nrm", [2, 512], F32))
        nrm2 = rst.enter_context(nc.sbuf_tensor("p3_nrm2", [2, 512], F32))
        rq_s = [rst.enter_context(nc.sbuf_tensor(f"p3_rq{i}", [2, 512], BF16))
                for i in range(2)]
        bcs = rst.enter_context(nc.sbuf_tensor("p3_bcs", [128, 512], BF16))

        with nc.psum_tensor("p3_pq", [128, 512], F32) as pq, \
             nc.psum_tensor("p3_pk", [128, 512], F32) as pk, \
             nc.psum_tensor("p3_pv", [128, 512], F32) as pv, \
             nc.psum_tensor("p3_pn", [2, 512], F32) as pn, \
             nc.psum_tensor("p3_pbc", [128, 512], F32) as pbc:

            S.op("vector", lambda e: e.memset(
                vres.ap().rearrange("p (t h one) -> p t h one", h=2, one=65)
                [:, :, :, 64:65], 1.0), sem=dv)

            h2g2 = [sems["h2g0"], sems["h2g1"]]
            load_done = {}
            qk_evac = {}
            v_evac = {}
            cst = dict(n_n=0, n_rq=0, n_pbc=0, n_bcs=0, last_tt=0, n_pnread=0)
            chain_pn = {}
            chain_done = {}
            rqc = {}
            pendA, pendB = [], []
            for g in range(32):
                slot = g % 2
                sl3 = g % 3
                w = ([(cc, ccA_done), (dv, n_ro)] if g < 2
                     else [(pe, load_done[(g - 2, "pe")])])
                if g in (16, 17):
                    w.append((cc, ccB_done))
                for i in range(4):
                    tg = 4 * g + i
                    srcT = cc1a_out if tg // 8 < 8 else cc1b_out
                    S.dma("sync", lambda e, tg=tg, i=i, slot=slot, srcT=srcT:
                          e.dma_start(
                        out=h2g[slot][:, i * 1024:(i + 1) * 1024],
                        in_=srcT.ap()[tg % 8, (tg // 8) % 8]),
                        h2g2[slot], waits=(w if i == 0 else []))
                n_l = h2g2[slot].n

                for (dst_ps, wsb) in ((pq, wq_ro), (pk, wk_ro)):
                    for c in range(8):
                        w2 = []
                        if c == 0:
                            w2 = [(h2g2[slot], n_l), (dv, n_ro)]
                            if g >= 1:
                                w2.append((ac, qk_evac.get(g - 1, (0, 0))[0]))
                        S.op("tensor", lambda e, c=c, dst_ps=dst_ps, wsb=wsb, slot=slot: e.matmul(
                            dst_ps[:, :], wsb[:, 128 * c:128 * c + 128],
                            h2g[slot].ap().rearrange("p (i cq) -> p i cq", cq=1024)
                            [:, :, c * 128:(c + 1) * 128],
                            start=(c == 0), stop=(c == 7)),
                            sem=(pe if c == 7 else None), waits=w2)
                n_qk = pe.n
                if _STAGE == "dbg_h2g" and g == 0:
                    S.wait("gpsimd", h2g2[0], h2g2[0].n)
                    for f in range(4):
                        S.dma("gpsimd", lambda e, f=f: e.dma_start(
                            out=consts["out"].ap().rearrange("(f p) c -> f p c",
                                                             p=128)[f],
                            in_=h2g[0][:, f * D:(f + 1) * D]), sems["dG"])
                    S.op("scalar", lambda e: e.activation(
                        qraw[0][:, 0:512], pq[:, :], AF.Copy), sem=ac,
                        waits=[(pe, n_qk)])
                    S.wait("gpsimd", ac, ac.n)
                    S.dma("gpsimd", lambda e: e.dma_start(
                        out=consts["out"].ap().rearrange("(f p) c -> f p c", p=128)
                        [4, :, 0:512],
                        in_=qraw[0][:, 0:512]), sems["dG"])
                    S.wait("gpsimd", sems["dG"], sems["dG"].n)
                    return
                for i in range(4):
                    for c in range(8):
                        w2 = []
                        if i == 0 and c == 0:
                            w2 = [(dG, n_wv)]
                            if g >= 1:
                                w2.append((ac, v_evac.get(g - 1, 0)))
                        S.op("tensor", lambda e, i=i, c=c, slot=slot: e.matmul(
                            pv[:, i * 128:(i + 1) * 128],
                            h2g[slot][:, i * 1024 + c * 128: i * 1024 + c * 128 + 128],
                            wv_bf[:, 128 * c:128 * c + 128],
                            start=(c == 0), stop=(c == 7)),
                            sem=(pe if (c == 7 and i == 3) else None), waits=w2)
                n_v = pe.n
                load_done[(g, "pe")] = n_v

                S.op("scalar", lambda e, g=g: e.activation(
                    vres.ap().rearrange("p (t h one) -> p t h one", h=2, one=65)
                    [:, 4 * g:4 * g + 4, :, 0:64],
                    pv.ap().rearrange("p (i h d) -> p i h d", i=4, h=2),
                    AF.Copy), sem=ac, waits=[(pe, n_v)])
                v_evac[g] = ac.n

                w_sq = [(pe, n_qk)]
                if (g - 2) in chain_pn:
                    w_sq.append((pe, chain_pn[g - 2]))
                if (g - 3) in chain_done:
                    w_sq.append((dv, chain_done[g - 3]))
                for (side, ps_in) in ((0, pq), (1, pk)):
                    S.op("scalar", lambda e, ps_in=ps_in, side=side, slot=slot:
                         e.activation(sqb[slot][:, side * 512:side * 512 + 512],
                                      ps_in[:, :], AF.Square),
                         sem=ac, waits=(w_sq if side == 0 else []))
                    S.op("scalar", lambda e, ps_in=ps_in, side=side, sl3=sl3:
                         e.activation(qraw[sl3][:, side * 512:side * 512 + 512],
                                      ps_in[:, :], AF.Copy), sem=ac)
                n_sq_done = ac.n
                qk_evac[g] = (n_sq_done, 0)

                def stageA(g=g, slot=slot, sl3=sl3, n_sq_done=n_sq_done):
                    for (side, is_q) in ((0, True), (1, False)):
                        off = side * 512
                        w_pn = [(ac, n_sq_done)]
                        if cst["n_pnread"]:
                            w_pn.append((dv, cst["n_pnread"]))
                        S.op("tensor", lambda e, slot=slot, off=off: e.matmul(
                            pn[:, :], ind_sq_sb[:, :], sqb[slot][:, off:off + 512],
                            start=True, stop=True), sem=pe, waits=w_pn)
                        n_pn = pe.n
                        if side == 1:
                            chain_pn[g] = n_pn
                        w_n = [(pe, n_pn)]
                        if cst["n_n"]:
                            w_n.append((ac, cst["n_n"]))
                        if cst["n_rq"]:
                            w_n.append((dv, cst["n_rq"]))
                        S.op("vector", lambda e: e.tensor_scalar_max(
                            nrm[:, :], pn[:, :], 1e-24), sem=dv, waits=w_n)
                        cst["n_pnread"] = dv.n
                        S.op("scalar", lambda e: e.activation(
                            nrm2[:, :], nrm[:, :], AF.Ln), sem=ac,
                            waits=[(dv, cst["n_pnread"])])
                        w_rq = [(pe, cst["n_pbc"])] if cst["n_pbc"] else []
                        if is_q:
                            S.op("scalar", lambda e: e.activation(
                                nrm[:, :], nrm2[:, :], AF.Exp, scale=-0.5),
                                sem=ac)
                            cst["n_n"] = ac.n
                            S.op("vector", lambda e, side=side: e.tensor_single_scalar(
                                rq_s[side][:, :], nrm[:, :], cols[0:2, 6:7],
                                ALU.mult), sem=dv,
                                waits=w_rq + [(ac, cst["n_n"])])
                        else:
                            S.op("scalar", lambda e, side=side: e.activation(
                                rq_s[side][:, :], nrm2[:, :], AF.Exp, scale=-0.5),
                                sem=ac, waits=w_rq)
                            cst["n_n"] = ac.n
                            S.op("vector", lambda e: e.engine_nop(), sem=dv,
                                 waits=[(ac, cst["n_n"])])
                        rqc[(g, side)] = dv.n
                        cst["n_rq"] = dv.n

                    def stageB(g=g, sl3=sl3):
                        for (side, dst) in ((0, qnf), (1, knf)):
                            off = side * 512
                            w_pbc = [(dv, rqc[(g, side)])]
                            if cst["n_bcs"]:
                                w_pbc.append((ac, cst["n_bcs"]))
                            S.op("tensor", lambda e, side=side: e.matmul(
                                pbc[:, :], ind_bc_sb[:, :], rq_s[side][:, :],
                                start=True, stop=True), sem=pe, waits=w_pbc)
                            n_pbc = pe.n
                            cst["n_pbc"] = n_pbc
                            w_bcs = [(pe, n_pbc)]
                            if cst["last_tt"]:
                                w_bcs.append((dv, cst["last_tt"]))
                            S.op("scalar", lambda e: e.activation(
                                bcs[:, :], pbc[:, :], AF.Copy), sem=ac, waits=w_bcs)
                            cst["n_bcs"] = ac.n
                            S.op("vector", lambda e, dst=dst, g=g, sl3=sl3,
                                 off=off: e.tensor_mul(
                                dst[:, 512 * g:512 * g + 512],
                                qraw[sl3][:, off:off + 512], bcs[:, :]),
                                sem=dv, waits=[(ac, cst["n_bcs"])])
                            cst["last_tt"] = dv.n
                        chain_done[g] = cst["last_tt"]
                    return stageB

                if pendB:
                    pendB.pop(0)()
                if pendA:
                    pendB.append(pendA.pop()())
                pendA.append(stageA)
            if pendB:
                pendB.pop(0)()
            if pendA:
                pendB.append(pendA.pop()())
            while pendB:
                pendB.pop(0)()



def _p4_attn(nc, S, astack, sems, consts, qnf, knf, vres, cc2h):
    dS, pe, ac, dv = (sems[n] for n in ("dS", "pe", "ac", "dv"))
    cc = sems["cc"]
    cols, triu_sb, ones64_sb = consts["cols"], consts["triu"], consts["ones64"]

    pa_sb = [astack.enter_context(nc.sbuf_tensor(f"p4_pa{i}", [128, 1024], BF16))
             for i in range(2)]
    pb_sb = [astack.enter_context(nc.sbuf_tensor(f"p4_pb{i}", [128, 1024], BF16))
             for i in range(2)]
    rs_sb = astack.enter_context(nc.sbuf_tensor("p4_rs", [1, 1024], F32))
    bca_sb = astack.enter_context(nc.sbuf_tensor("p4_bca", [64, 512], F32))
    bcb_sb = astack.enter_context(nc.sbuf_tensor("p4_bcb", [64, 512], F32))
    of_sb = [astack.enter_context(nc.sbuf_tensor(f"p4_of{i}", [128, 512], BF16))
             for i in range(2)]
    orawA = [astack.enter_context(nc.sbuf_tensor(f"p4_oa{i}", [65, 512], F32))
             for i in range(2)]
    orawB = [astack.enter_context(nc.sbuf_tensor(f"p4_ob{i}", [65, 512], F32))
             for i in range(2)]

    with nc.psum_tensor("p4_psA", [128, 1024], F32) as psA, \
         nc.psum_tensor("p4_psB", [128, 1024], F32) as psB, \
         nc.psum_tensor("p4_oA", [65, 512], F32) as oA, \
         nc.psum_tensor("p4_oB", [65, 512], F32) as oB, \
         nc.psum_tensor("p4_nA", [64, 512], F32) as nA, \
         nc.psum_tensor("p4_nB", [64, 512], F32) as nB:

        of_dma = []
        onorm = []
        pending_norm = []
        p4lim = int(os.environ.get("P4LIM", "999"))
        ngroups = 0
        for s in range(NSEQ):
            for qg in range(8):
                if ngroups >= p4lim:
                    continue
                ngroups += 1
                gi = s * 8 + qg
                if gi == 17:
                    S.op("gpsimd", lambda e: e.collective_compute(
                        "AllGather", ALU.bypass,
                        replica_groups=[list(range(NCORE))],
                        ins=[cc2h["a_in"].ap().opt()],
                        outs=[cc2h["a_out"].ap().opt()]),
                        sem=cc, waits=[(sems["of0"], sems["of0"].n),
                                       (sems["of1"], sems["of1"].n)])
                    cc2h["ccA_done"] = cc.n
                qc0 = s * SEQ + qg * 512
                nkt = 4 * qg + 4
                npair = nkt // 2
                exp_done = {}
                exp_done_h = {}
                diag_done = {}
                av_done = {}
                prev = onorm[-1] if onorm else None
                for pair in range(npair):
                    for kk in (2 * pair, 2 * pair + 1):
                        half = (kk % 2) * 512
                        gk = 32 * s + kk
                        w = []
                        if pair >= 1:
                            w.append((ac, exp_done_h[(pair - 1, kk % 2)]))
                        elif prev:
                            w.append((ac, prev["exp_last"]))
                        S.op("tensor", lambda e, gk=gk, half=half, qc0=qc0: e.matmul(
                            psA[:, half:half + 512],
                            knf[0:64, 128 * gk:128 * gk + 128],
                            qnf[0:64, qc0:qc0 + 512],
                            start=True, stop=True, tile_position=(0, 0)),
                            sem=pe, waits=w)
                        S.op("tensor", lambda e, gk=gk, half=half, qc0=qc0: e.matmul(
                            psB[:, half:half + 512],
                            knf[64:128, 128 * gk:128 * gk + 128],
                            qnf[64:128, qc0:qc0 + 512],
                            start=True, stop=True, tile_position=(64, 0)),
                            sem=pe)
                        n_sc = pe.n
                        w_exp = [(pe, n_sc)]
                        if pair >= 2:
                            w_exp.append((pe, av_done[pair - 2]))
                        elif prev:
                            w_exp.append((pe, prev["n_av"]))
                        S.op("scalar", lambda e, pair=pair, half=half: e.activation(
                            pa_sb[pair % 2][:, half:half + 512],
                            psA[:, half:half + 512], AF.Exp,
                            bias=cols[:, 7:8]), sem=ac, waits=w_exp)
                        S.op("scalar", lambda e, pair=pair, half=half: e.activation(
                            pb_sb[pair % 2][:, half:half + 512],
                            psB[:, half:half + 512], AF.Exp,
                            bias=cols[:, 7:8]), sem=ac)
                        exp_done_h[(pair, kk % 2)] = ac.n
                    exp_done[pair] = ac.n
                    for kk in (2 * pair, 2 * pair + 1):
                        if kk >= 4 * qg:
                            i0 = kk - 4 * qg
                            off = (kk % 2) * 512 + i0 * 128
                            for psb in (pa_sb, pb_sb):
                                S.op("vector", lambda e, psb=psb, pair=pair, off=off: e.tensor_mul(
                                    psb[pair % 2][:, off:off + 128],
                                    psb[pair % 2][:, off:off + 128],
                                    triu_sb[:, :]),
                                    sem=dv, waits=[(ac, exp_done[pair])])
                            diag_done[kk] = dv.n
                    for kk in (2 * pair, 2 * pair + 1):
                        gk2 = 32 * s + kk
                        i0 = max(0, kk - 4 * qg)
                        off = (kk % 2) * 512 + i0 * 128
                        nq = 512 - i0 * 128
                        wav = [(ac, exp_done_h[(pair, kk % 2)])]
                        if kk in diag_done:
                            wav.append((dv, diag_done[kk]))
                        if kk == 0 and prev:
                            wav.append((dv, prev["n_evac"]))
                        S.op("tensor", lambda e, gk2=gk2, off=off, nq=nq, i0=i0, kk=kk, pair=pair: e.matmul(
                            oA[:, i0 * 128: i0 * 128 + nq],
                            vres[:, 130 * gk2:130 * gk2 + 65],
                            pa_sb[pair % 2][:, off:off + nq],
                            start=(kk == 0), stop=(kk == nkt - 1)),
                            sem=pe, waits=wav)
                        S.op("tensor", lambda e, gk2=gk2, off=off, nq=nq, i0=i0, kk=kk, pair=pair: e.matmul(
                            oB[:, i0 * 128: i0 * 128 + nq],
                            vres[:, 130 * gk2 + 65:130 * gk2 + 130],
                            pb_sb[pair % 2][:, off:off + nq],
                            start=(kk == 0), stop=(kk == nkt - 1)),
                            sem=pe)
                    av_done[pair] = pe.n
                    if pair == 1 and pending_norm:
                        pending_norm.pop()()
                    S.maybe_bar(sems["bar"], 2)
                    if _STAGE == "dbg_p0" and s == 0 and qg == 0 and pair == 0:
                        S.wait("gpsimd", dv, dv.n)
                        S.wait("gpsimd", ac, ac.n)
                        S.dma("gpsimd", lambda e: e.dma_start(
                            out=consts["out"].ap()[0:128, :],
                            in_=pa_sb[0][:, :]), sems["dG"])
                        S.dma("gpsimd", lambda e: e.dma_start(
                            out=consts["out"].ap()[128:256, :],
                            in_=pb_sb[0][:, :]), sems["dG"])
                        S.wait("gpsimd", sems["dG"], sems["dG"].n)
                        S.flush()
                        raise StopIteration
                n_av = pe.n

                slot = gi % 2
                w_ev = [(pe, n_av)]
                if len(onorm) >= 2:
                    w_ev.append((dv, onorm[-2]["n_of"]))
                S.op("vector", lambda e, slot=slot: e.tensor_copy(
                    orawA[slot][:, :], oA[:, :]), sem=dv, waits=w_ev)
                S.op("vector", lambda e, slot=slot: e.tensor_copy(
                    orawB[slot][:, :], oB[:, :]), sem=dv)
                n_evac = dv.n
                entry = dict(n_evac=n_evac, n_av=n_av,
                             exp_last=exp_done[npair - 1])
                onorm.append(entry)

                def emit_norm(entry=entry, slot=slot, pidx=len(onorm) - 2,
                              s=s, qg=qg):
                    pv = onorm[pidx] if pidx >= 0 else None
                    w_rs = []
                    if pv:
                        w_rs.append((pe, pv["n_bc"]))
                    S.op("vector", lambda e, slot=slot: e.reciprocal(
                        rs_sb[0:1, 0:512], orawA[slot][64:65, :]), sem=dv,
                        waits=w_rs)
                    S.op("vector", lambda e, slot=slot: e.reciprocal(
                        rs_sb[0:1, 512:1024], orawB[slot][64:65, :]), sem=dv)
                    n_rs = dv.n
                    w_bc = [(dv, n_rs)]
                    if pv:
                        w_bc.append((ac, pv["n_bcs"]))
                    S.op("tensor", lambda e: e.matmul(
                        nA[:, :], ones64_sb[:, :], rs_sb[0:1, 0:512], start=True,
                        stop=True), sem=pe, waits=w_bc)
                    S.op("tensor", lambda e: e.matmul(
                        nB[:, :], ones64_sb[:, :], rs_sb[0:1, 512:1024], start=True,
                        stop=True), sem=pe)
                    n_bc = pe.n
                    w_bcs = [(pe, n_bc)]
                    if pv:
                        w_bcs.append((dv, pv["n_of"]))
                    S.op("scalar", lambda e: e.activation(bca_sb[:, :], nA[:, :],
                                                          AF.Copy),
                         sem=ac, waits=w_bcs)
                    S.op("scalar", lambda e: e.activation(bcb_sb[:, :], nB[:, :],
                                                          AF.Copy), sem=ac)
                    n_bcs = ac.n
                    wof = [(ac, n_bcs)]
                    if len(of_dma) >= 2:
                        wof.append(of_dma[-2])
                    S.op("vector", lambda e, slot=slot: e.tensor_mul(
                        of_sb[slot][0:64, :], orawA[slot][0:64, :], bca_sb[:, :]),
                        sem=dv, waits=wof)
                    S.op("vector", lambda e, slot=slot: e.tensor_mul(
                        of_sb[slot][64:128, :], orawB[slot][0:64, :], bcb_sb[:, :]),
                        sem=dv)
                    n_of = dv.n
                    entry["n_of"] = n_of
                    entry["n_bcs"] = n_bcs
                    entry["n_bc"] = n_bc

                    tg0 = 32 * s + 4 * qg
                    recv0 = tg0 % 8
                    slot_idx = tg0 // 8
                    cct = cc2h["a_in"] if slot_idx < 8 else cc2h["b_in"]
                    si = slot_idx % 8
                    ofs = sems["of0"] if slot == 0 else sems["of1"]
                    S.dma("sync", lambda e, recv0=recv0, si=si, cct=cct,
                          slot=slot: e.dma_start(
                        out=cct.ap()[recv0:recv0 + 4, :,
                                     si * 128:(si + 1) * 128]
                        .rearrange("i p q -> p i q"),
                        in_=of_sb[slot].ap().rearrange("p (i q) -> p i q", q=128)),
                        ofs, waits=[(dv, n_of)])
                    of_dma.append((ofs, ofs.n))

                pending_norm.append(emit_norm)
                S.maybe_bar(sems["bar"], 1)
        if pending_norm:
            pending_norm.pop()()



def _p5_oproj(nc, S, sems, consts, cc2a_out, ccA_done, cc2b_out, ccB_done,
              wout_in):
    dS, dG, pe, ac, dv, cc = (sems[n] for n in ("dS", "dG", "pe", "ac", "dv", "cc"))
    resid, rank_sb = consts["resid"], consts["rank"]

    with ExitStack() as pst:
        ofall = pst.enter_context(nc.sbuf_tensor("p5_ofall", [128, NCORE * 2048], BF16))
        wout_bf = pst.enter_context(nc.sbuf_tensor("p5_wout", [128, 8 * D], BF16))

        S.dma("gpsimd", lambda e: e.dma_start(
            out=wout_bf.ap().rearrange("p (k n) -> p k n", n=D),
            in_=wout_in.ap().rearrange("(k p) n -> p k n", p=128)), dG)
        n_w = dG.n

        rk_holder = {}

        def load_of(e, j, src, hoff):
            if "rk" not in rk_holder:
                rk_holder["rk"] = nc.values_load(
                    rank_sb[:, :], engines=[SP], min_val=0, max_val=7,
                    skip_runtime_bounds_check=True)
            rk = rk_holder["rk"]
            return e.dma_start(
                out=ofall.ap()[:, j * 2048 + hoff:j * 2048 + hoff + 1024],
                in_=src.ap()[j, DynSlice(rk, 1)].rearrange("one p q -> p (one q)"))

        n_ofh = {}
        ev = []

        def emit_half(hi, src, ccdone, py):
            for j in range(NCORE):
                S.dma("sync", lambda e, j=j, src=src: load_of(
                    e, j, src, hi * 1024), dS,
                    waits=([(cc, ccdone)] if j == 0 else []))
            n_ofh[hi] = dS.n
            for t in range(8 * hi, 8 * hi + 8):
                ps = py[t % 2]
                w = [(dS, n_ofh[hi]), (dG, n_w)]
                if len(ev) >= 2:
                    w.append((dv, ev[-2]))
                for j in range(NCORE):
                    for half in range(2):
                        S.op("tensor", lambda e, t=t, j=j, half=half, ps=ps: e.matmul(
                            ps[:, half * 512:(half + 1) * 512],
                            ofall[:, j * 2048 + t * 128: j * 2048 + t * 128 + 128],
                            wout_bf[:, j * D + half * 512: j * D + half * 512 + 512],
                            start=(j == 0), stop=(j == NCORE - 1)),
                            sem=(pe if (j == NCORE - 1 and half == 1) else None),
                            waits=(w if (j == 0 and half == 0) else []))
                n_mm = pe.n
                for half in range(2):
                    S.op("vector", lambda e, t=t, ps=ps, half=half: e.tensor_add(
                        resid[:, t * D + 512 * half:t * D + 512 * half + 512],
                        resid[:, t * D + 512 * half:t * D + 512 * half + 512],
                        ps[:, 512 * half:512 * half + 512]),
                        sem=dv, waits=[(pe, n_mm)])
                ev.append(dv.n)

        with nc.psum_tensor("p5_py0", [128, 1024], F32) as py0, \
             nc.psum_tensor("p5_py1", [128, 1024], F32) as py1:
            py = [py0, py1]
            emit_half(0, cc2a_out, ccA_done, py)
            emit_half(1, cc2b_out, ccB_done, py)


def _store_out(nc, S, sems, resid, out):
    dS = sems["dS"]
    S.dma("sync", lambda e: e.dma_start(
        out=out.ap().rearrange("(t p) c -> p t c", p=128),
        in_=resid.ap().rearrange("p (t c) -> p t c", c=D)), dS)
    n_done = dS.n
    S.wait("gpsimd", dS, n_done)



def kernel(L0_attn_x, L0_attn_ln_gamma, L0_attn_ln_beta, L0_attn_w_qkv,
           L0_attn_w_out, L0_attn_qkscale, L0_ffn_w_gate, L0_ffn_w_up,
           L0_ffn_w_down):
    x = np.ascontiguousarray(np.asarray(L0_attn_x, dtype=np.float32)).reshape(128, 128, D)
    gamma = np.asarray(L0_attn_ln_gamma, dtype=np.float32)
    beta = np.asarray(L0_attn_ln_beta, dtype=np.float32)
    wqkv = np.asarray(L0_attn_w_qkv, dtype=np.float32)
    w_out = np.ascontiguousarray(np.asarray(L0_attn_w_out, dtype=np.float32))
    qks = np.asarray(L0_attn_qkscale, dtype=np.float32).reshape(1, 1)
    wgT = np.ascontiguousarray(np.asarray(L0_ffn_w_gate, dtype=np.float32).T)
    wuT = np.ascontiguousarray(np.asarray(L0_ffn_w_up, dtype=np.float32).T)
    wdT = np.ascontiguousarray(np.asarray(L0_ffn_w_down, dtype=np.float32).T)

    cos, sin_eff, partner = _rope_consts()
    g_t = np.ascontiguousarray(np.broadcast_to(gamma, (128, D)))
    b_t = np.ascontiguousarray(np.broadcast_to(beta, (128, D)))

    in_maps = []
    for c in range(NCORE):
        qcols = np.arange(128 * c, 128 * c + 128)
        in_maps.append(dict(
            x=np.ascontiguousarray(x[c::8].reshape(2048, D)),
            g_t=g_t, b_t=b_t,
            wq_m=np.ascontiguousarray(wqkv[:, qcols]),
            wq_r=np.ascontiguousarray(wqkv[:, partner[qcols]]),
            wk_m=np.ascontiguousarray(wqkv[:, D + qcols]),
            wk_r=np.ascontiguousarray(wqkv[:, D + partner[qcols]]),
            wv_m=np.ascontiguousarray(wqkv[:, 2 * D + qcols]),
            cos_q=np.ascontiguousarray(np.broadcast_to(cos[qcols], (128, 128))),
            sin_q=np.ascontiguousarray(np.broadcast_to(sin_eff[qcols], (128, 128))),
            cos_k=np.ascontiguousarray(np.broadcast_to(cos[qcols], (128, 128))),
            sin_k=np.ascontiguousarray(np.broadcast_to(sin_eff[qcols], (128, 128))),
            w_out=w_out, wgT=wgT, wuT=wuT, wdT=wdT, qks=qks,
            rank=np.array([[c]], dtype=np.int32),
        ))

    nc = build_program()
    res = run_bass_kernel_spmd(nc, in_maps, core_ids=list(range(NCORE)))
    y = np.empty((128, 128, D), dtype=np.float32)
    for c in range(NCORE):
        y[c::8] = np.asarray(res.results[c]["out"]).reshape(NT, 128, D)
    return y.reshape(NSEQ, SEQ, D)


# version 11
# speedup vs baseline: 1.0154x; 1.0154x over previous
import os
from contextlib import ExitStack

import numpy as np
import ml_dtypes

import concourse.bass as bass
import concourse.mybir as mybir
from concourse.bass_types import DynSlice
from concourse.bass_utils import run_bass_kernel_spmd

F32 = mybir.dt.float32
BF16 = mybir.dt.bfloat16
I8 = mybir.dt.int8
I32 = mybir.dt.int32
AF = mybir.ActivationFunctionType
ALU = mybir.AluOpType
AX = mybir.AxisListType.X
SP = mybir.EngineType.SP

BF = ml_dtypes.bfloat16

_STAGE = "full"
_PARANOID = 0

D = 1024
INNER = 4096
SEQ = 4096
NSEQ = 4
NCORE = 8
NT = 16
NB = 4
TPB = 4
LN_EPS = 1e-5

ENGS = ("sync", "gpsimd", "tensor", "vector", "scalar")


def _rope_consts():
    inv = np.float32(10000.0) ** (-(np.arange(0, D, 2, dtype=np.float32) / np.float32(D)))
    emb = np.repeat(inv, 2)
    ang = np.float32(SEQ) * emb
    sin = np.sin(ang).astype(np.float32)
    cos = np.cos(ang).astype(np.float32)
    idx = np.arange(D)
    partner = np.where(idx < D // 2, 2 * idx + 1, 2 * idx - D)
    sin_eff = np.where(idx < D // 2, -sin, sin).astype(np.float32)
    return cos, sin_eff, partner


class Sem:
    def __init__(self, nc, name):
        self.h = nc.alloc_semaphore(name)
        self.n = 0


class Sched:
    def __init__(self, nc, block, sems=None):
        self.nc = nc
        self.block = block
        self.q = {e: [] for e in ENGS}
        self.sems = sems

    def op(self, eng, fn, sem=None, k=1, waits=(), chain=True):
        waits = list(waits)
        if chain and self.sems is not None and eng in ("vector", "scalar"):
            own = self.sems["dv" if eng == "vector" else "ac"]
            if own.n > 0 and (sem is not own or True):
                waits.append((own, own.n))
            if sem is None:
                sem = own
        waits = tuple((s.h, int(c)) for (s, c) in waits if c)
        if sem is not None:
            sem.n += k
        h = sem.h if sem is not None else None

        def run(e, fn=fn, waits=waits, h=h, kk=k):
            for (sh, c) in waits:
                e.wait_ge(sh, c)
            inst = fn(e)
            if h is not None:
                inst.then_inc(h, kk)

        self.q[eng].append(run)
        return sem.n if sem is not None else None

    def wait(self, eng, sem, n):
        self.q[eng].append(lambda e, sh=sem.h, n=int(n): e.wait_ge(sh, n))

    def dma(self, eng, fn, sem, waits=()):
        return self.op(eng, fn, sem=sem, k=16, waits=waits)

    def flush(self):
        for eng in ENGS:
            lst = self.q[eng]
            if not lst:
                continue

            def body(e, lst=lst):
                for f in lst:
                    f(e)

            getattr(self.block, eng)(body)
        self.q = {e: [] for e in ENGS}

    def barrier(self, bar):
        self.op("sync", lambda e: e.nop(), sem=bar, k=1)
        self.op("gpsimd", lambda e: e.engine_nop(), sem=bar, k=1)
        self.op("vector", lambda e: e.engine_nop(), sem=bar, k=1)
        self.op("scalar", lambda e: e.activation(
            self.scratch11[:, :], self.scratch11[:, :], AF.Identity), sem=bar, k=1)
        self.op("tensor", lambda e: e.nop(), sem=bar, k=1)
        t = bar.n
        for eng in ENGS:
            self.wait(eng, bar, t)

    def maybe_bar(self, bar, level):
        if _PARANOID >= level:
            self.barrier(bar)



def build_program():
    nc = bass.Bass("TRN2", target_bir_lowering=False, debug=False)

    x_in = nc.dram_tensor("x", [2048, D], F32, kind="ExternalInput")
    g_in = nc.dram_tensor("g_t", [128, D], F32, kind="ExternalInput")
    b_in = nc.dram_tensor("b_t", [128, D], F32, kind="ExternalInput")
    wqm_in = nc.dram_tensor("wq_m", [D, 128], F32, kind="ExternalInput")
    wqr_in = nc.dram_tensor("wq_r", [D, 128], F32, kind="ExternalInput")
    wkm_in = nc.dram_tensor("wk_m", [D, 128], F32, kind="ExternalInput")
    wkr_in = nc.dram_tensor("wk_r", [D, 128], F32, kind="ExternalInput")
    wvm_in = nc.dram_tensor("wv_m", [D, 128], F32, kind="ExternalInput")
    cosq_in = nc.dram_tensor("cos_q", [128, 128], F32, kind="ExternalInput")
    sinq_in = nc.dram_tensor("sin_q", [128, 128], F32, kind="ExternalInput")
    cosk_in = nc.dram_tensor("cos_k", [128, 128], F32, kind="ExternalInput")
    sink_in = nc.dram_tensor("sin_k", [128, 128], F32, kind="ExternalInput")
    wout_in = nc.dram_tensor("w_out", [D, D], F32, kind="ExternalInput")
    wgT_in = nc.dram_tensor("wgT", [D, INNER], F32, kind="ExternalInput")
    wuT_in = nc.dram_tensor("wuT", [D, INNER], F32, kind="ExternalInput")
    wdT_in = nc.dram_tensor("wdT", [INNER, D], F32, kind="ExternalInput")
    qks_in = nc.dram_tensor("qks", [1, 1], F32, kind="ExternalInput")
    rank_in = nc.dram_tensor("rank", [1, 1], I32, kind="ExternalInput")
    out = nc.dram_tensor("out", [2048, D], F32, kind="ExternalOutput")

    wgq_d = nc.dram_tensor("wgq_d", [8, 128, 8, 512], BF16)
    wuq_d = nc.dram_tensor("wuq_d", [8, 128, 8, 512], BF16)
    wdq_d = nc.dram_tensor("wdq_d", [INNER, D], BF16)
    cc1a_in = nc.dram_tensor("cc1a_in", [8, 128, 1024], BF16)
    cc1b_in = nc.dram_tensor("cc1b_in", [8, 128, 1024], BF16)
    cc1a_out = nc.dram_tensor("cc1a_out", [NCORE, 8, 128, 1024], BF16)
    cc1b_out = nc.dram_tensor("cc1b_out", [NCORE, 8, 128, 1024], BF16)
    cc2a_in = nc.dram_tensor("cc2a_in", [NCORE, 128, 8 * 128], BF16)
    cc2b_in = nc.dram_tensor("cc2b_in", [NCORE, 128, 8 * 128], BF16)
    cc2a_out = nc.dram_tensor("cc2a_out", [NCORE, NCORE, 128, 8 * 128], BF16)
    cc2b_out = nc.dram_tensor("cc2b_out", [NCORE, NCORE, 128, 8 * 128], BF16)

    I128 = nc.inline_tensor(np.eye(128, dtype=BF), name="I128")
    triu_np = (np.arange(128)[:, None] <= np.arange(128)[None, :]).astype(BF)
    TRIU = nc.inline_tensor(triu_np, name="TRIU")
    ind_sq_np = np.zeros((128, 2), dtype=BF)
    ind_sq_np[0:64, 0] = 1
    ind_sq_np[64:128, 1] = 1
    IND_SQ = nc.inline_tensor(ind_sq_np, name="IND_SQ")
    ind_bc_np = np.zeros((2, 128), dtype=BF)
    ind_bc_np[0, 0:64] = 1
    ind_bc_np[1, 64:128] = 1
    IND_BC = nc.inline_tensor(ind_bc_np, name="IND_BC")
    ONES_1x64 = nc.inline_tensor(np.ones((1, 64), dtype=np.float32), name="ONES_1x64")
    ONES_R_F = nc.inline_tensor(np.ones((1, 128), dtype=np.float32), name="ONES_R_F")
    ONES_C_F = nc.inline_tensor(np.ones((128, 1), dtype=np.float32), name="ONES_C_F")

    semnames = ("dS", "dG", "pe", "ac", "dv", "cc", "bar",
                "wq0", "wq1", "ts0", "ts1", "wg0", "wg1", "wu0", "wu1",
                "wd0", "wd1", "h2f0", "h2f1", "h2g0", "h2g1", "of0", "of1")
    sems = {n: Sem(nc, n) for n in semnames}
    dS, dG, pe, ac, dv, cc, bar = (sems[n] for n in
                                   ("dS", "dG", "pe", "ac", "dv", "cc", "bar"))

    with ExitStack() as stack:
        block = stack.enter_context(nc.Block())
        S = Sched(nc, block, sems=sems)
        S.scratch11 = stack.enter_context(nc.sbuf_tensor("bar_scr", [1, 1], F32))

        def sb(name, shape, dt=F32):
            return stack.enter_context(nc.sbuf_tensor(name, shape, dt))

        resid = sb("resid", [128, NT * D])
        stats = sb("stats", [128, 160])
        cols = sb("cols", [128, 16])
        sc11 = sb("sc11", [1, 8])
        i128_sb = sb("i128_sb", [128, 128], BF16)
        triu_sb = sb("triu_sb", [128, 128], BF16)
        ind_sq_sb = sb("ind_sq_sb", [128, 2], BF16)
        ind_bc_sb = sb("ind_bc_sb", [2, 128], BF16)
        ones64_sb = sb("ones64_sb", [1, 64], F32)
        onesrf_sb = sb("onesrf_sb", [1, 128], F32)
        onescf_sb = sb("onescf_sb", [128, 1], F32)
        rank_sb = sb("rank_sb", [1, 1], I32)
        qks_sb = sb("qks_sb", [1, 1], F32)

        consts = dict(resid=resid, stats=stats, cols=cols, sc11=sc11,
                      i128=i128_sb, triu=triu_sb, ind_sq=ind_sq_sb,
                      ind_bc=ind_bc_sb, ones64=ones64_sb, onesrf=onesrf_sb,
                      onescf=onescf_sb, rank=rank_sb, qks=qks_sb, out=out)

        S.op("vector", lambda e: e.memset(cols[:, 8:9], LN_EPS), sem=sems["dv"])
        for t in range(NT):
            S.dma("sync", lambda e, t=t: e.dma_start(
                out=resid[:, t * D:(t + 1) * D],
                in_=x_in.ap().rearrange("(t p) c -> t p c", p=128)[t]), dS)
        for dst, src in ((i128_sb, I128), (triu_sb, TRIU), (ind_sq_sb, IND_SQ),
                         (ind_bc_sb, IND_BC), (ones64_sb, ONES_1x64),
                         (onesrf_sb, ONES_R_F), (onescf_sb, ONES_C_F),
                         (rank_sb, rank_in), (qks_sb, qks_in)):
            S.dma("sync", lambda e, d=dst, s=src: e.dma_start(out=d[:, :], in_=s[:, :]), dS)
        n_x = dS.n
        n_consts = dS.n

        with nc.psum_tensor("ps_misc", [128, 8], F32) as ps_misc:
            def serial0(eng, fn, sem, extra=()):
                S.op(eng, fn, sem=sem,
                     waits=list(extra) + [(pe, pe.n), (ac, ac.n), (dv, dv.n)])

            serial0("scalar", lambda e: e.activation(sc11[:, 0:1], qks_sb[:, :],
                                                     AF.Abs), ac,
                    extra=[(dS, n_consts)])
            serial0("tensor", lambda e: e.matmul(ps_misc[:, 0:1], onesrf_sb[:, :],
                                                 qks_sb[:, :], start=True, stop=True),
                    pe, extra=[(dS, n_consts)])
            serial0("tensor", lambda e: e.matmul(ps_misc[:, 1:2], onesrf_sb[:, :],
                                                 sc11[:, 0:1], start=True, stop=True),
                    pe)
            serial0("scalar", lambda e: e.activation(cols[:, 6:7], ps_misc[:, 0:1],
                                                     AF.Copy), ac)
            serial0("scalar", lambda e: e.activation(cols[:, 7:8], ps_misc[:, 1:2],
                                                     AF.Copy, scale=-1.0), ac)

            mats = ((wgT_in, wgq_d, 8, INNER, 0, 3, 1.0 / 127.0),
                    (wuT_in, wuq_d, 8, INNER, 1, 4, 1.0 / 127.0),
                    (wdT_in, wdq_d, 32, D, 2, 5, 0.5 / 127.0))
            with ExitStack() as wst:
                s0dump = wst.enter_context(
                    nc.sbuf_tensor("s0_dump", [128, D], F32))
                B = _WQBufs(wst, nc, sems)
                absc = [wst.enter_context(
                    nc.sbuf_tensor(f"s0_ac{m}", [128, mats[m][2]], F32))
                    for m in range(3)]
                csum = [wst.enter_context(
                    nc.sbuf_tensor(f"s0_cs{m}", [128, 1], F32)) for m in range(3)]

                n_sacc_dv, n_sacc_ac = _ln_stats_acc(nc, S, sems, consts, s0dump,
                                                     (dS, n_x))
                n_sc = []
                for (w_in, wq_d, nck, wid, cs_, ci_, im_), am, cm in zip(
                        mats, absc, csum):
                    _wq_pass1(nc, S, sems, B, w_in, nck, wid, am)
                    n_sc.append(_wq_chain(nc, S, sems, am, cm, nck, wid, cols,
                                          cs_, ci_, im_, sc11, onesrf_sb,
                                          onescf_sb, ps_misc))
                n_stats_f1 = _ln_stats_tail(nc, S, sems, consts, n_sacc_ac)
                for (w_in, wq_d, nck, wid, cs_, ci_, im_), ns in zip(mats, n_sc):
                    _wq_pass2(nc, S, sems, B, w_in, wq_d, nck, wid, cols, cs_, ns)
                S.wait("sync", sems["ts0"], sems["ts0"].n)
                S.wait("sync", sems["ts1"], sems["ts1"].n)
                S.barrier(bar)
                S.flush()

        if _STAGE == "dbg_wq":
            S.dma("gpsimd", lambda e: e.dma_start(
                out=out.ap()[0:512, :].rearrange("(r s) c -> r s c", s=4),
                in_=wgq_d.ap()[0:128, :].rearrange("r (s c) -> r s c", c=1024)),
                dG)
            S.dma("sync", lambda e: e.dma_start(
                out=out.ap()[512:640, 0:16], in_=cols[:, :]), dS)
            S.dma("sync", lambda e: e.dma_start(
                out=out.ap()[640:641, 0:8], in_=sc11[:, :]), dS)
            S.wait("gpsimd", dG, dG.n)
            S.wait("gpsimd", dS, dS.n)
            S.flush()
            return nc
        if _STAGE in ("dbg_stats", "dbg_red"):
            with nc.sbuf_tensor("dbg_dump", [128, D], F32) as ddump, \
                 nc.sbuf_tensor("dbg_snap", [128, 112], F32) as dsnap:
                consts["snap"] = dsnap
                _ln_stats(nc, S, sems, consts, ddump, (dS, n_x),
                          chain=(_STAGE == "dbg_stats"))
                S.barrier(bar)
                S.dma("sync", lambda e: e.dma_start(
                    out=out.ap()[0:128, 0:160], in_=consts["stats"][:, :]), dS)
                S.dma("sync", lambda e: e.dma_start(
                    out=out.ap()[128:256, 0:112], in_=dsnap[:, :]), dS)
                S.wait("gpsimd", dS, dS.n)
                S.flush()
                consts.pop("snap")
            return nc

        if _STAGE == "dbg_xq":
            with nc.sbuf_tensor("dbg_dump", [128, D], F32) as ddump, \
                 nc.sbuf_tensor("dbg_x8", [128, D], I8) as x8, \
                 nc.sbuf_tensor("dbg_xb", [128, D], BF16) as xb:
                n_st = _ln_stats(nc, S, sems, consts, ddump, (dS, n_x))
                stats = consts["stats"]
                for t in range(NT):
                    S.op("vector", lambda e, t=t: e.tensor_scalar(
                        x8[:, :], resid[:, t * D:(t + 1) * D],
                        stats[:, 0 + t:1 + t], stats[:, 16 + t:17 + t],
                        ALU.subtract, ALU.mult),
                        sem=sems["dv"], waits=[(sems["dv"], n_st)] if t == 0 else [])
                    S.op("vector", lambda e: e.tensor_copy(xb[:, :], x8[:, :]),
                         sem=sems["dv"])
                    n_q = sems["dv"].n
                    S.dma("gpsimd", lambda e, t=t: e.dma_start(
                        out=out.ap().rearrange("(t p) c -> t p c", p=128)[t],
                        in_=xb[:, :]), dG, waits=[(sems["dv"], n_q)])
                    S.wait("vector", dG, dG.n)
                S.wait("gpsimd", dG, dG.n)
                S.flush()
            return nc

        with ExitStack() as p2st:
            p2_accs, p2_emit, p2_state = _p2_setup(nc, S, sems, consts,
                                                    p2st, g_in, b_in,
                                                    cc1a_in, cc1b_in,
                                                    cc1a_out, cc1b_out)

            def post_block(b, ffnctx):
                p2_accs(b, ffnctx)
                if b == 1:
                    p2_emit(0, ffnctx)
                elif b == 3:
                    p2_emit(1, ffnctx)

            _ffn_phase(nc, S, sems, consts, wgq_d, wuq_d, wdq_d,
                       wait_x=(dS, n_x), tag="f1", stats_n=n_stats_f1,
                       post_block=post_block)
        ccA_done, ccB_done = p2_state["cc_done"]
        if _STAGE == "dbg_raw":
            S.flush()
            return nc
        S.barrier(bar)
        S.flush()

        if _STAGE == "ffn1":
            _store_out(nc, S, sems, resid, out)
            S.flush()
            return nc

        with ExitStack() as astack:
            def asb(name, shape, dt=F32):
                return astack.enter_context(nc.sbuf_tensor(name, shape, dt))

            qnf = asb("qnf", [128, NSEQ * SEQ], BF16)
            knf = asb("knf", [128, NSEQ * SEQ], BF16)
            vres = asb("vres", [128, 128 * 130], BF16)

            _p3_qkv(nc, S, astack, sems, consts,
                    (wqm_in, wqr_in, wkm_in, wkr_in, wvm_in),
                    (cosq_in, sinq_in, cosk_in, sink_in),
                    cc1a_out, ccA_done, cc1b_out, ccB_done, qnf, knf, vres)
            if _STAGE in ("dbg_wro", "dbg_h2g"):
                S.flush()
                return nc
            S.barrier(bar)
            S.flush()

            if _STAGE in ("dbg_qnf", "dbg_knf", "dbg_vres"):
                src_t = {"dbg_qnf": qnf, "dbg_knf": knf, "dbg_vres": vres}[_STAGE]
                ncol = min(src_t.shape[1], 16384)
                S.dma("gpsimd", lambda e: e.dma_start(
                    out=out.ap().rearrange("(t p) c -> p t c", p=128)
                    [:, 0:ncol // D, :],
                    in_=src_t.ap()[:, 0:(ncol // D) * D]
                    .rearrange("p (t c) -> p t c", c=D)), sems["dG"])
                S.wait("gpsimd", sems["dG"], sems["dG"].n)
                S.flush()
                return nc

            cc2h = dict(a_in=cc2a_in, b_in=cc2b_in, a_out=cc2a_out,
                        ccA_done=None)
            if _STAGE == "dbg_p0":
                try:
                    _p4_attn(nc, S, astack, sems, consts, qnf, knf, vres, cc2h)
                except StopIteration:
                    pass
                return nc
            _p4_attn(nc, S, astack, sems, consts, qnf, knf, vres, cc2h)
            if _STAGE == "dbg_of":
                S.wait("gpsimd", sems["of0"], sems["of0"].n)
                S.wait("gpsimd", sems["of1"], sems["of1"].n)
                for r in range(NCORE):
                    for sl in range(2):
                        S.dma("gpsimd", lambda e, r=r, sl=sl: e.dma_start(
                            out=out.ap().rearrange("(a p) c -> a p c", p=128)
                            [r * 2 + sl],
                            in_=(cc2a_in if sl == 0 else cc2b_in).ap()[r]),
                            sems["dG"])
                S.wait("gpsimd", sems["dG"], sems["dG"].n)
                S.flush()
                return nc
            S.op("gpsimd", lambda e: e.collective_compute(
                "AllGather", ALU.bypass, replica_groups=[list(range(NCORE))],
                ins=[cc2b_in.ap().opt()], outs=[cc2b_out.ap().opt()]),
                sem=cc, waits=[(sems["of0"], sems["of0"].n),
                               (sems["of1"], sems["of1"].n)])
            ccB_done = cc.n
            S.flush()

        _p5_oproj(nc, S, sems, consts, cc2a_out, cc2h["ccA_done"],
                  cc2b_out, ccB_done, wout_in)
        S.barrier(bar)
        S.flush()

        if _STAGE == "attn":
            _store_out(nc, S, sems, resid, out)
            S.flush()
            return nc

        _ffn_phase(nc, S, sems, consts, wgq_d, wuq_d, wdq_d,
                   wait_x=None, tag="f2")
        S.barrier(bar)
        S.flush()

        _store_out(nc, S, sems, resid, out)
        S.flush()

    return nc



class _WQBufs:

    def __init__(self, st, nc, sems):
        self.wbuf = [st.enter_context(nc.sbuf_tensor(f"s0wb{i}", [128, INNER], F32))
                     for i in range(4)]
        self.t8 = st.enter_context(nc.sbuf_tensor("s0t8", [128, INNER], I8))
        self.tern = [st.enter_context(nc.sbuf_tensor(f"s0tn{i}", [128, INNER], BF16))
                     for i in range(2)]
        self.q = [("sync", sems["wq0"]), ("gpsimd", sems["wq1"])]
        self.sq = [("sync", sems["ts0"]), ("gpsimd", sems["ts1"])]
        self.last_read = [None, None, None, None]
        self.last_store = [None, None]


def _wq_pass1(nc, S, sems, B, w_in, nchunk, width, abscols):
    dv = sems["dv"]
    wv = w_in.ap().rearrange("(k p) w -> k p w", p=128)
    for k in range(nchunk):
        s = k % 4
        eng, sem = B.q[k % 2]
        w = [B.last_read[s]] if B.last_read[s] else []
        S.dma(eng, lambda e, k=k, s=s: e.dma_start(
            out=B.wbuf[s][:, 0:width], in_=wv[k]), sem, waits=w)
        n_load = sem.n
        S.op("vector", lambda e, k=k, s=s: e.reduce_sum(
            abscols[:, k:k + 1], B.wbuf[s][:, 0:width], AX,
            apply_absolute_value=True), sem=dv, waits=[(sem, n_load)])
        B.last_read[s] = (dv, dv.n)


def _wq_chain(nc, S, sems, abscols, colsum, nchunk, width, cols, col_s, col_inv,
              inv_mult, sc11, onesrf, onescf, ps_misc):
    pe, ac, dv = sems["pe"], sems["ac"], sems["dv"]

    def serial(eng, fn, sem):
        S.op(eng, fn, sem=sem,
             waits=[(pe, pe.n), (ac, ac.n), (dv, dv.n)])

    serial("vector", lambda e: e.reduce_sum(colsum[:, :], abscols[:, 0:nchunk],
                                            AX), dv)
    serial("tensor", lambda e: e.matmul(ps_misc[0:1, 2:3], colsum[:, :],
                                        onescf[:, :], start=True, stop=True), pe)
    serial("scalar", lambda e: e.activation(sc11[:, 1:2], ps_misc[0:1, 2:3],
                                            AF.Copy,
                                            scale=1.0 / (128 * nchunk * width)),
           ac)
    serial("vector", lambda e: e.tensor_scalar_max(sc11[:, 2:3], sc11[:, 1:2],
                                                   1e-5), dv)
    serial("vector", lambda e: e.reciprocal(sc11[:, 3:4], sc11[:, 2:3]), dv)
    serial("vector", lambda e: e.tensor_single_scalar(sc11[:, 4:5], sc11[:, 2:3],
                                                      inv_mult, ALU.mult), dv)
    serial("tensor", lambda e: e.matmul(ps_misc[:, 3:4], onesrf[:, :],
                                        sc11[:, 3:4], start=True, stop=True), pe)
    serial("tensor", lambda e: e.matmul(ps_misc[:, 4:5], onesrf[:, :],
                                        sc11[:, 4:5], start=True, stop=True), pe)
    serial("scalar", lambda e: e.activation(cols[:, col_s:col_s + 1],
                                            ps_misc[:, 3:4], AF.Copy), ac)
    serial("scalar", lambda e: e.activation(cols[:, col_inv:col_inv + 1],
                                            ps_misc[:, 4:5], AF.Copy), ac)
    return ac.n


def _wq_pass2(nc, S, sems, B, w_in, wq_d, nchunk, width, cols, col_s, n_scale):
    ac, dv = sems["ac"], sems["dv"]
    wv = w_in.ap().rearrange("(k p) w -> k p w", p=128)
    if len(wq_d.shape) == 4:
        wqv = [wq_d.ap()[:, :, k, :].rearrange("ic p j -> p ic j")
               for k in range(nchunk)]
    else:
        w4 = wq_d.ap().rearrange("(k p) w -> k p w", p=128)
        wqv = [w4[k] for k in range(nchunk)]

    for k in range(nchunk):
        s = k % 4
        ts_ = k % 2
        eng, sem = B.q[k % 2]
        w = [B.last_read[s]] if B.last_read[s] else []
        S.dma(eng, lambda e, k=k, s=s: e.dma_start(
            out=B.wbuf[s][:, 0:width], in_=wv[k]), sem, waits=w)
        n_load = sem.n
        wts = [(sem, n_load)]
        if k < 2:
            wts.append((ac, n_scale))
        S.op("vector", lambda e, s=s: e.tensor_single_scalar(
            B.t8[:, 0:width], B.wbuf[s][:, 0:width], cols[:, col_s:col_s + 1],
            ALU.mult), sem=dv, waits=wts)
        B.last_read[s] = (dv, dv.n)
        w2 = [B.last_store[ts_]] if B.last_store[ts_] else []
        S.op("vector", lambda e, ts_=ts_: e.tensor_scalar(
            B.tern[ts_][:, 0:width], B.t8[:, 0:width], 1.0, -1.0, ALU.min,
            ALU.max), sem=dv, waits=w2)
        n_tern = dv.n
        seng, ssem = B.sq[ts_]
        if len(wq_d.shape) == 4:
            S.dma(seng, lambda e, k=k, ts_=ts_: e.dma_start(
                out=wqv[k],
                in_=B.tern[ts_].ap().rearrange("p (ic j) -> p ic j", j=512)),
                ssem, waits=[(dv, n_tern)])
        else:
            S.dma(seng, lambda e, k=k, ts_=ts_: e.dma_start(
                out=wqv[k], in_=B.tern[ts_][:, 0:width]), ssem,
                waits=[(dv, n_tern)])
        B.last_store[ts_] = (ssem, ssem.n)



def _ln_stats_acc(nc, S, sems, consts, dump, wait_x):
    dS, pe, ac, dv = (sems[n] for n in ("dS", "pe", "ac", "dv"))
    resid, stats, cols = consts["resid"], consts["stats"], consts["cols"]

    S.op("scalar", lambda e: e.preload_act_table(AF.Sqrt), sem=ac)

    sumc = stats[:, 96:112]
    sqc = stats[:, 112:128]
    rmx = stats[:, 128:144]
    rmn = stats[:, 144:160]
    for t in range(NT):
        w = [wait_x] if (wait_x and t == 0) else []
        S.op("scalar", lambda e, t=t: e.activation(
            dump[:, 0:D], resid[:, t * D:(t + 1) * D], AF.Copy,
            accum_out=sumc[:, t:t + 1]), sem=ac, waits=w)
        S.op("scalar", lambda e, t=t: e.activation(
            dump[:, 0:D], resid[:, t * D:(t + 1) * D], AF.Square,
            accum_out=sqc[:, t:t + 1]), sem=ac)
        S.op("vector", lambda e, t=t: e.reduce_max(
            rmx[:, t:t + 1], resid[:, t * D:(t + 1) * D], AX),
            sem=dv, waits=([wait_x] if (wait_x and t == 0) else []))
        S.op("vector", lambda e, t=t: e.tensor_reduce(
            rmn[:, t:t + 1], resid[:, t * D:(t + 1) * D], AX, ALU.min), sem=dv)
    return dv.n, ac.n


def _ln_stats(nc, S, sems, consts, dump, wait_x, chain=True):
    n_dv, n_acc = _ln_stats_acc(nc, S, sems, consts, dump, wait_x)
    if not chain:
        return n_dv
    return _ln_stats_tail(nc, S, sems, consts, n_acc)


def _ln_stats_tail(nc, S, sems, consts, n_acc):
    dS, pe, ac, dv = (sems[n] for n in ("dS", "pe", "ac", "dv"))
    resid, stats, cols = consts["resid"], consts["stats"], consts["cols"]
    sumc = stats[:, 96:112]
    sqc = stats[:, 112:128]
    rmx = stats[:, 128:144]
    rmn = stats[:, 144:160]

    mu = stats[:, 0:16]
    a_q = stats[:, 16:32]
    c_g = stats[:, 32:48]
    c_u = stats[:, 48:64]
    t1 = stats[:, 64:80]
    t2 = stats[:, 80:96]

    S.op("vector", lambda e: e.tensor_single_scalar(mu[:, :], sumc[:, :], 1.0 / D,
                                                    ALU.mult),
         sem=dv, waits=[(ac, n_acc)])
    S.op("vector", lambda e: e.tensor_single_scalar(sqc[:, :], sqc[:, :], 1.0 / D,
                                                    ALU.mult), sem=dv)
    S.op("vector", lambda e: e.tensor_mul(t1[:, :], mu[:, :], mu[:, :]), sem=dv)
    S.op("vector", lambda e: e.tensor_sub(sqc[:, :], sqc[:, :], t1[:, :]), sem=dv)
    n_var = dv.n
    S.op("scalar", lambda e: e.activation(t1[:, :], sqc[:, :], AF.Sqrt,
                                          bias=cols[:, 8:9]),
         sem=ac, waits=[(dv, n_var)])
    n_std = ac.n
    S.op("vector", lambda e: e.reciprocal(t1[:, :], t1[:, :]),
         sem=dv, waits=[(ac, n_std)])
    n_rstd = dv.n
    snap = consts.get("snap")

    def snp(j):
        if snap is not None:
            S.op("vector", lambda e, j=j: e.tensor_copy(
                snap[:, 16 * j:16 * j + 16],
                stats[:, 128:144] if j < 5 else (stats[:, 80:96] if j == 5
                                                 else stats[:, 16:32])), sem=dv)

    S.op("vector", lambda e: e.tensor_sub(rmx[:, :], rmx[:, :], mu[:, :]),
         sem=dv, waits=[(dv, n_rstd)])
    snp(0)
    S.op("vector", lambda e: e.tensor_sub(rmn[:, :], mu[:, :], rmn[:, :]), sem=dv)
    S.op("vector", lambda e: e.tensor_max(rmx[:, :], rmx[:, :], rmn[:, :]), sem=dv)
    snp(1)
    S.op("vector", lambda e: e.tensor_mul(rmx[:, :], rmx[:, :], t1[:, :]), sem=dv)
    snp(2)
    S.op("vector", lambda e: e.tensor_scalar_max(rmx[:, :], rmx[:, :], 1e-5), sem=dv)
    snp(3)
    snp(4)
    S.op("vector", lambda e: e.reciprocal(t2[:, :], rmx[:, :]), sem=dv)
    snp(5)
    S.op("vector", lambda e: e.scalar_tensor_tensor(
        a_q[:, :], t2[:, :], 127.0, t1[:, :], ALU.mult, ALU.mult), sem=dv)
    snp(6)
    S.op("vector", lambda e: e.tensor_single_scalar(
        c_g[:, :], rmx[:, :], cols[:, 3:4], ALU.mult), sem=dv)
    S.op("vector", lambda e: e.tensor_single_scalar(
        c_u[:, :], rmx[:, :], cols[:, 4:5], ALU.mult), sem=dv)
    return dv.n



def _ffn_phase(nc, S, sems, consts, wgq_d, wuq_d, wdq_d, wait_x, tag,
               stats_n=None, post_block=None):
    dS, pe, ac, dv = (sems[n] for n in ("dS", "pe", "ac", "dv"))
    resid, stats, cols, i128_sb = (consts["resid"], consts["stats"],
                                   consts["cols"], consts["i128"])
    mu = stats[:, 0:16]
    a_q = stats[:, 16:32]
    c_g = stats[:, 32:48]
    c_u = stats[:, 48:64]
    hcd = stats[:, 96:112]

    wgv = wgq_d.ap().rearrange("ic p kk j -> ic p (kk j)")
    wuv = wuq_d.ap().rearrange("ic p kk j -> ic p (kk j)")
    wdv = wdq_d.ap().rearrange("(jc p) (h w) -> jc h p w", p=128, w=512)

    with ExitStack() as fst:
        def fsb(name, shape, dt):
            return fst.enter_context(nc.sbuf_tensor(f"{tag}_{name}", shape, dt))

        wg_sb = [fsb(f"wg{i}", [128, 8 * 512], BF16) for i in range(2)]
        wu_sb = [fsb(f"wu{i}", [128, 8 * 512], BF16) for i in range(2)]
        wd_sb = [fsb(f"wd{i}", [128, 512], BF16) for i in range(2)]
        xq8 = fsb("xq8", [128, D], I8)
        xqb = fsb("xqb", [128, D], BF16)
        xqT = [fsb(f"xqT{i}", [128, D], BF16) for i in range(TPB)]
        silu_s = [fsb(f"silu{i}", [128, 512], F32) for i in range(2)]
        gq8 = fsb("gq8", [128, INNER], I8)
        mcol = fsb("mcol", [128, 8], F32)
        dump = fsb("dump", [128, D], F32)

        big = fsb("big", [128, TPB * 16384], mybir.dt.uint8)
        base = nc.lookup_mloc(big).addr
        raw_s = [nc.alloc_sbuf_tensor_at(f"{tag}_raw{i}", [128, INNER], F32,
                                         offset=base + i * 16384)
                 for i in range(TPB)]
        gqb_s = [nc.alloc_sbuf_tensor_at(f"{tag}_gqb{i}", [128, INNER], BF16,
                                         offset=base + i * 16384)
                 for i in range(TPB)]
        gqT_s = [nc.alloc_sbuf_tensor_at(f"{tag}_gqT{i}", [128, INNER], BF16,
                                         offset=base + i * 16384 + 8192)
                 for i in range(TPB)]

        if stats_n is None:
            n_stats = _ln_stats(nc, S, sems, consts, dump, wait_x)
        else:
            n_stats = stats_n
        S.maybe_bar(sems["bar"], 1)
        S.op("scalar", lambda e: e.preload_act_table(AF.Silu), sem=ac)

        with nc.psum_tensor(f"{tag}_ptr", [128, 1024], F32) as ptr, \
             nc.psum_tensor(f"{tag}_pg0", [128, 512], F32) as pg0, \
             nc.psum_tensor(f"{tag}_pg1", [128, 512], F32) as pg1, \
             nc.psum_tensor(f"{tag}_pu0", [128, 512], F32) as pu0, \
             nc.psum_tensor(f"{tag}_pu1", [128, 512], F32) as pu1, \
             nc.psum_tensor(f"{tag}_pd0", [128, 512], F32) as pd0, \
             nc.psum_tensor(f"{tag}_pd1", [128, 512], F32) as pd1:
            pg, pu = [pg0, pg1], [pu0, pu1]
            pd4 = [pg0, pg1, pd0, pd1]

            strip_loads = {}
            strip_free = {}
            tr_pe = {}
            ptr_evac = []
            n_gu_last = 0
            n_down_evac = 0
            n_down_mm = 0

            wg2 = [sems["wg0"], sems["wg1"]]
            wu2 = [sems["wu0"], sems["wu1"]]
            wd2 = [sems["wd0"], sems["wd1"]]

            def load_strips(b, ic):
                slot = ic % 2
                prevkey = (b, ic - 2) if ic >= 2 else (b - 1, ic + 6)
                prev = strip_free.get(prevkey)
                w = [(pe, prev)] if prev else []
                S.dma("sync", lambda e, ic=ic, slot=slot: e.dma_start(
                    out=wg_sb[slot][:, :], in_=wgv[ic]), wg2[slot], waits=w)
                S.dma("sync", lambda e, ic=ic, slot=slot: e.dma_start(
                    out=wu_sb[slot][:, :], in_=wuv[ic]), wu2[slot])
                strip_loads[(b, ic)] = (wg2[slot].n, wu2[slot].n)

            def emit_xq(b, n_gu_prev):
                xq_evac = {}
                for i, t in enumerate([b * TPB + j for j in range(TPB)]):
                    wq = [(dv, n_stats)] if (b == 0 and i == 0) else []
                    if i >= 1:
                        wq.append((pe, xq_evac[i - 1][1]))
                    S.op("vector", lambda e, t=t: e.tensor_scalar(
                        xq8[:, :], resid[:, t * D:(t + 1) * D],
                        mu[:, t:t + 1], a_q[:, t:t + 1], ALU.subtract, ALU.mult),
                        sem=dv, waits=wq)
                    S.op("vector", lambda e: e.tensor_copy(xqb[:, :], xq8[:, :]),
                         sem=dv)
                    n_xq = dv.n
                    for c in range(8):
                        w = []
                        if c == 0:
                            w = [(dv, n_xq)]
                            if ptr_evac:
                                w.append((ac, ptr_evac[-1]))
                        S.op("tensor", lambda e, c=c: e.matmul(
                            ptr[:, c * 128:(c + 1) * 128],
                            xqb[:, 128 * c:128 * c + 128], i128_sb[:, :],
                            start=True, stop=True),
                            sem=(pe if c == 7 else None), waits=w)
                    n_tr = pe.n
                    wev = [(pe, n_tr)]
                    if b > 0:
                        wev.append((pe, n_gu_prev))
                    S.op("scalar", lambda e, i=i: e.activation(
                        xqT[i][:, 0:512], ptr[:, 0:512], AF.Copy),
                        sem=ac, waits=wev)
                    S.op("scalar", lambda e, i=i: e.activation(
                        xqT[i][:, 512:1024], ptr[:, 512:1024], AF.Copy), sem=ac)
                    ptr_evac.append(ac.n)
                    xq_evac[i] = (ac.n, n_tr)
                return ac.n

            xq_all = {0: emit_xq(0, 0)}
            for b in range(NB):
                tiles = [b * TPB + i for i in range(TPB)]
                n_xq_all = xq_all[b]

                if (b, 0) not in strip_loads:
                    load_strips(b, 0)
                if (b, 1) not in strip_loads:
                    load_strips(b, 1)
                evac_g, evac_u = {}, {}
                for ic in range(8):
                    n_wg, n_wu = strip_loads[(b, ic)]
                    for i, t in enumerate(tiles):
                        ps = i % 2
                        w_mm = [(wg2[ic % 2], n_wg), (wu2[ic % 2], n_wu),
                                (ac, n_xq_all)]
                        if b > 0 and ic == 0 and i < 2:
                            w_mm.append((dv, n_down_evac))
                        prev = (ic, i - 2) if i >= 2 else ((ic - 1, i + 2) if ic >= 1 else None)
                        if prev in evac_g:
                            w_mm.append((ac, evac_g[prev]))
                        if prev in evac_u:
                            w_mm.append((dv, evac_u[prev]))
                        for kk in range(8):
                            S.op("tensor", lambda e, i=i, kk=kk, ps=ps, ic=ic: e.matmul(
                                pg[ps][:, :], xqT[i][:, 128 * kk:128 * kk + 128],
                                wg_sb[ic % 2][:, 512 * kk:512 * kk + 512],
                                start=(kk == 0), stop=(kk == 7)),
                                sem=(pe if kk == 7 else None),
                                waits=(w_mm if kk == 0 else []))
                        n_g = pe.n
                        for kk in range(8):
                            S.op("tensor", lambda e, i=i, kk=kk, ps=ps, ic=ic: e.matmul(
                                pu[ps][:, :], xqT[i][:, 128 * kk:128 * kk + 128],
                                wu_sb[ic % 2][:, 512 * kk:512 * kk + 512],
                                start=(kk == 0), stop=(kk == 7)),
                                sem=(pe if kk == 7 else None))
                        n_u = pe.n
                        strip_free[(b, ic)] = n_u
                        w_silu = [(pe, n_g)]
                        if prev in evac_u:
                            w_silu.append((dv, evac_u[prev]))
                        S.op("scalar", lambda e, t=t, ps=ps: e.activation(
                            silu_s[ps][:, :], pg[ps][:, :], AF.Silu,
                            scale=c_g[:, t:t + 1]),
                            sem=ac, waits=w_silu)
                        evac_g[(ic, i)] = ac.n
                        w_raw = [(ac, evac_g[(ic, i)]), (pe, n_u)]
                        if b > 0 and ic == 0:
                            w_raw.append((pe, n_down_mm))
                        S.op("vector", lambda e, i=i, ic=ic, ps=ps: e.tensor_mul(
                            raw_s[i][:, 512 * ic:512 * ic + 512], silu_s[ps][:, :],
                            pu[ps][:, :]),
                            sem=dv, waits=w_raw)
                        evac_u[(ic, i)] = dv.n
                    S.maybe_bar(sems["bar"], 2)
                    if ic + 2 < 8:
                        load_strips(b, ic + 2)
                    elif b + 1 < NB:
                        load_strips(b + 1, ic + 2 - 8)
                n_gu_last = pe.n

                if _STAGE == "dbg_raw" and b == 0:
                    for i in range(TPB):
                        S.dma("gpsimd", lambda e, i=i: e.dma_start(
                            out=consts["out"].ap().rearrange(
                                "(f p) c -> f p c", p=128)[4 * i:4 * i + 4],
                            in_=raw_s[i].ap().rearrange("p (f c) -> f p c", c=D)),
                            sems["dG"], waits=[(dv, evac_u[(7, i)])])
                    S.wait("gpsimd", sems["dG"], sems["dG"].n)
                    return

                for i, t in enumerate(tiles):
                    n_raw = evac_u[(7, i)]
                    S.op("vector", lambda e, i=i: e.reduce_max(
                        mcol[:, 0:1], raw_s[i][:, :], AX, apply_absolute_value=True),
                        sem=dv, waits=[(dv, n_raw)])
                    S.op("vector", lambda e, t=t: e.tensor_single_scalar(
                        mcol[:, 1:2], mcol[:, 0:1], c_u[:, t:t + 1], ALU.mult), sem=dv)
                    S.op("vector", lambda e: e.tensor_scalar_max(
                        mcol[:, 1:2], mcol[:, 1:2], 1e-5), sem=dv)
                    S.op("vector", lambda e: e.reciprocal(mcol[:, 2:3], mcol[:, 1:2]),
                         sem=dv)
                    S.op("vector", lambda e, t=t: e.scalar_tensor_tensor(
                        mcol[:, 3:4], mcol[:, 2:3], 127.0, c_u[:, t:t + 1],
                        ALU.mult, ALU.mult), sem=dv)
                    S.op("vector", lambda e, t=t: e.tensor_single_scalar(
                        hcd[:, t:t + 1], mcol[:, 1:2], cols[:, 5:6], ALU.mult), sem=dv)
                    wq8 = []
                    if i >= 1:
                        wq8.append((pe, tr_pe[i - 1]))
                    S.op("vector", lambda e, i=i: e.tensor_single_scalar(
                        gq8[:, :], raw_s[i][:, :], mcol[:, 3:4], ALU.mult),
                        sem=dv, waits=wq8)
                    S.op("vector", lambda e, i=i: e.tensor_copy(
                        gqb_s[i][:, :], gq8[:, :]), sem=dv)
                    n_gqb = dv.n
                    for grp in range(8):
                        half = (grp % 2) * 512
                        for c in range(4):
                            jc = grp * 4 + c
                            w = []
                            if c == 0:
                                if grp == 0:
                                    w.append((dv, n_gqb))
                                if ptr_evac:
                                    w.append((ac, ptr_evac[-1] if grp < 2 else ptr_evac[-2]))
                            S.op("tensor", lambda e, i=i, jc=jc, half=half, c=c: e.matmul(
                                ptr[:, half + c * 128: half + c * 128 + 128],
                                gqb_s[i][:, 128 * jc:128 * jc + 128], i128_sb[:, :],
                                start=True, stop=True),
                                sem=(pe if c == 3 else None), waits=w)
                        n_tr = pe.n
                        S.op("scalar", lambda e, i=i, grp=grp, half=half: e.activation(
                            gqT_s[i][:, grp * 512: grp * 512 + 512],
                            ptr[:, half: half + 512], AF.Copy),
                            sem=ac, waits=[(pe, n_tr)])
                        ptr_evac.append(ac.n)
                    tr_pe[i] = pe.n
                    S.maybe_bar(sems["bar"], 2)
                n_gqT_ready = ac.n

                ev_prev = n_down_evac
                wd_slot_free = getattr(_ffn_phase, "_wd_free", {0: 0, 1: 0})
                for half in range(2):
                    first_mm_wait = [(ac, n_gqT_ready)]
                    if half == 0:
                        first_mm_wait.append((ac, evac_g[(7, 3)]))
                        first_mm_wait.append((dv, evac_u[(7, 3)]))
                        if b > 0:
                            first_mm_wait.append((dv, ev_prev))
                    else:
                        first_mm_wait.append((dv, n_down_evac))
                    for jc in range(32):
                        slot = jc % 2
                        wl = [(pe, wd_slot_free[slot])] if wd_slot_free[slot] else []
                        S.dma("sync", lambda e, jc=jc, half=half, slot=slot: e.dma_start(
                            out=wd_sb[slot][:, :], in_=wdv[jc, half]), wd2[slot],
                            waits=wl)
                        n_wd = wd2[slot].n
                        for i, t in enumerate(tiles):
                            w = []
                            if jc == 0:
                                w = list(first_mm_wait) if i == 0 else []
                            if i == 0:
                                w.append((wd2[slot], n_wd))
                            S.op("tensor", lambda e, i=i, jc=jc, slot=slot: e.matmul(
                                pd4[i][:, :], gqT_s[i][:, 128 * jc:128 * jc + 128],
                                wd_sb[slot][:, :],
                                start=(jc == 0), stop=(jc == 31)),
                                sem=(pe if i == TPB - 1 else None), waits=w)
                        wd_slot_free[slot] = pe.n
                    n_dmm = pe.n
                    for i, t in enumerate(tiles):
                        S.op("vector", lambda e, t=t, i=i, half=half: e.scalar_tensor_tensor(
                            resid[:, t * D + half * 512: t * D + half * 512 + 512],
                            pd4[i][:, :], hcd[:, t:t + 1],
                            resid[:, t * D + half * 512: t * D + half * 512 + 512],
                            ALU.mult, ALU.add),
                            sem=dv, waits=[(pe, n_dmm)])
                    n_down_evac = dv.n
                    if half == 0 and b + 1 < NB:
                        xq_all[b + 1] = emit_xq(b + 1, n_gu_last)
                    S.maybe_bar(sems["bar"], 2)
                n_down_mm = pe.n
                if post_block is not None:
                    post_block(b, dict(ptr=ptr, ptr_evac=ptr_evac,
                                       n_down_evac=n_down_evac, dump=dump))
                S.maybe_bar(sems["bar"], 1)



def _p2_setup(nc, S, sems, consts, st, g_in, b_in, cc1a_in, cc1b_in,
              cc1a_out, cc1b_out):
    dS, pe, ac, dv, cc = (sems[n] for n in ("dS", "pe", "ac", "dv", "cc"))
    resid, stats, i128_sb = consts["resid"], consts["stats"], consts["i128"]

    def psb(name, shape, dt=F32):
        return st.enter_context(nc.sbuf_tensor(name, shape, dt))

    g_t = psb("p2_g", [128, D])
    b_t = psb("p2_b", [128, D])
    nbuf = psb("p2_n", [128, D])
    h2b = psb("p2_h2b", [128, D], BF16)
    h2f = [psb(f"p2_h2f{i}", [128, D], BF16) for i in range(2)]

    S.dma("sync", lambda e: e.dma_start(out=g_t[:, :], in_=g_in[:, :]), dS)
    S.dma("sync", lambda e: e.dma_start(out=b_t[:, :], in_=b_in[:, :]), dS)
    n_gb = dS.n

    sumc = stats[:, 96:112]
    sqc = stats[:, 112:128]
    rstd2 = stats[:, 64:80]
    b2 = stats[:, 80:96]
    state = dict(prev_tt=0, prev_ev={}, prev_dma={}, cc_done=[None, None])

    def emit_accs(b, ffnctx):
        dump = ffnctx["dump"]
        for k, t in enumerate(range(4 * b, 4 * b + 4)):
            w = [(dv, ffnctx["n_down_evac"])] if k == 0 else []
            S.op("scalar", lambda e, t=t: e.activation(
                dump[:, :], resid[:, t * D:(t + 1) * D], AF.Copy,
                accum_out=sumc[:, t:t + 1]), sem=ac, waits=w)
            S.op("scalar", lambda e, t=t: e.activation(
                dump[:, :], resid[:, t * D:(t + 1) * D], AF.Square,
                accum_out=sqc[:, t:t + 1]), sem=ac)
        state["acc_n"] = ac.n

    def emit_half(h, ffnctx):
        dump = ffnctx["dump"]
        t0 = 8 * h
        hs = slice(t0, t0 + 8)
        n_acc = state["acc_n"]
        S.op("vector", lambda e: e.tensor_single_scalar(
            sumc[:, hs], sumc[:, hs], 1.0 / D, ALU.mult),
            sem=dv, waits=[(ac, n_acc)])
        S.op("vector", lambda e: e.tensor_single_scalar(
            sqc[:, hs], sqc[:, hs], 1.0 / D, ALU.mult), sem=dv)
        S.op("vector", lambda e: e.tensor_mul(b2[:, hs], sumc[:, hs],
                                              sumc[:, hs]), sem=dv)
        S.op("vector", lambda e: e.tensor_sub(sqc[:, hs], sqc[:, hs],
                                              b2[:, hs]), sem=dv)
        n_var = dv.n
        S.op("scalar", lambda e: e.preload_act_table(AF.Sqrt), sem=ac)
        S.op("scalar", lambda e: e.activation(rstd2[:, hs], sqc[:, hs], AF.Sqrt,
                                              bias=consts["cols"][:, 8:9]),
             sem=ac, waits=[(dv, n_var)])
        n_std2 = ac.n
        if h == 0:
            S.op("scalar", lambda e: e.preload_act_table(AF.Silu), sem=ac)
        S.op("vector", lambda e: e.reciprocal(rstd2[:, hs], rstd2[:, hs]),
             sem=dv, waits=[(ac, n_std2)])
        n_rstd = dv.n

        ptr2 = ffnctx["ptr"]
        ptr_evac = ffnctx["ptr_evac"]
        for t in range(t0, t0 + 8):
            slot = t % 2
            w_n = [(dv, n_rstd), (dS, n_gb)]
            if state["prev_tt"]:
                w_n.append((dv, state["prev_tt"]))
            S.op("vector", lambda e, t=t: e.tensor_scalar(
                nbuf[:, :], resid[:, t * D:(t + 1) * D],
                sumc[:, t:t + 1], rstd2[:, t:t + 1], ALU.subtract, ALU.mult),
                sem=dv, waits=w_n)
            w_m = []
            if (t - 1) in state["prev_ev"]:
                w_m.append((pe, state["prev_ev"][(t - 1, "pe")]))
            S.op("vector", lambda e: e.tensor_mul(nbuf[:, :], nbuf[:, :],
                                                  g_t[:, :]),
                 sem=dv, waits=w_m)
            S.op("vector", lambda e: e.tensor_add(h2b[:, :], nbuf[:, :],
                                                  b_t[:, :]), sem=dv)
            state["prev_tt"] = dv.n
            n_h2b = dv.n
            for c in range(8):
                w2 = []
                if c == 0:
                    w2 = [(dv, n_h2b)]
                    if ptr_evac:
                        w2.append((ac, ptr_evac[-1]))
                S.op("tensor", lambda e, c=c: e.matmul(
                    ptr2[:, c * 128:(c + 1) * 128],
                    h2b[:, 128 * c:128 * c + 128], i128_sb[:, :],
                    start=True, stop=True),
                    sem=(pe if c == 7 else None), waits=w2)
            n_tr = pe.n
            wev = [(pe, n_tr)]
            if (t - 2) in state["prev_dma"]:
                wev.append(state["prev_dma"][t - 2])
            S.op("scalar", lambda e, slot=slot: e.activation(
                h2f[slot][:, 0:512], ptr2[:, 0:512], AF.Copy),
                sem=ac, waits=wev)
            S.op("scalar", lambda e, slot=slot: e.activation(
                h2f[slot][:, 512:1024], ptr2[:, 512:1024], AF.Copy), sem=ac)
            state["prev_ev"][t] = ac.n
            state["prev_ev"][(t, "pe")] = n_tr
            ptr_evac.append(ac.n)
            h2fs = sems["h2f0"] if slot == 0 else sems["h2f1"]
            cct = cc1a_in if h == 0 else cc1b_in
            S.dma("sync", lambda e, t=t, slot=slot, cct=cct: e.dma_start(
                out=cct.ap()[t % 8], in_=h2f[slot][:, :]),
                h2fs, waits=[(ac, state["prev_ev"][t])])
            state["prev_dma"][t] = (h2fs, h2fs.n)

        src_t, dst_t = ((cc1a_in, cc1a_out) if h == 0 else (cc1b_in, cc1b_out))
        S.op("gpsimd", lambda e, src_t=src_t, dst_t=dst_t: e.collective_compute(
            "AllGather", ALU.bypass, replica_groups=[list(range(NCORE))],
            ins=[src_t.ap().opt()], outs=[dst_t.ap().opt()]),
            sem=cc, waits=[(sems["h2f0"], sems["h2f0"].n),
                           (sems["h2f1"], sems["h2f1"].n)])
        state["cc_done"][h] = cc.n

    return emit_accs, emit_half, state



def _p3_qkv(nc, S, astack, sems, consts, wslices, ropes, cc1a_out, ccA_done,
            cc1b_out, ccB_done, qnf, knf, vres):
    dS, dG, pe, ac, dv, cc = (sems[n] for n in ("dS", "dG", "pe", "ac", "dv", "cc"))
    cols, i128_sb = consts["cols"], consts["i128"]
    ind_sq_sb, ind_bc_sb = consts["ind_sq"], consts["ind_bc"]
    wqm_in, wqr_in, wkm_in, wkr_in, wvm_in = wslices
    cosq_in, sinq_in, cosk_in, sink_in = ropes

    S.op("scalar", lambda e: e.add_instruction(mybir.InstLoadActFuncSet(
        name=nc.get_next_instruction_name(), act_func_set_id=6,
        ins=[], outs=[])), sem=sems["ac"])
    wq_ro = astack.enter_context(nc.sbuf_tensor("wq_ro", [128, D], BF16))
    wk_ro = astack.enter_context(nc.sbuf_tensor("wk_ro", [128, D], BF16))
    wv_bf = astack.enter_context(nc.sbuf_tensor("wv_bf", [128, D], BF16))

    def _rope_fold():
        with ExitStack() as ropest:
            wm = ropest.enter_context(nc.sbuf_tensor("p3_wm", [128, D], F32))
            wr = ropest.enter_context(nc.sbuf_tensor("p3_wr", [128, D], F32))
            cosb = ropest.enter_context(nc.sbuf_tensor("p3_cos", [128, 128], F32))
            sinb = ropest.enter_context(nc.sbuf_tensor("p3_sin", [128, 128], F32))
            tmp = ropest.enter_context(nc.sbuf_tensor("p3_tmp", [128, 128], F32))
            tmp2 = ropest.enter_context(nc.sbuf_tensor("p3_tmp2", [128, 128], F32))

            for (m_in, r_in, c_in, s_in, dst) in (
                    (wqm_in, wqr_in, cosq_in, sinq_in, wq_ro),
                    (wkm_in, wkr_in, cosk_in, sink_in, wk_ro)):
                n_prev_use = dv.n
                S.dma("gpsimd", lambda e, m=m_in: e.dma_start(
                    out=wm.ap().rearrange("p (k c) -> p k c", c=128),
                    in_=m.ap().rearrange("(k p) c -> p k c", p=128)),
                    dG, waits=[(dv, n_prev_use)])
                S.dma("gpsimd", lambda e, r=r_in: e.dma_start(
                    out=wr.ap().rearrange("p (k c) -> p k c", c=128),
                    in_=r.ap().rearrange("(k p) c -> p k c", p=128)), dG)
                S.dma("gpsimd", lambda e, c=c_in: e.dma_start(
                    out=cosb[:, :], in_=c[:, :]), dG)
                S.dma("gpsimd", lambda e, s=s_in: e.dma_start(
                    out=sinb[:, :], in_=s[:, :]), dG)
                n_l = dG.n
                for k in range(8):
                    S.op("vector", lambda e, k=k: e.tensor_mul(
                        tmp[:, :], wr[:, 128 * k:128 * k + 128], sinb[:, :]),
                        sem=dv, waits=[(dG, n_l)] if k == 0 else [])
                    S.op("vector", lambda e, k=k: e.tensor_mul(
                        tmp2[:, :], wm[:, 128 * k:128 * k + 128], cosb[:, :]),
                        sem=dv)
                    S.op("vector", lambda e, k=k, dst=dst: e.tensor_add(
                        dst[:, 128 * k:128 * k + 128], tmp2[:, :], tmp[:, :]),
                        sem=dv)
        return dv.n

    with ExitStack() as rst:
        n_ro = _rope_fold()
        if _STAGE == "dbg_wro":
            S.wait("gpsimd", dv, n_ro)
            S.dma("gpsimd", lambda e: e.dma_start(
                out=consts["out"].ap()[0:128, :], in_=wq_ro[:, :]), dG)
            S.dma("gpsimd", lambda e: e.dma_start(
                out=consts["out"].ap()[128:256, :], in_=wk_ro[:, :]), dG)
            S.wait("gpsimd", dG, dG.n)
            S.flush()
            return
        S.dma("gpsimd", lambda e: e.dma_start(
            out=wv_bf.ap().rearrange("p (k c) -> p k c", c=128),
            in_=wvm_in.ap().rearrange("(k p) c -> p k c", p=128)), dG)
        n_wv = dG.n

        h2g = [rst.enter_context(nc.sbuf_tensor(f"p3_h2g{i}", [128, 8 * 512], BF16))
               for i in range(2)]
        qraw = [rst.enter_context(nc.sbuf_tensor(f"p3_qraw{i}", [128, 1024], BF16))
                for i in range(3)]
        sqb = [rst.enter_context(nc.sbuf_tensor(f"p3_sqb{i}", [128, 1024], BF16))
               for i in range(2)]
        nrm = rst.enter_context(nc.sbuf_tensor("p3_nrm", [2, 512], F32))
        nrm2 = rst.enter_context(nc.sbuf_tensor("p3_nrm2", [2, 512], F32))
        rq_s = [rst.enter_context(nc.sbuf_tensor(f"p3_rq{i}", [2, 512], BF16))
                for i in range(2)]
        bcs = rst.enter_context(nc.sbuf_tensor("p3_bcs", [128, 512], BF16))

        with nc.psum_tensor("p3_pq", [128, 512], F32) as pq, \
             nc.psum_tensor("p3_pk", [128, 512], F32) as pk, \
             nc.psum_tensor("p3_pv", [128, 512], F32) as pv, \
             nc.psum_tensor("p3_pn", [2, 512], F32) as pn, \
             nc.psum_tensor("p3_pbc", [128, 512], F32) as pbc:

            S.op("vector", lambda e: e.memset(
                vres.ap().rearrange("p (t h one) -> p t h one", h=2, one=65)
                [:, :, :, 64:65], 1.0), sem=dv)

            h2g2 = [sems["h2g0"], sems["h2g1"]]
            load_done = {}
            qk_evac = {}
            v_evac = {}
            cst = dict(n_n=0, n_rq=0, n_pbc=0, n_bcs=0, last_tt=0, n_pnread=0)
            chain_pn = {}
            chain_done = {}
            rqc = {}
            pendA, pendB = [], []
            for g in range(32):
                slot = g % 2
                sl3 = g % 3
                w = ([(cc, ccA_done), (dv, n_ro)] if g < 2
                     else [(pe, load_done[(g - 2, "pe")])])
                if g in (16, 17):
                    w.append((cc, ccB_done))
                for i in range(4):
                    tg = 4 * g + i
                    srcT = cc1a_out if tg // 8 < 8 else cc1b_out
                    S.dma("gpsimd", lambda e, tg=tg, i=i, slot=slot, srcT=srcT:
                          e.dma_start(
                        out=h2g[slot][:, i * 1024:(i + 1) * 1024],
                        in_=srcT.ap()[tg % 8, (tg // 8) % 8]),
                        h2g2[slot], waits=(w if i == 0 else []))
                n_l = h2g2[slot].n

                for (dst_ps, wsb) in ((pq, wq_ro), (pk, wk_ro)):
                    for c in range(8):
                        w2 = []
                        if c == 0:
                            w2 = [(h2g2[slot], n_l), (dv, n_ro)]
                            if g >= 1:
                                w2.append((ac, qk_evac.get(g - 1, (0, 0))[0]))
                        S.op("tensor", lambda e, c=c, dst_ps=dst_ps, wsb=wsb, slot=slot: e.matmul(
                            dst_ps[:, :], wsb[:, 128 * c:128 * c + 128],
                            h2g[slot].ap().rearrange("p (i cq) -> p i cq", cq=1024)
                            [:, :, c * 128:(c + 1) * 128],
                            start=(c == 0), stop=(c == 7)),
                            sem=(pe if c == 7 else None), waits=w2)
                n_qk = pe.n
                if _STAGE == "dbg_h2g" and g == 0:
                    S.wait("gpsimd", h2g2[0], h2g2[0].n)
                    for f in range(4):
                        S.dma("gpsimd", lambda e, f=f: e.dma_start(
                            out=consts["out"].ap().rearrange("(f p) c -> f p c",
                                                             p=128)[f],
                            in_=h2g[0][:, f * D:(f + 1) * D]), sems["dG"])
                    S.op("scalar", lambda e: e.activation(
                        qraw[0][:, 0:512], pq[:, :], AF.Copy), sem=ac,
                        waits=[(pe, n_qk)])
                    S.wait("gpsimd", ac, ac.n)
                    S.dma("gpsimd", lambda e: e.dma_start(
                        out=consts["out"].ap().rearrange("(f p) c -> f p c", p=128)
                        [4, :, 0:512],
                        in_=qraw[0][:, 0:512]), sems["dG"])
                    S.wait("gpsimd", sems["dG"], sems["dG"].n)
                    return
                for i in range(4):
                    for c in range(8):
                        w2 = []
                        if i == 0 and c == 0:
                            w2 = [(dG, n_wv)]
                            if g >= 1:
                                w2.append((ac, v_evac.get(g - 1, 0)))
                        S.op("tensor", lambda e, i=i, c=c, slot=slot: e.matmul(
                            pv[:, i * 128:(i + 1) * 128],
                            h2g[slot][:, i * 1024 + c * 128: i * 1024 + c * 128 + 128],
                            wv_bf[:, 128 * c:128 * c + 128],
                            start=(c == 0), stop=(c == 7)),
                            sem=(pe if (c == 7 and i == 3) else None), waits=w2)
                n_v = pe.n
                load_done[(g, "pe")] = n_v

                S.op("scalar", lambda e, g=g: e.activation(
                    vres.ap().rearrange("p (t h one) -> p t h one", h=2, one=65)
                    [:, 4 * g:4 * g + 4, :, 0:64],
                    pv.ap().rearrange("p (i h d) -> p i h d", i=4, h=2),
                    AF.Copy), sem=ac, waits=[(pe, n_v)])
                v_evac[g] = ac.n

                w_sq = [(pe, n_qk)]
                if (g - 2) in chain_pn:
                    w_sq.append((pe, chain_pn[g - 2]))
                if (g - 3) in chain_done:
                    w_sq.append((dv, chain_done[g - 3]))
                for (side, ps_in) in ((0, pq), (1, pk)):
                    S.op("scalar", lambda e, ps_in=ps_in, side=side, slot=slot:
                         e.activation(sqb[slot][:, side * 512:side * 512 + 512],
                                      ps_in[:, :], AF.Square),
                         sem=ac, waits=(w_sq if side == 0 else []))
                    S.op("scalar", lambda e, ps_in=ps_in, side=side, sl3=sl3:
                         e.activation(qraw[sl3][:, side * 512:side * 512 + 512],
                                      ps_in[:, :], AF.Copy), sem=ac)
                n_sq_done = ac.n
                qk_evac[g] = (n_sq_done, 0)

                def stageA(g=g, slot=slot, sl3=sl3, n_sq_done=n_sq_done):
                    for (side, is_q) in ((0, True), (1, False)):
                        off = side * 512
                        w_pn = [(ac, n_sq_done)]
                        if cst["n_pnread"]:
                            w_pn.append((dv, cst["n_pnread"]))
                        S.op("tensor", lambda e, slot=slot, off=off: e.matmul(
                            pn[:, :], ind_sq_sb[:, :], sqb[slot][:, off:off + 512],
                            start=True, stop=True), sem=pe, waits=w_pn)
                        n_pn = pe.n
                        if side == 1:
                            chain_pn[g] = n_pn
                        w_n = [(pe, n_pn)]
                        if cst["n_n"]:
                            w_n.append((ac, cst["n_n"]))
                        if cst["n_rq"]:
                            w_n.append((dv, cst["n_rq"]))
                        S.op("vector", lambda e: e.tensor_scalar_max(
                            nrm[:, :], pn[:, :], 1e-24), sem=dv, waits=w_n)
                        cst["n_pnread"] = dv.n
                        S.op("scalar", lambda e: e.activation(
                            nrm2[:, :], nrm[:, :], AF.Ln), sem=ac,
                            waits=[(dv, cst["n_pnread"])])
                        w_rq = [(pe, cst["n_pbc"])] if cst["n_pbc"] else []
                        if is_q:
                            S.op("scalar", lambda e: e.activation(
                                nrm[:, :], nrm2[:, :], AF.Exp, scale=-0.5),
                                sem=ac)
                            cst["n_n"] = ac.n
                            S.op("vector", lambda e, side=side: e.tensor_single_scalar(
                                rq_s[side][:, :], nrm[:, :], cols[0:2, 6:7],
                                ALU.mult), sem=dv,
                                waits=w_rq + [(ac, cst["n_n"])])
                        else:
                            S.op("scalar", lambda e, side=side: e.activation(
                                rq_s[side][:, :], nrm2[:, :], AF.Exp, scale=-0.5),
                                sem=ac, waits=w_rq)
                            cst["n_n"] = ac.n
                            S.op("vector", lambda e: e.engine_nop(), sem=dv,
                                 waits=[(ac, cst["n_n"])])
                        rqc[(g, side)] = dv.n
                        cst["n_rq"] = dv.n

                    def stageB(g=g, sl3=sl3):
                        for (side, dst) in ((0, qnf), (1, knf)):
                            off = side * 512
                            w_pbc = [(dv, rqc[(g, side)])]
                            if cst["n_bcs"]:
                                w_pbc.append((ac, cst["n_bcs"]))
                            S.op("tensor", lambda e, side=side: e.matmul(
                                pbc[:, :], ind_bc_sb[:, :], rq_s[side][:, :],
                                start=True, stop=True), sem=pe, waits=w_pbc)
                            n_pbc = pe.n
                            cst["n_pbc"] = n_pbc
                            w_bcs = [(pe, n_pbc)]
                            if cst["last_tt"]:
                                w_bcs.append((dv, cst["last_tt"]))
                            S.op("scalar", lambda e: e.activation(
                                bcs[:, :], pbc[:, :], AF.Copy), sem=ac, waits=w_bcs)
                            cst["n_bcs"] = ac.n
                            S.op("vector", lambda e, dst=dst, g=g, sl3=sl3,
                                 off=off: e.tensor_mul(
                                dst[:, 512 * g:512 * g + 512],
                                qraw[sl3][:, off:off + 512], bcs[:, :]),
                                sem=dv, waits=[(ac, cst["n_bcs"])])
                            cst["last_tt"] = dv.n
                        chain_done[g] = cst["last_tt"]
                    return stageB

                if pendB:
                    pendB.pop(0)()
                if pendA:
                    pendB.append(pendA.pop()())
                pendA.append(stageA)
            if pendB:
                pendB.pop(0)()
            if pendA:
                pendB.append(pendA.pop()())
            while pendB:
                pendB.pop(0)()



def _p4_attn(nc, S, astack, sems, consts, qnf, knf, vres, cc2h):
    dS, pe, ac, dv = (sems[n] for n in ("dS", "pe", "ac", "dv"))
    cc = sems["cc"]
    cols, triu_sb, ones64_sb = consts["cols"], consts["triu"], consts["ones64"]

    pa_sb = [astack.enter_context(nc.sbuf_tensor(f"p4_pa{i}", [128, 1024], BF16))
             for i in range(2)]
    pb_sb = [astack.enter_context(nc.sbuf_tensor(f"p4_pb{i}", [128, 1024], BF16))
             for i in range(2)]
    rs_sb = astack.enter_context(nc.sbuf_tensor("p4_rs", [1, 1024], F32))
    bca_sb = astack.enter_context(nc.sbuf_tensor("p4_bca", [64, 512], F32))
    bcb_sb = astack.enter_context(nc.sbuf_tensor("p4_bcb", [64, 512], F32))
    of_sb = [astack.enter_context(nc.sbuf_tensor(f"p4_of{i}", [128, 512], BF16))
             for i in range(2)]
    orawA = [astack.enter_context(nc.sbuf_tensor(f"p4_oa{i}", [65, 512], F32))
             for i in range(2)]
    orawB = [astack.enter_context(nc.sbuf_tensor(f"p4_ob{i}", [65, 512], F32))
             for i in range(2)]

    with nc.psum_tensor("p4_psA", [128, 1024], F32) as psA, \
         nc.psum_tensor("p4_psB", [128, 1024], F32) as psB, \
         nc.psum_tensor("p4_oA", [65, 512], F32) as oA, \
         nc.psum_tensor("p4_oB", [65, 512], F32) as oB, \
         nc.psum_tensor("p4_nA", [64, 512], F32) as nA, \
         nc.psum_tensor("p4_nB", [64, 512], F32) as nB:

        of_dma = []
        onorm = []
        pending_norm = []
        p4lim = int(os.environ.get("P4LIM", "999"))
        ngroups = 0
        for s in range(NSEQ):
            for qg in range(8):
                if ngroups >= p4lim:
                    continue
                ngroups += 1
                gi = s * 8 + qg
                if gi == 17:
                    S.op("gpsimd", lambda e: e.collective_compute(
                        "AllGather", ALU.bypass,
                        replica_groups=[list(range(NCORE))],
                        ins=[cc2h["a_in"].ap().opt()],
                        outs=[cc2h["a_out"].ap().opt()]),
                        sem=cc, waits=[(sems["of0"], sems["of0"].n),
                                       (sems["of1"], sems["of1"].n)])
                    cc2h["ccA_done"] = cc.n
                qc0 = s * SEQ + qg * 512
                nkt = 4 * qg + 4
                npair = nkt // 2
                exp_done = {}
                exp_done_h = {}
                diag_done = {}
                av_done = {}
                prev = onorm[-1] if onorm else None
                for pair in range(npair):
                    for kk in (2 * pair, 2 * pair + 1):
                        half = (kk % 2) * 512
                        gk = 32 * s + kk
                        w = []
                        if pair >= 1:
                            w.append((ac, exp_done_h[(pair - 1, kk % 2)]))
                        elif prev:
                            w.append((ac, prev["exp_last"]))
                        S.op("tensor", lambda e, gk=gk, half=half, qc0=qc0: e.matmul(
                            psA[:, half:half + 512],
                            knf[0:64, 128 * gk:128 * gk + 128],
                            qnf[0:64, qc0:qc0 + 512],
                            start=True, stop=True, tile_position=(0, 0)),
                            sem=pe, waits=w)
                        S.op("tensor", lambda e, gk=gk, half=half, qc0=qc0: e.matmul(
                            psB[:, half:half + 512],
                            knf[64:128, 128 * gk:128 * gk + 128],
                            qnf[64:128, qc0:qc0 + 512],
                            start=True, stop=True, tile_position=(64, 0)),
                            sem=pe)
                        n_sc = pe.n
                        w_exp = [(pe, n_sc)]
                        if pair >= 2:
                            w_exp.append((pe, av_done[pair - 2]))
                        elif prev:
                            w_exp.append((pe, prev["n_av"]))
                        S.op("scalar", lambda e, pair=pair, half=half: e.activation(
                            pa_sb[pair % 2][:, half:half + 512],
                            psA[:, half:half + 512], AF.Exp,
                            bias=cols[:, 7:8]), sem=ac, waits=w_exp)
                        S.op("scalar", lambda e, pair=pair, half=half: e.activation(
                            pb_sb[pair % 2][:, half:half + 512],
                            psB[:, half:half + 512], AF.Exp,
                            bias=cols[:, 7:8]), sem=ac)
                        exp_done_h[(pair, kk % 2)] = ac.n
                    exp_done[pair] = ac.n
                    for kk in (2 * pair, 2 * pair + 1):
                        if kk >= 4 * qg:
                            i0 = kk - 4 * qg
                            off = (kk % 2) * 512 + i0 * 128
                            for psb in (pa_sb, pb_sb):
                                S.op("vector", lambda e, psb=psb, pair=pair, off=off: e.tensor_mul(
                                    psb[pair % 2][:, off:off + 128],
                                    psb[pair % 2][:, off:off + 128],
                                    triu_sb[:, :]),
                                    sem=dv, waits=[(ac, exp_done[pair])])
                            diag_done[kk] = dv.n
                    for kk in (2 * pair, 2 * pair + 1):
                        gk2 = 32 * s + kk
                        i0 = max(0, kk - 4 * qg)
                        off = (kk % 2) * 512 + i0 * 128
                        nq = 512 - i0 * 128
                        wav = [(ac, exp_done_h[(pair, kk % 2)])]
                        if kk in diag_done:
                            wav.append((dv, diag_done[kk]))
                        if kk == 0 and prev:
                            wav.append((dv, prev["n_evac"]))
                        S.op("tensor", lambda e, gk2=gk2, off=off, nq=nq, i0=i0, kk=kk, pair=pair: e.matmul(
                            oA[:, i0 * 128: i0 * 128 + nq],
                            vres[:, 130 * gk2:130 * gk2 + 65],
                            pa_sb[pair % 2][:, off:off + nq],
                            start=(kk == 0), stop=(kk == nkt - 1)),
                            sem=pe, waits=wav)
                        S.op("tensor", lambda e, gk2=gk2, off=off, nq=nq, i0=i0, kk=kk, pair=pair: e.matmul(
                            oB[:, i0 * 128: i0 * 128 + nq],
                            vres[:, 130 * gk2 + 65:130 * gk2 + 130],
                            pb_sb[pair % 2][:, off:off + nq],
                            start=(kk == 0), stop=(kk == nkt - 1)),
                            sem=pe)
                    av_done[pair] = pe.n
                    if pair == 1 and pending_norm:
                        pending_norm.pop()()
                    S.maybe_bar(sems["bar"], 2)
                    if _STAGE == "dbg_p0" and s == 0 and qg == 0 and pair == 0:
                        S.wait("gpsimd", dv, dv.n)
                        S.wait("gpsimd", ac, ac.n)
                        S.dma("gpsimd", lambda e: e.dma_start(
                            out=consts["out"].ap()[0:128, :],
                            in_=pa_sb[0][:, :]), sems["dG"])
                        S.dma("gpsimd", lambda e: e.dma_start(
                            out=consts["out"].ap()[128:256, :],
                            in_=pb_sb[0][:, :]), sems["dG"])
                        S.wait("gpsimd", sems["dG"], sems["dG"].n)
                        S.flush()
                        raise StopIteration
                n_av = pe.n

                slot = gi % 2
                w_ev = [(pe, n_av)]
                if len(onorm) >= 2:
                    w_ev.append((dv, onorm[-2]["n_of"]))
                S.op("vector", lambda e, slot=slot: e.tensor_copy(
                    orawA[slot][:, :], oA[:, :]), sem=dv, waits=w_ev)
                S.op("vector", lambda e, slot=slot: e.tensor_copy(
                    orawB[slot][:, :], oB[:, :]), sem=dv)
                n_evac = dv.n
                entry = dict(n_evac=n_evac, n_av=n_av,
                             exp_last=exp_done[npair - 1])
                onorm.append(entry)

                def emit_norm(entry=entry, slot=slot, pidx=len(onorm) - 2,
                              s=s, qg=qg):
                    pv = onorm[pidx] if pidx >= 0 else None
                    w_rs = []
                    if pv:
                        w_rs.append((pe, pv["n_bc"]))
                    S.op("vector", lambda e, slot=slot: e.reciprocal(
                        rs_sb[0:1, 0:512], orawA[slot][64:65, :]), sem=dv,
                        waits=w_rs)
                    S.op("vector", lambda e, slot=slot: e.reciprocal(
                        rs_sb[0:1, 512:1024], orawB[slot][64:65, :]), sem=dv)
                    n_rs = dv.n
                    w_bc = [(dv, n_rs)]
                    if pv:
                        w_bc.append((ac, pv["n_bcs"]))
                    S.op("tensor", lambda e: e.matmul(
                        nA[:, :], ones64_sb[:, :], rs_sb[0:1, 0:512], start=True,
                        stop=True), sem=pe, waits=w_bc)
                    S.op("tensor", lambda e: e.matmul(
                        nB[:, :], ones64_sb[:, :], rs_sb[0:1, 512:1024], start=True,
                        stop=True), sem=pe)
                    n_bc = pe.n
                    w_bcs = [(pe, n_bc)]
                    if pv:
                        w_bcs.append((dv, pv["n_of"]))
                    S.op("scalar", lambda e: e.activation(bca_sb[:, :], nA[:, :],
                                                          AF.Copy),
                         sem=ac, waits=w_bcs)
                    S.op("scalar", lambda e: e.activation(bcb_sb[:, :], nB[:, :],
                                                          AF.Copy), sem=ac)
                    n_bcs = ac.n
                    wof = [(ac, n_bcs)]
                    if len(of_dma) >= 2:
                        wof.append(of_dma[-2])
                    S.op("vector", lambda e, slot=slot: e.tensor_mul(
                        of_sb[slot][0:64, :], orawA[slot][0:64, :], bca_sb[:, :]),
                        sem=dv, waits=wof)
                    S.op("vector", lambda e, slot=slot: e.tensor_mul(
                        of_sb[slot][64:128, :], orawB[slot][0:64, :], bcb_sb[:, :]),
                        sem=dv)
                    n_of = dv.n
                    entry["n_of"] = n_of
                    entry["n_bcs"] = n_bcs
                    entry["n_bc"] = n_bc

                    tg0 = 32 * s + 4 * qg
                    recv0 = tg0 % 8
                    slot_idx = tg0 // 8
                    cct = cc2h["a_in"] if slot_idx < 8 else cc2h["b_in"]
                    si = slot_idx % 8
                    ofs = sems["of0"] if slot == 0 else sems["of1"]
                    S.dma("sync", lambda e, recv0=recv0, si=si, cct=cct,
                          slot=slot: e.dma_start(
                        out=cct.ap()[recv0:recv0 + 4, :,
                                     si * 128:(si + 1) * 128]
                        .rearrange("i p q -> p i q"),
                        in_=of_sb[slot].ap().rearrange("p (i q) -> p i q", q=128)),
                        ofs, waits=[(dv, n_of)])
                    of_dma.append((ofs, ofs.n))

                pending_norm.append(emit_norm)
                S.maybe_bar(sems["bar"], 1)
        if pending_norm:
            pending_norm.pop()()



def _p5_oproj(nc, S, sems, consts, cc2a_out, ccA_done, cc2b_out, ccB_done,
              wout_in):
    dS, dG, pe, ac, dv, cc = (sems[n] for n in ("dS", "dG", "pe", "ac", "dv", "cc"))
    resid, rank_sb = consts["resid"], consts["rank"]

    with ExitStack() as pst:
        ofall = pst.enter_context(nc.sbuf_tensor("p5_ofall", [128, NCORE * 2048], BF16))
        wout_bf = pst.enter_context(nc.sbuf_tensor("p5_wout", [128, 8 * D], BF16))

        S.dma("gpsimd", lambda e: e.dma_start(
            out=wout_bf.ap().rearrange("p (k n) -> p k n", n=D),
            in_=wout_in.ap().rearrange("(k p) n -> p k n", p=128)), dG)
        n_w = dG.n

        rk_holder = {}

        def load_of(e, j, src, hoff):
            if "rk" not in rk_holder:
                rk_holder["rk"] = nc.values_load(
                    rank_sb[:, :], engines=[SP], min_val=0, max_val=7,
                    skip_runtime_bounds_check=True)
            rk = rk_holder["rk"]
            return e.dma_start(
                out=ofall.ap()[:, j * 2048 + hoff:j * 2048 + hoff + 1024],
                in_=src.ap()[j, DynSlice(rk, 1)].rearrange("one p q -> p (one q)"))

        n_ofh = {}
        ev = []

        def emit_half(hi, src, ccdone, py):
            for j in range(NCORE):
                S.dma("sync", lambda e, j=j, src=src: load_of(
                    e, j, src, hi * 1024), dS,
                    waits=([(cc, ccdone)] if j == 0 else []))
            n_ofh[hi] = dS.n
            for t in range(8 * hi, 8 * hi + 8):
                ps = py[t % 2]
                w = [(dS, n_ofh[hi]), (dG, n_w)]
                if len(ev) >= 2:
                    w.append((dv, ev[-2]))
                for j in range(NCORE):
                    for half in range(2):
                        S.op("tensor", lambda e, t=t, j=j, half=half, ps=ps: e.matmul(
                            ps[:, half * 512:(half + 1) * 512],
                            ofall[:, j * 2048 + t * 128: j * 2048 + t * 128 + 128],
                            wout_bf[:, j * D + half * 512: j * D + half * 512 + 512],
                            start=(j == 0), stop=(j == NCORE - 1)),
                            sem=(pe if (j == NCORE - 1 and half == 1) else None),
                            waits=(w if (j == 0 and half == 0) else []))
                n_mm = pe.n
                for half in range(2):
                    S.op("vector", lambda e, t=t, ps=ps, half=half: e.tensor_add(
                        resid[:, t * D + 512 * half:t * D + 512 * half + 512],
                        resid[:, t * D + 512 * half:t * D + 512 * half + 512],
                        ps[:, 512 * half:512 * half + 512]),
                        sem=dv, waits=[(pe, n_mm)])
                ev.append(dv.n)

        with nc.psum_tensor("p5_py0", [128, 1024], F32) as py0, \
             nc.psum_tensor("p5_py1", [128, 1024], F32) as py1:
            py = [py0, py1]
            emit_half(0, cc2a_out, ccA_done, py)
            emit_half(1, cc2b_out, ccB_done, py)


def _store_out(nc, S, sems, resid, out):
    dS = sems["dS"]
    S.dma("sync", lambda e: e.dma_start(
        out=out.ap().rearrange("(t p) c -> p t c", p=128),
        in_=resid.ap().rearrange("p (t c) -> p t c", c=D)), dS)
    n_done = dS.n
    S.wait("gpsimd", dS, n_done)



def kernel(L0_attn_x, L0_attn_ln_gamma, L0_attn_ln_beta, L0_attn_w_qkv,
           L0_attn_w_out, L0_attn_qkscale, L0_ffn_w_gate, L0_ffn_w_up,
           L0_ffn_w_down):
    x = np.ascontiguousarray(np.asarray(L0_attn_x, dtype=np.float32)).reshape(128, 128, D)
    gamma = np.asarray(L0_attn_ln_gamma, dtype=np.float32)
    beta = np.asarray(L0_attn_ln_beta, dtype=np.float32)
    wqkv = np.asarray(L0_attn_w_qkv, dtype=np.float32)
    w_out = np.ascontiguousarray(np.asarray(L0_attn_w_out, dtype=np.float32))
    qks = np.asarray(L0_attn_qkscale, dtype=np.float32).reshape(1, 1)
    wgT = np.ascontiguousarray(np.asarray(L0_ffn_w_gate, dtype=np.float32).T)
    wuT = np.ascontiguousarray(np.asarray(L0_ffn_w_up, dtype=np.float32).T)
    wdT = np.ascontiguousarray(np.asarray(L0_ffn_w_down, dtype=np.float32).T)

    cos, sin_eff, partner = _rope_consts()
    g_t = np.ascontiguousarray(np.broadcast_to(gamma, (128, D)))
    b_t = np.ascontiguousarray(np.broadcast_to(beta, (128, D)))

    in_maps = []
    for c in range(NCORE):
        qcols = np.arange(128 * c, 128 * c + 128)
        in_maps.append(dict(
            x=np.ascontiguousarray(x[c::8].reshape(2048, D)),
            g_t=g_t, b_t=b_t,
            wq_m=np.ascontiguousarray(wqkv[:, qcols]),
            wq_r=np.ascontiguousarray(wqkv[:, partner[qcols]]),
            wk_m=np.ascontiguousarray(wqkv[:, D + qcols]),
            wk_r=np.ascontiguousarray(wqkv[:, D + partner[qcols]]),
            wv_m=np.ascontiguousarray(wqkv[:, 2 * D + qcols]),
            cos_q=np.ascontiguousarray(np.broadcast_to(cos[qcols], (128, 128))),
            sin_q=np.ascontiguousarray(np.broadcast_to(sin_eff[qcols], (128, 128))),
            cos_k=np.ascontiguousarray(np.broadcast_to(cos[qcols], (128, 128))),
            sin_k=np.ascontiguousarray(np.broadcast_to(sin_eff[qcols], (128, 128))),
            w_out=w_out, wgT=wgT, wuT=wuT, wdT=wdT, qks=qks,
            rank=np.array([[c]], dtype=np.int32),
        ))

    nc = build_program()
    res = run_bass_kernel_spmd(nc, in_maps, core_ids=list(range(NCORE)))
    y = np.empty((128, 128, D), dtype=np.float32)
    for c in range(NCORE):
        y[c::8] = np.asarray(res.results[c]["out"]).reshape(NT, 128, D)
    return y.reshape(NSEQ, SEQ, D)


# version 12
# speedup vs baseline: 1.0301x; 1.0146x over previous
import os
from contextlib import ExitStack

import numpy as np
import ml_dtypes

import concourse.bass as bass
import concourse.mybir as mybir
from concourse.bass_types import DynSlice
from concourse.bass_utils import run_bass_kernel_spmd

F32 = mybir.dt.float32
BF16 = mybir.dt.bfloat16
I8 = mybir.dt.int8
I32 = mybir.dt.int32
AF = mybir.ActivationFunctionType
ALU = mybir.AluOpType
AX = mybir.AxisListType.X
SP = mybir.EngineType.SP

BF = ml_dtypes.bfloat16

_STAGE = "full"
_PARANOID = 0

D = 1024
INNER = 4096
SEQ = 4096
NSEQ = 4
NCORE = 8
NT = 16
NB = 4
TPB = 4
LN_EPS = 1e-5

ENGS = ("sync", "gpsimd", "tensor", "vector", "scalar")


def _rope_consts():
    inv = np.float32(10000.0) ** (-(np.arange(0, D, 2, dtype=np.float32) / np.float32(D)))
    emb = np.repeat(inv, 2)
    ang = np.float32(SEQ) * emb
    sin = np.sin(ang).astype(np.float32)
    cos = np.cos(ang).astype(np.float32)
    idx = np.arange(D)
    partner = np.where(idx < D // 2, 2 * idx + 1, 2 * idx - D)
    sin_eff = np.where(idx < D // 2, -sin, sin).astype(np.float32)
    return cos, sin_eff, partner


class Sem:
    def __init__(self, nc, name):
        self.h = nc.alloc_semaphore(name)
        self.n = 0


class Sched:
    def __init__(self, nc, block, sems=None):
        self.nc = nc
        self.block = block
        self.q = {e: [] for e in ENGS}
        self.sems = sems

    def op(self, eng, fn, sem=None, k=1, waits=(), chain=True):
        waits = list(waits)
        if chain and self.sems is not None and eng in ("vector", "scalar"):
            own = self.sems["dv" if eng == "vector" else "ac"]
            if own.n > 0 and (sem is not own or True):
                waits.append((own, own.n))
            if sem is None:
                sem = own
        waits = tuple((s.h, int(c)) for (s, c) in waits if c)
        if sem is not None:
            sem.n += k
        h = sem.h if sem is not None else None

        def run(e, fn=fn, waits=waits, h=h, kk=k):
            for (sh, c) in waits:
                e.wait_ge(sh, c)
            inst = fn(e)
            if h is not None:
                inst.then_inc(h, kk)

        self.q[eng].append(run)
        return sem.n if sem is not None else None

    def wait(self, eng, sem, n):
        self.q[eng].append(lambda e, sh=sem.h, n=int(n): e.wait_ge(sh, n))

    def dma(self, eng, fn, sem, waits=()):
        return self.op(eng, fn, sem=sem, k=16, waits=waits)

    def flush(self):
        for eng in ENGS:
            lst = self.q[eng]
            if not lst:
                continue

            def body(e, lst=lst):
                for f in lst:
                    f(e)

            getattr(self.block, eng)(body)
        self.q = {e: [] for e in ENGS}

    def barrier(self, bar):
        self.op("sync", lambda e: e.nop(), sem=bar, k=1)
        self.op("gpsimd", lambda e: e.engine_nop(), sem=bar, k=1)
        self.op("vector", lambda e: e.engine_nop(), sem=bar, k=1)
        self.op("scalar", lambda e: e.activation(
            self.scratch11[:, :], self.scratch11[:, :], AF.Identity), sem=bar, k=1)
        self.op("tensor", lambda e: e.nop(), sem=bar, k=1)
        t = bar.n
        for eng in ENGS:
            self.wait(eng, bar, t)

    def maybe_bar(self, bar, level):
        if _PARANOID >= level:
            self.barrier(bar)



def build_program():
    nc = bass.Bass("TRN2", target_bir_lowering=False, debug=False)

    x_in = nc.dram_tensor("x", [2048, D], F32, kind="ExternalInput")
    g_in = nc.dram_tensor("g_t", [128, D], F32, kind="ExternalInput")
    b_in = nc.dram_tensor("b_t", [128, D], F32, kind="ExternalInput")
    wqm_in = nc.dram_tensor("wq_m", [D, 128], F32, kind="ExternalInput")
    wqr_in = nc.dram_tensor("wq_r", [D, 128], F32, kind="ExternalInput")
    wkm_in = nc.dram_tensor("wk_m", [D, 128], F32, kind="ExternalInput")
    wkr_in = nc.dram_tensor("wk_r", [D, 128], F32, kind="ExternalInput")
    wvm_in = nc.dram_tensor("wv_m", [D, 128], F32, kind="ExternalInput")
    cosq_in = nc.dram_tensor("cos_q", [128, 128], F32, kind="ExternalInput")
    sinq_in = nc.dram_tensor("sin_q", [128, 128], F32, kind="ExternalInput")
    cosk_in = nc.dram_tensor("cos_k", [128, 128], F32, kind="ExternalInput")
    sink_in = nc.dram_tensor("sin_k", [128, 128], F32, kind="ExternalInput")
    wout_in = nc.dram_tensor("w_out", [D, D], F32, kind="ExternalInput")
    wgT_in = nc.dram_tensor("wgT", [D, INNER], F32, kind="ExternalInput")
    wuT_in = nc.dram_tensor("wuT", [D, INNER], F32, kind="ExternalInput")
    wdT_in = nc.dram_tensor("wdT", [INNER, D], F32, kind="ExternalInput")
    qks_in = nc.dram_tensor("qks", [1, 1], F32, kind="ExternalInput")
    rank_in = nc.dram_tensor("rank", [1, 1], I32, kind="ExternalInput")
    out = nc.dram_tensor("out", [2048, D], F32, kind="ExternalOutput")

    wgq_d = nc.dram_tensor("wgq_d", [8, 128, 8, 512], BF16)
    wuq_d = nc.dram_tensor("wuq_d", [8, 128, 8, 512], BF16)
    wdq_d = nc.dram_tensor("wdq_d", [INNER, D], BF16)
    cc1a_in = nc.dram_tensor("cc1a_in", [8, 128, 1024], BF16)
    cc1b_in = nc.dram_tensor("cc1b_in", [8, 128, 1024], BF16)
    cc1a_out = nc.dram_tensor("cc1a_out", [NCORE, 8, 128, 1024], BF16)
    cc1b_out = nc.dram_tensor("cc1b_out", [NCORE, 8, 128, 1024], BF16)
    cc2a_in = nc.dram_tensor("cc2a_in", [NCORE, 128, 8 * 128], BF16)
    cc2b_in = nc.dram_tensor("cc2b_in", [NCORE, 128, 8 * 128], BF16)
    cc2a_out = nc.dram_tensor("cc2a_out", [NCORE, NCORE, 128, 8 * 128], BF16)
    cc2b_out = nc.dram_tensor("cc2b_out", [NCORE, NCORE, 128, 8 * 128], BF16)

    I128 = nc.inline_tensor(np.eye(128, dtype=BF), name="I128")
    triu_np = (np.arange(128)[:, None] <= np.arange(128)[None, :]).astype(BF)
    TRIU = nc.inline_tensor(triu_np, name="TRIU")
    ind_sq_np = np.zeros((128, 2), dtype=BF)
    ind_sq_np[0:64, 0] = 1
    ind_sq_np[64:128, 1] = 1
    IND_SQ = nc.inline_tensor(ind_sq_np, name="IND_SQ")
    ind_bc_np = np.zeros((2, 128), dtype=BF)
    ind_bc_np[0, 0:64] = 1
    ind_bc_np[1, 64:128] = 1
    IND_BC = nc.inline_tensor(ind_bc_np, name="IND_BC")
    ONES_1x64 = nc.inline_tensor(np.ones((1, 64), dtype=np.float32), name="ONES_1x64")
    ONES_R_F = nc.inline_tensor(np.ones((1, 128), dtype=np.float32), name="ONES_R_F")
    ONES_C_F = nc.inline_tensor(np.ones((128, 1), dtype=np.float32), name="ONES_C_F")

    semnames = ("dS", "dG", "pe", "ac", "dv", "cc", "bar",
                "wq0", "wq1", "wq2", "wq3", "ts0", "ts1",
                "wg0", "wg1", "wu0", "wu1",
                "wd0", "wd1", "h2f0", "h2f1", "h2g0", "h2g1", "of0", "of1")
    sems = {n: Sem(nc, n) for n in semnames}
    dS, dG, pe, ac, dv, cc, bar = (sems[n] for n in
                                   ("dS", "dG", "pe", "ac", "dv", "cc", "bar"))

    with ExitStack() as stack:
        block = stack.enter_context(nc.Block())
        S = Sched(nc, block, sems=sems)
        S.scratch11 = stack.enter_context(nc.sbuf_tensor("bar_scr", [1, 1], F32))

        def sb(name, shape, dt=F32):
            return stack.enter_context(nc.sbuf_tensor(name, shape, dt))

        resid = sb("resid", [128, NT * D])
        stats = sb("stats", [128, 160])
        cols = sb("cols", [128, 16])
        sc11 = sb("sc11", [1, 8])
        i128_sb = sb("i128_sb", [128, 128], BF16)
        triu_sb = sb("triu_sb", [128, 128], BF16)
        ind_sq_sb = sb("ind_sq_sb", [128, 2], BF16)
        ind_bc_sb = sb("ind_bc_sb", [2, 128], BF16)
        ones64_sb = sb("ones64_sb", [1, 64], F32)
        onesrf_sb = sb("onesrf_sb", [1, 128], F32)
        onescf_sb = sb("onescf_sb", [128, 1], F32)
        rank_sb = sb("rank_sb", [1, 1], I32)
        qks_sb = sb("qks_sb", [1, 1], F32)

        consts = dict(resid=resid, stats=stats, cols=cols, sc11=sc11,
                      i128=i128_sb, triu=triu_sb, ind_sq=ind_sq_sb,
                      ind_bc=ind_bc_sb, ones64=ones64_sb, onesrf=onesrf_sb,
                      onescf=onescf_sb, rank=rank_sb, qks=qks_sb, out=out)

        S.op("vector", lambda e: e.memset(cols[:, 8:9], LN_EPS), sem=sems["dv"])
        for t in range(NT):
            S.dma("sync", lambda e, t=t: e.dma_start(
                out=resid[:, t * D:(t + 1) * D],
                in_=x_in.ap().rearrange("(t p) c -> t p c", p=128)[t]), dS)
        for dst, src in ((i128_sb, I128), (triu_sb, TRIU), (ind_sq_sb, IND_SQ),
                         (ind_bc_sb, IND_BC), (ones64_sb, ONES_1x64),
                         (onesrf_sb, ONES_R_F), (onescf_sb, ONES_C_F),
                         (rank_sb, rank_in), (qks_sb, qks_in)):
            S.dma("sync", lambda e, d=dst, s=src: e.dma_start(out=d[:, :], in_=s[:, :]), dS)
        n_x = dS.n
        n_consts = dS.n

        with nc.psum_tensor("ps_misc", [128, 8], F32) as ps_misc:
            def serial0(eng, fn, sem, extra=()):
                S.op(eng, fn, sem=sem,
                     waits=list(extra) + [(pe, pe.n), (ac, ac.n), (dv, dv.n)])

            serial0("scalar", lambda e: e.activation(sc11[:, 0:1], qks_sb[:, :],
                                                     AF.Abs), ac,
                    extra=[(dS, n_consts)])
            serial0("tensor", lambda e: e.matmul(ps_misc[:, 0:1], onesrf_sb[:, :],
                                                 qks_sb[:, :], start=True, stop=True),
                    pe, extra=[(dS, n_consts)])
            serial0("tensor", lambda e: e.matmul(ps_misc[:, 1:2], onesrf_sb[:, :],
                                                 sc11[:, 0:1], start=True, stop=True),
                    pe)
            serial0("scalar", lambda e: e.activation(cols[:, 6:7], ps_misc[:, 0:1],
                                                     AF.Copy), ac)
            serial0("scalar", lambda e: e.activation(cols[:, 7:8], ps_misc[:, 1:2],
                                                     AF.Copy, scale=-1.0), ac)

            mats = ((wgT_in, wgq_d, 8, INNER, 0, 3, 1.0 / 127.0),
                    (wuT_in, wuq_d, 8, INNER, 1, 4, 1.0 / 127.0),
                    (wdT_in, wdq_d, 32, D, 2, 5, 0.5 / 127.0))
            with ExitStack() as wst:
                s0dump = wst.enter_context(
                    nc.sbuf_tensor("s0_dump", [128, D], F32))
                B = _WQBufs(wst, nc, sems)
                absc = [wst.enter_context(
                    nc.sbuf_tensor(f"s0_ac{m}", [128, mats[m][2]], F32))
                    for m in range(3)]
                csum = [wst.enter_context(
                    nc.sbuf_tensor(f"s0_cs{m}", [128, 1], F32)) for m in range(3)]

                n_sacc_dv, n_sacc_ac = _ln_stats_acc(nc, S, sems, consts, s0dump,
                                                     (dS, n_x))
                n_sc = []
                for (w_in, wq_d, nck, wid, cs_, ci_, im_), am, cm in zip(
                        mats, absc, csum):
                    _wq_pass1(nc, S, sems, B, w_in, nck, wid, am)
                    n_sc.append(_wq_chain(nc, S, sems, am, cm, nck, wid, cols,
                                          cs_, ci_, im_, sc11, onesrf_sb,
                                          onescf_sb, ps_misc))
                n_stats_f1 = _ln_stats_tail(nc, S, sems, consts, n_sacc_ac)
                for (w_in, wq_d, nck, wid, cs_, ci_, im_), ns in zip(mats, n_sc):
                    _wq_pass2(nc, S, sems, B, w_in, wq_d, nck, wid, cols, cs_, ns)
                S.wait("sync", sems["ts0"], sems["ts0"].n)
                S.wait("sync", sems["ts1"], sems["ts1"].n)
                S.barrier(bar)
                S.flush()

        if _STAGE == "dbg_wq":
            S.dma("gpsimd", lambda e: e.dma_start(
                out=out.ap()[0:512, :].rearrange("(r s) c -> r s c", s=4),
                in_=wgq_d.ap()[0:128, :].rearrange("r (s c) -> r s c", c=1024)),
                dG)
            S.dma("sync", lambda e: e.dma_start(
                out=out.ap()[512:640, 0:16], in_=cols[:, :]), dS)
            S.dma("sync", lambda e: e.dma_start(
                out=out.ap()[640:641, 0:8], in_=sc11[:, :]), dS)
            S.wait("gpsimd", dG, dG.n)
            S.wait("gpsimd", dS, dS.n)
            S.flush()
            return nc
        if _STAGE in ("dbg_stats", "dbg_red"):
            with nc.sbuf_tensor("dbg_dump", [128, D], F32) as ddump, \
                 nc.sbuf_tensor("dbg_snap", [128, 112], F32) as dsnap:
                consts["snap"] = dsnap
                _ln_stats(nc, S, sems, consts, ddump, (dS, n_x),
                          chain=(_STAGE == "dbg_stats"))
                S.barrier(bar)
                S.dma("sync", lambda e: e.dma_start(
                    out=out.ap()[0:128, 0:160], in_=consts["stats"][:, :]), dS)
                S.dma("sync", lambda e: e.dma_start(
                    out=out.ap()[128:256, 0:112], in_=dsnap[:, :]), dS)
                S.wait("gpsimd", dS, dS.n)
                S.flush()
                consts.pop("snap")
            return nc

        if _STAGE == "dbg_xq":
            with nc.sbuf_tensor("dbg_dump", [128, D], F32) as ddump, \
                 nc.sbuf_tensor("dbg_x8", [128, D], I8) as x8, \
                 nc.sbuf_tensor("dbg_xb", [128, D], BF16) as xb:
                n_st = _ln_stats(nc, S, sems, consts, ddump, (dS, n_x))
                stats = consts["stats"]
                for t in range(NT):
                    S.op("vector", lambda e, t=t: e.tensor_scalar(
                        x8[:, :], resid[:, t * D:(t + 1) * D],
                        stats[:, 0 + t:1 + t], stats[:, 16 + t:17 + t],
                        ALU.subtract, ALU.mult),
                        sem=sems["dv"], waits=[(sems["dv"], n_st)] if t == 0 else [])
                    S.op("vector", lambda e: e.tensor_copy(xb[:, :], x8[:, :]),
                         sem=sems["dv"])
                    n_q = sems["dv"].n
                    S.dma("gpsimd", lambda e, t=t: e.dma_start(
                        out=out.ap().rearrange("(t p) c -> t p c", p=128)[t],
                        in_=xb[:, :]), dG, waits=[(sems["dv"], n_q)])
                    S.wait("vector", dG, dG.n)
                S.wait("gpsimd", dG, dG.n)
                S.flush()
            return nc

        with ExitStack() as p2st:
            p2_accs, p2_emit, p2_state = _p2_setup(nc, S, sems, consts,
                                                    p2st, g_in, b_in,
                                                    cc1a_in, cc1b_in,
                                                    cc1a_out, cc1b_out)

            def post_block(b, ffnctx):
                p2_accs(b, ffnctx)
                if b == 1:
                    p2_emit(0, ffnctx)
                elif b == 3:
                    p2_emit(1, ffnctx)

            _ffn_phase(nc, S, sems, consts, wgq_d, wuq_d, wdq_d,
                       wait_x=(dS, n_x), tag="f1", stats_n=n_stats_f1,
                       post_block=post_block)
        ccA_done, ccB_done = p2_state["cc_done"]
        if _STAGE == "dbg_raw":
            S.flush()
            return nc
        S.barrier(bar)
        S.flush()

        if _STAGE == "ffn1":
            _store_out(nc, S, sems, resid, out)
            S.flush()
            return nc

        with ExitStack() as astack:
            def asb(name, shape, dt=F32):
                return astack.enter_context(nc.sbuf_tensor(name, shape, dt))

            qnf = asb("qnf", [128, NSEQ * SEQ], BF16)
            knf = asb("knf", [128, NSEQ * SEQ], BF16)
            vres = asb("vres", [128, 128 * 130], BF16)

            _p3_qkv(nc, S, astack, sems, consts,
                    (wqm_in, wqr_in, wkm_in, wkr_in, wvm_in),
                    (cosq_in, sinq_in, cosk_in, sink_in),
                    cc1a_out, ccA_done, cc1b_out, ccB_done, qnf, knf, vres)
            if _STAGE in ("dbg_wro", "dbg_h2g"):
                S.flush()
                return nc
            S.barrier(bar)
            S.flush()

            if _STAGE in ("dbg_qnf", "dbg_knf", "dbg_vres"):
                src_t = {"dbg_qnf": qnf, "dbg_knf": knf, "dbg_vres": vres}[_STAGE]
                ncol = min(src_t.shape[1], 16384)
                S.dma("gpsimd", lambda e: e.dma_start(
                    out=out.ap().rearrange("(t p) c -> p t c", p=128)
                    [:, 0:ncol // D, :],
                    in_=src_t.ap()[:, 0:(ncol // D) * D]
                    .rearrange("p (t c) -> p t c", c=D)), sems["dG"])
                S.wait("gpsimd", sems["dG"], sems["dG"].n)
                S.flush()
                return nc

            cc2h = dict(a_in=cc2a_in, b_in=cc2b_in, a_out=cc2a_out,
                        ccA_done=None)
            if _STAGE == "dbg_p0":
                try:
                    _p4_attn(nc, S, astack, sems, consts, qnf, knf, vres, cc2h)
                except StopIteration:
                    pass
                return nc
            _p4_attn(nc, S, astack, sems, consts, qnf, knf, vres, cc2h)
            if _STAGE == "dbg_of":
                S.wait("gpsimd", sems["of0"], sems["of0"].n)
                S.wait("gpsimd", sems["of1"], sems["of1"].n)
                for r in range(NCORE):
                    for sl in range(2):
                        S.dma("gpsimd", lambda e, r=r, sl=sl: e.dma_start(
                            out=out.ap().rearrange("(a p) c -> a p c", p=128)
                            [r * 2 + sl],
                            in_=(cc2a_in if sl == 0 else cc2b_in).ap()[r]),
                            sems["dG"])
                S.wait("gpsimd", sems["dG"], sems["dG"].n)
                S.flush()
                return nc
            S.op("gpsimd", lambda e: e.collective_compute(
                "AllGather", ALU.bypass, replica_groups=[list(range(NCORE))],
                ins=[cc2b_in.ap().opt()], outs=[cc2b_out.ap().opt()]),
                sem=cc, waits=[(sems["of0"], sems["of0"].n),
                               (sems["of1"], sems["of1"].n)])
            ccB_done = cc.n
            S.flush()

        _p5_oproj(nc, S, sems, consts, cc2a_out, cc2h["ccA_done"],
                  cc2b_out, ccB_done, wout_in)
        S.barrier(bar)
        S.flush()

        if _STAGE == "attn":
            _store_out(nc, S, sems, resid, out)
            S.flush()
            return nc

        _ffn_phase(nc, S, sems, consts, wgq_d, wuq_d, wdq_d,
                   wait_x=None, tag="f2")
        S.barrier(bar)
        S.flush()

        _store_out(nc, S, sems, resid, out)
        S.flush()

    return nc



class _WQBufs:

    def __init__(self, st, nc, sems):
        self.wbuf = [st.enter_context(nc.sbuf_tensor(f"s0wb{i}", [128, INNER], F32))
                     for i in range(4)]
        self.t8 = st.enter_context(nc.sbuf_tensor("s0t8", [128, INNER], I8))
        self.tern = [st.enter_context(nc.sbuf_tensor(f"s0tn{i}", [128, INNER], BF16))
                     for i in range(2)]
        self.q = [("sync", sems["wq0"]), ("gpsimd", sems["wq1"]),
                  ("sync", sems["wq2"]), ("gpsimd", sems["wq3"])]
        self.sq = [("sync", sems["ts0"]), ("gpsimd", sems["ts1"])]
        self.last_read = [None, None, None, None]
        self.last_store = [None, None]


def _wq_pass1(nc, S, sems, B, w_in, nchunk, width, abscols):
    dv = sems["dv"]
    wv = w_in.ap().rearrange("(k p) w -> k p w", p=128)
    for k in range(nchunk):
        s = k % 4
        eng, sem = B.q[s]
        w = [B.last_read[s]] if B.last_read[s] else []
        S.dma(eng, lambda e, k=k, s=s: e.dma_start(
            out=B.wbuf[s][:, 0:width], in_=wv[k]), sem, waits=w)
        n_load = sem.n
        S.op("vector", lambda e, k=k, s=s: e.reduce_sum(
            abscols[:, k:k + 1], B.wbuf[s][:, 0:width], AX,
            apply_absolute_value=True), sem=dv, waits=[(sem, n_load)])
        B.last_read[s] = (dv, dv.n)


def _wq_chain(nc, S, sems, abscols, colsum, nchunk, width, cols, col_s, col_inv,
              inv_mult, sc11, onesrf, onescf, ps_misc):
    pe, ac, dv = sems["pe"], sems["ac"], sems["dv"]

    def serial(eng, fn, sem):
        S.op(eng, fn, sem=sem,
             waits=[(pe, pe.n), (ac, ac.n), (dv, dv.n)])

    serial("vector", lambda e: e.reduce_sum(colsum[:, :], abscols[:, 0:nchunk],
                                            AX), dv)
    serial("tensor", lambda e: e.matmul(ps_misc[0:1, 2:3], colsum[:, :],
                                        onescf[:, :], start=True, stop=True), pe)
    serial("scalar", lambda e: e.activation(sc11[:, 1:2], ps_misc[0:1, 2:3],
                                            AF.Copy,
                                            scale=1.0 / (128 * nchunk * width)),
           ac)
    serial("vector", lambda e: e.tensor_scalar_max(sc11[:, 2:3], sc11[:, 1:2],
                                                   1e-5), dv)
    serial("vector", lambda e: e.reciprocal(sc11[:, 3:4], sc11[:, 2:3]), dv)
    serial("vector", lambda e: e.tensor_single_scalar(sc11[:, 4:5], sc11[:, 2:3],
                                                      inv_mult, ALU.mult), dv)
    serial("tensor", lambda e: e.matmul(ps_misc[:, 3:4], onesrf[:, :],
                                        sc11[:, 3:4], start=True, stop=True), pe)
    serial("tensor", lambda e: e.matmul(ps_misc[:, 4:5], onesrf[:, :],
                                        sc11[:, 4:5], start=True, stop=True), pe)
    serial("scalar", lambda e: e.activation(cols[:, col_s:col_s + 1],
                                            ps_misc[:, 3:4], AF.Copy), ac)
    serial("scalar", lambda e: e.activation(cols[:, col_inv:col_inv + 1],
                                            ps_misc[:, 4:5], AF.Copy), ac)
    return ac.n


def _wq_pass2(nc, S, sems, B, w_in, wq_d, nchunk, width, cols, col_s, n_scale):
    ac, dv = sems["ac"], sems["dv"]
    wv = w_in.ap().rearrange("(k p) w -> k p w", p=128)
    if len(wq_d.shape) == 4:
        wqv = [wq_d.ap()[:, :, k, :].rearrange("ic p j -> p ic j")
               for k in range(nchunk)]
    else:
        w4 = wq_d.ap().rearrange("(k p) w -> k p w", p=128)
        wqv = [w4[k] for k in range(nchunk)]

    for k in range(nchunk):
        s = k % 4
        ts_ = k % 2
        eng, sem = B.q[s]
        w = [B.last_read[s]] if B.last_read[s] else []
        S.dma(eng, lambda e, k=k, s=s: e.dma_start(
            out=B.wbuf[s][:, 0:width], in_=wv[k]), sem, waits=w)
        n_load = sem.n
        wts = [(sem, n_load)]
        if k < 2:
            wts.append((ac, n_scale))
        S.op("vector", lambda e, s=s: e.tensor_single_scalar(
            B.t8[:, 0:width], B.wbuf[s][:, 0:width], cols[:, col_s:col_s + 1],
            ALU.mult), sem=dv, waits=wts)
        B.last_read[s] = (dv, dv.n)
        w2 = [B.last_store[ts_]] if B.last_store[ts_] else []
        S.op("vector", lambda e, ts_=ts_: e.tensor_scalar(
            B.tern[ts_][:, 0:width], B.t8[:, 0:width], 1.0, -1.0, ALU.min,
            ALU.max), sem=dv, waits=w2)
        n_tern = dv.n
        seng, ssem = B.sq[ts_]
        if len(wq_d.shape) == 4:
            S.dma(seng, lambda e, k=k, ts_=ts_: e.dma_start(
                out=wqv[k],
                in_=B.tern[ts_].ap().rearrange("p (ic j) -> p ic j", j=512)),
                ssem, waits=[(dv, n_tern)])
        else:
            S.dma(seng, lambda e, k=k, ts_=ts_: e.dma_start(
                out=wqv[k], in_=B.tern[ts_][:, 0:width]), ssem,
                waits=[(dv, n_tern)])
        B.last_store[ts_] = (ssem, ssem.n)



def _ln_stats_acc(nc, S, sems, consts, dump, wait_x):
    dS, pe, ac, dv = (sems[n] for n in ("dS", "pe", "ac", "dv"))
    resid, stats, cols = consts["resid"], consts["stats"], consts["cols"]

    S.op("scalar", lambda e: e.preload_act_table(AF.Sqrt), sem=ac)

    sumc = stats[:, 96:112]
    sqc = stats[:, 112:128]
    rmx = stats[:, 128:144]
    rmn = stats[:, 144:160]
    for t in range(NT):
        w = [wait_x] if (wait_x and t == 0) else []
        S.op("scalar", lambda e, t=t: e.activation(
            dump[:, 0:D], resid[:, t * D:(t + 1) * D], AF.Copy,
            accum_out=sumc[:, t:t + 1]), sem=ac, waits=w)
        S.op("scalar", lambda e, t=t: e.activation(
            dump[:, 0:D], resid[:, t * D:(t + 1) * D], AF.Square,
            accum_out=sqc[:, t:t + 1]), sem=ac)
        S.op("vector", lambda e, t=t: e.reduce_max(
            rmx[:, t:t + 1], resid[:, t * D:(t + 1) * D], AX),
            sem=dv, waits=([wait_x] if (wait_x and t == 0) else []))
        S.op("vector", lambda e, t=t: e.tensor_reduce(
            rmn[:, t:t + 1], resid[:, t * D:(t + 1) * D], AX, ALU.min), sem=dv)
    return dv.n, ac.n


def _ln_stats(nc, S, sems, consts, dump, wait_x, chain=True):
    n_dv, n_acc = _ln_stats_acc(nc, S, sems, consts, dump, wait_x)
    if not chain:
        return n_dv
    return _ln_stats_tail(nc, S, sems, consts, n_acc)


def _ln_stats_tail(nc, S, sems, consts, n_acc):
    dS, pe, ac, dv = (sems[n] for n in ("dS", "pe", "ac", "dv"))
    resid, stats, cols = consts["resid"], consts["stats"], consts["cols"]
    sumc = stats[:, 96:112]
    sqc = stats[:, 112:128]
    rmx = stats[:, 128:144]
    rmn = stats[:, 144:160]

    mu = stats[:, 0:16]
    a_q = stats[:, 16:32]
    c_g = stats[:, 32:48]
    c_u = stats[:, 48:64]
    t1 = stats[:, 64:80]
    t2 = stats[:, 80:96]

    S.op("vector", lambda e: e.tensor_single_scalar(mu[:, :], sumc[:, :], 1.0 / D,
                                                    ALU.mult),
         sem=dv, waits=[(ac, n_acc)])
    S.op("vector", lambda e: e.tensor_single_scalar(sqc[:, :], sqc[:, :], 1.0 / D,
                                                    ALU.mult), sem=dv)
    S.op("vector", lambda e: e.tensor_mul(t1[:, :], mu[:, :], mu[:, :]), sem=dv)
    S.op("vector", lambda e: e.tensor_sub(sqc[:, :], sqc[:, :], t1[:, :]), sem=dv)
    n_var = dv.n
    S.op("scalar", lambda e: e.activation(t1[:, :], sqc[:, :], AF.Sqrt,
                                          bias=cols[:, 8:9]),
         sem=ac, waits=[(dv, n_var)])
    n_std = ac.n
    S.op("vector", lambda e: e.reciprocal(t1[:, :], t1[:, :]),
         sem=dv, waits=[(ac, n_std)])
    n_rstd = dv.n
    snap = consts.get("snap")

    def snp(j):
        if snap is not None:
            S.op("vector", lambda e, j=j: e.tensor_copy(
                snap[:, 16 * j:16 * j + 16],
                stats[:, 128:144] if j < 5 else (stats[:, 80:96] if j == 5
                                                 else stats[:, 16:32])), sem=dv)

    S.op("vector", lambda e: e.tensor_sub(rmx[:, :], rmx[:, :], mu[:, :]),
         sem=dv, waits=[(dv, n_rstd)])
    snp(0)
    S.op("vector", lambda e: e.tensor_sub(rmn[:, :], mu[:, :], rmn[:, :]), sem=dv)
    S.op("vector", lambda e: e.tensor_max(rmx[:, :], rmx[:, :], rmn[:, :]), sem=dv)
    snp(1)
    S.op("vector", lambda e: e.tensor_mul(rmx[:, :], rmx[:, :], t1[:, :]), sem=dv)
    snp(2)
    S.op("vector", lambda e: e.tensor_scalar_max(rmx[:, :], rmx[:, :], 1e-5), sem=dv)
    snp(3)
    snp(4)
    S.op("vector", lambda e: e.reciprocal(t2[:, :], rmx[:, :]), sem=dv)
    snp(5)
    S.op("vector", lambda e: e.scalar_tensor_tensor(
        a_q[:, :], t2[:, :], 127.0, t1[:, :], ALU.mult, ALU.mult), sem=dv)
    snp(6)
    S.op("vector", lambda e: e.tensor_single_scalar(
        c_g[:, :], rmx[:, :], cols[:, 3:4], ALU.mult), sem=dv)
    S.op("vector", lambda e: e.tensor_single_scalar(
        c_u[:, :], rmx[:, :], cols[:, 4:5], ALU.mult), sem=dv)
    return dv.n



def _ffn_phase(nc, S, sems, consts, wgq_d, wuq_d, wdq_d, wait_x, tag,
               stats_n=None, post_block=None):
    dS, pe, ac, dv = (sems[n] for n in ("dS", "pe", "ac", "dv"))
    resid, stats, cols, i128_sb = (consts["resid"], consts["stats"],
                                   consts["cols"], consts["i128"])
    mu = stats[:, 0:16]
    a_q = stats[:, 16:32]
    c_g = stats[:, 32:48]
    c_u = stats[:, 48:64]
    hcd = stats[:, 96:112]

    wgv = wgq_d.ap().rearrange("ic p kk j -> ic p (kk j)")
    wuv = wuq_d.ap().rearrange("ic p kk j -> ic p (kk j)")
    wdv = wdq_d.ap().rearrange("(jc p) (h w) -> jc h p w", p=128, w=512)

    with ExitStack() as fst:
        def fsb(name, shape, dt):
            return fst.enter_context(nc.sbuf_tensor(f"{tag}_{name}", shape, dt))

        wg_sb = [fsb(f"wg{i}", [128, 8 * 512], BF16) for i in range(2)]
        wu_sb = [fsb(f"wu{i}", [128, 8 * 512], BF16) for i in range(2)]
        wd_sb = [fsb(f"wd{i}", [128, 512], BF16) for i in range(2)]
        xq8 = fsb("xq8", [128, D], I8)
        xqb = fsb("xqb", [128, D], BF16)
        xqT = [fsb(f"xqT{i}", [128, D], BF16) for i in range(TPB)]
        silu_s = [fsb(f"silu{i}", [128, 512], F32) for i in range(2)]
        gq8 = fsb("gq8", [128, INNER], I8)
        mcol = fsb("mcol", [128, 8], F32)
        dump = fsb("dump", [128, D], F32)

        big = fsb("big", [128, TPB * 16384], mybir.dt.uint8)
        base = nc.lookup_mloc(big).addr
        raw_s = [nc.alloc_sbuf_tensor_at(f"{tag}_raw{i}", [128, INNER], F32,
                                         offset=base + i * 16384)
                 for i in range(TPB)]
        gqb_s = [nc.alloc_sbuf_tensor_at(f"{tag}_gqb{i}", [128, INNER], BF16,
                                         offset=base + i * 16384)
                 for i in range(TPB)]
        gqT_s = [nc.alloc_sbuf_tensor_at(f"{tag}_gqT{i}", [128, INNER], BF16,
                                         offset=base + i * 16384 + 8192)
                 for i in range(TPB)]

        if stats_n is None:
            n_stats = _ln_stats(nc, S, sems, consts, dump, wait_x)
        else:
            n_stats = stats_n
        S.maybe_bar(sems["bar"], 1)
        S.op("scalar", lambda e: e.preload_act_table(AF.Silu), sem=ac)

        with nc.psum_tensor(f"{tag}_ptr", [128, 1024], F32) as ptr, \
             nc.psum_tensor(f"{tag}_pg0", [128, 512], F32) as pg0, \
             nc.psum_tensor(f"{tag}_pg1", [128, 512], F32) as pg1, \
             nc.psum_tensor(f"{tag}_pu0", [128, 512], F32) as pu0, \
             nc.psum_tensor(f"{tag}_pu1", [128, 512], F32) as pu1, \
             nc.psum_tensor(f"{tag}_pd0", [128, 512], F32) as pd0, \
             nc.psum_tensor(f"{tag}_pd1", [128, 512], F32) as pd1:
            pg, pu = [pg0, pg1], [pu0, pu1]
            pd4 = [pg0, pg1, pd0, pd1]

            strip_loads = {}
            strip_free = {}
            tr_pe = {}
            ptr_evac = []
            n_gu_last = 0
            n_down_evac = 0
            n_down_mm = 0

            wg2 = [sems["wg0"], sems["wg1"]]
            wu2 = [sems["wu0"], sems["wu1"]]
            wd2 = [sems["wd0"], sems["wd1"]]

            def load_strips(b, ic):
                slot = ic % 2
                prevkey = (b, ic - 2) if ic >= 2 else (b - 1, ic + 6)
                prev = strip_free.get(prevkey)
                w = [(pe, prev)] if prev else []
                S.dma("sync", lambda e, ic=ic, slot=slot: e.dma_start(
                    out=wg_sb[slot][:, :], in_=wgv[ic]), wg2[slot], waits=w)
                S.dma("sync", lambda e, ic=ic, slot=slot: e.dma_start(
                    out=wu_sb[slot][:, :], in_=wuv[ic]), wu2[slot])
                strip_loads[(b, ic)] = (wg2[slot].n, wu2[slot].n)

            def emit_xq(b, n_gu_prev):
                xq_evac = {}
                for i, t in enumerate([b * TPB + j for j in range(TPB)]):
                    wq = [(dv, n_stats)] if (b == 0 and i == 0) else []
                    if i >= 1:
                        wq.append((pe, xq_evac[i - 1][1]))
                    S.op("vector", lambda e, t=t: e.tensor_scalar(
                        xq8[:, :], resid[:, t * D:(t + 1) * D],
                        mu[:, t:t + 1], a_q[:, t:t + 1], ALU.subtract, ALU.mult),
                        sem=dv, waits=wq)
                    S.op("vector", lambda e: e.tensor_copy(xqb[:, :], xq8[:, :]),
                         sem=dv)
                    n_xq = dv.n
                    for c in range(8):
                        w = []
                        if c == 0:
                            w = [(dv, n_xq)]
                            if ptr_evac:
                                w.append((ac, ptr_evac[-1]))
                        S.op("tensor", lambda e, c=c: e.matmul(
                            ptr[:, c * 128:(c + 1) * 128],
                            xqb[:, 128 * c:128 * c + 128], i128_sb[:, :],
                            start=True, stop=True),
                            sem=(pe if c == 7 else None), waits=w)
                    n_tr = pe.n
                    wev = [(pe, n_tr)]
                    if b > 0:
                        wev.append((pe, n_gu_prev))
                    S.op("scalar", lambda e, i=i: e.activation(
                        xqT[i][:, 0:512], ptr[:, 0:512], AF.Copy),
                        sem=ac, waits=wev)
                    S.op("scalar", lambda e, i=i: e.activation(
                        xqT[i][:, 512:1024], ptr[:, 512:1024], AF.Copy), sem=ac)
                    ptr_evac.append(ac.n)
                    xq_evac[i] = (ac.n, n_tr)
                return ac.n

            xq_all = {0: emit_xq(0, 0)}
            for b in range(NB):
                tiles = [b * TPB + i for i in range(TPB)]
                n_xq_all = xq_all[b]

                if (b, 0) not in strip_loads:
                    load_strips(b, 0)
                if (b, 1) not in strip_loads:
                    load_strips(b, 1)
                evac_g, evac_u = {}, {}
                for ic in range(8):
                    n_wg, n_wu = strip_loads[(b, ic)]
                    for i, t in enumerate(tiles):
                        ps = i % 2
                        w_mm = [(wg2[ic % 2], n_wg), (wu2[ic % 2], n_wu),
                                (ac, n_xq_all)]
                        if b > 0 and ic == 0 and i < 2:
                            w_mm.append((dv, n_down_evac))
                        prev = (ic, i - 2) if i >= 2 else ((ic - 1, i + 2) if ic >= 1 else None)
                        if prev in evac_g:
                            w_mm.append((ac, evac_g[prev]))
                        if prev in evac_u:
                            w_mm.append((dv, evac_u[prev]))
                        for kk in range(8):
                            S.op("tensor", lambda e, i=i, kk=kk, ps=ps, ic=ic: e.matmul(
                                pg[ps][:, :], xqT[i][:, 128 * kk:128 * kk + 128],
                                wg_sb[ic % 2][:, 512 * kk:512 * kk + 512],
                                start=(kk == 0), stop=(kk == 7)),
                                sem=(pe if kk == 7 else None),
                                waits=(w_mm if kk == 0 else []))
                        n_g = pe.n
                        for kk in range(8):
                            S.op("tensor", lambda e, i=i, kk=kk, ps=ps, ic=ic: e.matmul(
                                pu[ps][:, :], xqT[i][:, 128 * kk:128 * kk + 128],
                                wu_sb[ic % 2][:, 512 * kk:512 * kk + 512],
                                start=(kk == 0), stop=(kk == 7)),
                                sem=(pe if kk == 7 else None))
                        n_u = pe.n
                        strip_free[(b, ic)] = n_u
                        w_silu = [(pe, n_g)]
                        if prev in evac_u:
                            w_silu.append((dv, evac_u[prev]))
                        S.op("scalar", lambda e, t=t, ps=ps: e.activation(
                            silu_s[ps][:, :], pg[ps][:, :], AF.Silu,
                            scale=c_g[:, t:t + 1]),
                            sem=ac, waits=w_silu)
                        evac_g[(ic, i)] = ac.n
                        w_raw = [(ac, evac_g[(ic, i)]), (pe, n_u)]
                        if b > 0 and ic == 0:
                            w_raw.append((pe, n_down_mm))
                        S.op("vector", lambda e, i=i, ic=ic, ps=ps: e.tensor_mul(
                            raw_s[i][:, 512 * ic:512 * ic + 512], silu_s[ps][:, :],
                            pu[ps][:, :]),
                            sem=dv, waits=w_raw)
                        evac_u[(ic, i)] = dv.n
                    S.maybe_bar(sems["bar"], 2)
                    if ic + 2 < 8:
                        load_strips(b, ic + 2)
                    elif b + 1 < NB:
                        load_strips(b + 1, ic + 2 - 8)
                n_gu_last = pe.n

                if _STAGE == "dbg_raw" and b == 0:
                    for i in range(TPB):
                        S.dma("gpsimd", lambda e, i=i: e.dma_start(
                            out=consts["out"].ap().rearrange(
                                "(f p) c -> f p c", p=128)[4 * i:4 * i + 4],
                            in_=raw_s[i].ap().rearrange("p (f c) -> f p c", c=D)),
                            sems["dG"], waits=[(dv, evac_u[(7, i)])])
                    S.wait("gpsimd", sems["dG"], sems["dG"].n)
                    return

                for i, t in enumerate(tiles):
                    n_raw = evac_u[(7, i)]
                    S.op("vector", lambda e, i=i: e.reduce_max(
                        mcol[:, 0:1], raw_s[i][:, :], AX, apply_absolute_value=True),
                        sem=dv, waits=[(dv, n_raw)])
                    S.op("vector", lambda e, t=t: e.tensor_single_scalar(
                        mcol[:, 1:2], mcol[:, 0:1], c_u[:, t:t + 1], ALU.mult), sem=dv)
                    S.op("vector", lambda e: e.tensor_scalar_max(
                        mcol[:, 1:2], mcol[:, 1:2], 1e-5), sem=dv)
                    S.op("vector", lambda e: e.reciprocal(mcol[:, 2:3], mcol[:, 1:2]),
                         sem=dv)
                    S.op("vector", lambda e, t=t: e.scalar_tensor_tensor(
                        mcol[:, 3:4], mcol[:, 2:3], 127.0, c_u[:, t:t + 1],
                        ALU.mult, ALU.mult), sem=dv)
                    S.op("vector", lambda e, t=t: e.tensor_single_scalar(
                        hcd[:, t:t + 1], mcol[:, 1:2], cols[:, 5:6], ALU.mult), sem=dv)
                    wq8 = []
                    if i >= 1:
                        wq8.append((pe, tr_pe[i - 1]))
                    S.op("vector", lambda e, i=i: e.tensor_single_scalar(
                        gq8[:, :], raw_s[i][:, :], mcol[:, 3:4], ALU.mult),
                        sem=dv, waits=wq8)
                    S.op("vector", lambda e, i=i: e.tensor_copy(
                        gqb_s[i][:, :], gq8[:, :]), sem=dv)
                    n_gqb = dv.n
                    for grp in range(8):
                        half = (grp % 2) * 512
                        for c in range(4):
                            jc = grp * 4 + c
                            w = []
                            if c == 0:
                                if grp == 0:
                                    w.append((dv, n_gqb))
                                if ptr_evac:
                                    w.append((ac, ptr_evac[-1] if grp < 2 else ptr_evac[-2]))
                            S.op("tensor", lambda e, i=i, jc=jc, half=half, c=c: e.matmul(
                                ptr[:, half + c * 128: half + c * 128 + 128],
                                gqb_s[i][:, 128 * jc:128 * jc + 128], i128_sb[:, :],
                                start=True, stop=True),
                                sem=(pe if c == 3 else None), waits=w)
                        n_tr = pe.n
                        S.op("scalar", lambda e, i=i, grp=grp, half=half: e.activation(
                            gqT_s[i][:, grp * 512: grp * 512 + 512],
                            ptr[:, half: half + 512], AF.Copy),
                            sem=ac, waits=[(pe, n_tr)])
                        ptr_evac.append(ac.n)
                    tr_pe[i] = pe.n
                    S.maybe_bar(sems["bar"], 2)
                n_gqT_ready = ac.n

                ev_prev = n_down_evac
                wd_slot_free = getattr(_ffn_phase, "_wd_free", {0: 0, 1: 0})
                for half in range(2):
                    first_mm_wait = [(ac, n_gqT_ready)]
                    if half == 0:
                        first_mm_wait.append((ac, evac_g[(7, 3)]))
                        first_mm_wait.append((dv, evac_u[(7, 3)]))
                        if b > 0:
                            first_mm_wait.append((dv, ev_prev))
                    else:
                        first_mm_wait.append((dv, n_down_evac))
                    for jc in range(32):
                        slot = jc % 2
                        wl = [(pe, wd_slot_free[slot])] if wd_slot_free[slot] else []
                        S.dma("sync", lambda e, jc=jc, half=half, slot=slot: e.dma_start(
                            out=wd_sb[slot][:, :], in_=wdv[jc, half]), wd2[slot],
                            waits=wl)
                        n_wd = wd2[slot].n
                        for i, t in enumerate(tiles):
                            w = []
                            if jc == 0:
                                w = list(first_mm_wait) if i == 0 else []
                            if i == 0:
                                w.append((wd2[slot], n_wd))
                            S.op("tensor", lambda e, i=i, jc=jc, slot=slot: e.matmul(
                                pd4[i][:, :], gqT_s[i][:, 128 * jc:128 * jc + 128],
                                wd_sb[slot][:, :],
                                start=(jc == 0), stop=(jc == 31)),
                                sem=(pe if i == TPB - 1 else None), waits=w)
                        wd_slot_free[slot] = pe.n
                    n_dmm = pe.n
                    for i, t in enumerate(tiles):
                        S.op("vector", lambda e, t=t, i=i, half=half: e.scalar_tensor_tensor(
                            resid[:, t * D + half * 512: t * D + half * 512 + 512],
                            pd4[i][:, :], hcd[:, t:t + 1],
                            resid[:, t * D + half * 512: t * D + half * 512 + 512],
                            ALU.mult, ALU.add),
                            sem=dv, waits=[(pe, n_dmm)])
                    n_down_evac = dv.n
                    if half == 0 and b + 1 < NB:
                        xq_all[b + 1] = emit_xq(b + 1, n_gu_last)
                    S.maybe_bar(sems["bar"], 2)
                n_down_mm = pe.n
                if post_block is not None:
                    post_block(b, dict(ptr=ptr, ptr_evac=ptr_evac,
                                       n_down_evac=n_down_evac, dump=dump))
                S.maybe_bar(sems["bar"], 1)



def _p2_setup(nc, S, sems, consts, st, g_in, b_in, cc1a_in, cc1b_in,
              cc1a_out, cc1b_out):
    dS, pe, ac, dv, cc = (sems[n] for n in ("dS", "pe", "ac", "dv", "cc"))
    resid, stats, i128_sb = consts["resid"], consts["stats"], consts["i128"]

    def psb(name, shape, dt=F32):
        return st.enter_context(nc.sbuf_tensor(name, shape, dt))

    g_t = psb("p2_g", [128, D])
    b_t = psb("p2_b", [128, D])
    nbuf = psb("p2_n", [128, D])
    h2b = psb("p2_h2b", [128, D], BF16)
    h2f = [psb(f"p2_h2f{i}", [128, D], BF16) for i in range(2)]

    S.dma("sync", lambda e: e.dma_start(out=g_t[:, :], in_=g_in[:, :]), dS)
    S.dma("sync", lambda e: e.dma_start(out=b_t[:, :], in_=b_in[:, :]), dS)
    n_gb = dS.n

    sumc = stats[:, 96:112]
    sqc = stats[:, 112:128]
    rstd2 = stats[:, 64:80]
    b2 = stats[:, 80:96]
    state = dict(prev_tt=0, prev_ev={}, prev_dma={}, cc_done=[None, None])

    def emit_accs(b, ffnctx):
        dump = ffnctx["dump"]
        for k, t in enumerate(range(4 * b, 4 * b + 4)):
            w = [(dv, ffnctx["n_down_evac"])] if k == 0 else []
            S.op("scalar", lambda e, t=t: e.activation(
                dump[:, :], resid[:, t * D:(t + 1) * D], AF.Copy,
                accum_out=sumc[:, t:t + 1]), sem=ac, waits=w)
            S.op("scalar", lambda e, t=t: e.activation(
                dump[:, :], resid[:, t * D:(t + 1) * D], AF.Square,
                accum_out=sqc[:, t:t + 1]), sem=ac)
        state["acc_n"] = ac.n

    def emit_half(h, ffnctx):
        dump = ffnctx["dump"]
        t0 = 8 * h
        hs = slice(t0, t0 + 8)
        n_acc = state["acc_n"]
        S.op("vector", lambda e: e.tensor_single_scalar(
            sumc[:, hs], sumc[:, hs], 1.0 / D, ALU.mult),
            sem=dv, waits=[(ac, n_acc)])
        S.op("vector", lambda e: e.tensor_single_scalar(
            sqc[:, hs], sqc[:, hs], 1.0 / D, ALU.mult), sem=dv)
        S.op("vector", lambda e: e.tensor_mul(b2[:, hs], sumc[:, hs],
                                              sumc[:, hs]), sem=dv)
        S.op("vector", lambda e: e.tensor_sub(sqc[:, hs], sqc[:, hs],
                                              b2[:, hs]), sem=dv)
        n_var = dv.n
        S.op("scalar", lambda e: e.preload_act_table(AF.Sqrt), sem=ac)
        S.op("scalar", lambda e: e.activation(rstd2[:, hs], sqc[:, hs], AF.Sqrt,
                                              bias=consts["cols"][:, 8:9]),
             sem=ac, waits=[(dv, n_var)])
        n_std2 = ac.n
        if h == 0:
            S.op("scalar", lambda e: e.preload_act_table(AF.Silu), sem=ac)
        S.op("vector", lambda e: e.reciprocal(rstd2[:, hs], rstd2[:, hs]),
             sem=dv, waits=[(ac, n_std2)])
        n_rstd = dv.n

        ptr2 = ffnctx["ptr"]
        ptr_evac = ffnctx["ptr_evac"]
        for t in range(t0, t0 + 8):
            slot = t % 2
            w_n = [(dv, n_rstd), (dS, n_gb)]
            if state["prev_tt"]:
                w_n.append((dv, state["prev_tt"]))
            S.op("vector", lambda e, t=t: e.tensor_scalar(
                nbuf[:, :], resid[:, t * D:(t + 1) * D],
                sumc[:, t:t + 1], rstd2[:, t:t + 1], ALU.subtract, ALU.mult),
                sem=dv, waits=w_n)
            w_m = []
            if (t - 1) in state["prev_ev"]:
                w_m.append((pe, state["prev_ev"][(t - 1, "pe")]))
            S.op("vector", lambda e: e.tensor_mul(nbuf[:, :], nbuf[:, :],
                                                  g_t[:, :]),
                 sem=dv, waits=w_m)
            S.op("vector", lambda e: e.tensor_add(h2b[:, :], nbuf[:, :],
                                                  b_t[:, :]), sem=dv)
            state["prev_tt"] = dv.n
            n_h2b = dv.n
            for c in range(8):
                w2 = []
                if c == 0:
                    w2 = [(dv, n_h2b)]
                    if ptr_evac:
                        w2.append((ac, ptr_evac[-1]))
                S.op("tensor", lambda e, c=c: e.matmul(
                    ptr2[:, c * 128:(c + 1) * 128],
                    h2b[:, 128 * c:128 * c + 128], i128_sb[:, :],
                    start=True, stop=True),
                    sem=(pe if c == 7 else None), waits=w2)
            n_tr = pe.n
            wev = [(pe, n_tr)]
            if (t - 2) in state["prev_dma"]:
                wev.append(state["prev_dma"][t - 2])
            S.op("scalar", lambda e, slot=slot: e.activation(
                h2f[slot][:, 0:512], ptr2[:, 0:512], AF.Copy),
                sem=ac, waits=wev)
            S.op("scalar", lambda e, slot=slot: e.activation(
                h2f[slot][:, 512:1024], ptr2[:, 512:1024], AF.Copy), sem=ac)
            state["prev_ev"][t] = ac.n
            state["prev_ev"][(t, "pe")] = n_tr
            ptr_evac.append(ac.n)
            h2fs = sems["h2f0"] if slot == 0 else sems["h2f1"]
            cct = cc1a_in if h == 0 else cc1b_in
            S.dma("sync", lambda e, t=t, slot=slot, cct=cct: e.dma_start(
                out=cct.ap()[t % 8], in_=h2f[slot][:, :]),
                h2fs, waits=[(ac, state["prev_ev"][t])])
            state["prev_dma"][t] = (h2fs, h2fs.n)

        src_t, dst_t = ((cc1a_in, cc1a_out) if h == 0 else (cc1b_in, cc1b_out))
        S.op("gpsimd", lambda e, src_t=src_t, dst_t=dst_t: e.collective_compute(
            "AllGather", ALU.bypass, replica_groups=[list(range(NCORE))],
            ins=[src_t.ap().opt()], outs=[dst_t.ap().opt()]),
            sem=cc, waits=[(sems["h2f0"], sems["h2f0"].n),
                           (sems["h2f1"], sems["h2f1"].n)])
        state["cc_done"][h] = cc.n

    return emit_accs, emit_half, state



def _p3_qkv(nc, S, astack, sems, consts, wslices, ropes, cc1a_out, ccA_done,
            cc1b_out, ccB_done, qnf, knf, vres):
    dS, dG, pe, ac, dv, cc = (sems[n] for n in ("dS", "dG", "pe", "ac", "dv", "cc"))
    cols, i128_sb = consts["cols"], consts["i128"]
    ind_sq_sb, ind_bc_sb = consts["ind_sq"], consts["ind_bc"]
    wqm_in, wqr_in, wkm_in, wkr_in, wvm_in = wslices
    cosq_in, sinq_in, cosk_in, sink_in = ropes

    S.op("scalar", lambda e: e.add_instruction(mybir.InstLoadActFuncSet(
        name=nc.get_next_instruction_name(), act_func_set_id=6,
        ins=[], outs=[])), sem=sems["ac"])
    wq_ro = astack.enter_context(nc.sbuf_tensor("wq_ro", [128, D], BF16))
    wk_ro = astack.enter_context(nc.sbuf_tensor("wk_ro", [128, D], BF16))
    wv_bf = astack.enter_context(nc.sbuf_tensor("wv_bf", [128, D], BF16))

    def _rope_fold():
        with ExitStack() as ropest:
            wm = ropest.enter_context(nc.sbuf_tensor("p3_wm", [128, D], F32))
            wr = ropest.enter_context(nc.sbuf_tensor("p3_wr", [128, D], F32))
            cosb = ropest.enter_context(nc.sbuf_tensor("p3_cos", [128, 128], F32))
            sinb = ropest.enter_context(nc.sbuf_tensor("p3_sin", [128, 128], F32))
            tmp = ropest.enter_context(nc.sbuf_tensor("p3_tmp", [128, 128], F32))
            tmp2 = ropest.enter_context(nc.sbuf_tensor("p3_tmp2", [128, 128], F32))

            for (m_in, r_in, c_in, s_in, dst) in (
                    (wqm_in, wqr_in, cosq_in, sinq_in, wq_ro),
                    (wkm_in, wkr_in, cosk_in, sink_in, wk_ro)):
                n_prev_use = dv.n
                S.dma("gpsimd", lambda e, m=m_in: e.dma_start(
                    out=wm.ap().rearrange("p (k c) -> p k c", c=128),
                    in_=m.ap().rearrange("(k p) c -> p k c", p=128)),
                    dG, waits=[(dv, n_prev_use)])
                S.dma("gpsimd", lambda e, r=r_in: e.dma_start(
                    out=wr.ap().rearrange("p (k c) -> p k c", c=128),
                    in_=r.ap().rearrange("(k p) c -> p k c", p=128)), dG)
                S.dma("gpsimd", lambda e, c=c_in: e.dma_start(
                    out=cosb[:, :], in_=c[:, :]), dG)
                S.dma("gpsimd", lambda e, s=s_in: e.dma_start(
                    out=sinb[:, :], in_=s[:, :]), dG)
                n_l = dG.n
                for k in range(8):
                    S.op("vector", lambda e, k=k: e.tensor_mul(
                        tmp[:, :], wr[:, 128 * k:128 * k + 128], sinb[:, :]),
                        sem=dv, waits=[(dG, n_l)] if k == 0 else [])
                    S.op("vector", lambda e, k=k: e.tensor_mul(
                        tmp2[:, :], wm[:, 128 * k:128 * k + 128], cosb[:, :]),
                        sem=dv)
                    S.op("vector", lambda e, k=k, dst=dst: e.tensor_add(
                        dst[:, 128 * k:128 * k + 128], tmp2[:, :], tmp[:, :]),
                        sem=dv)
        return dv.n

    with ExitStack() as rst:
        n_ro = _rope_fold()
        if _STAGE == "dbg_wro":
            S.wait("gpsimd", dv, n_ro)
            S.dma("gpsimd", lambda e: e.dma_start(
                out=consts["out"].ap()[0:128, :], in_=wq_ro[:, :]), dG)
            S.dma("gpsimd", lambda e: e.dma_start(
                out=consts["out"].ap()[128:256, :], in_=wk_ro[:, :]), dG)
            S.wait("gpsimd", dG, dG.n)
            S.flush()
            return
        S.dma("gpsimd", lambda e: e.dma_start(
            out=wv_bf.ap().rearrange("p (k c) -> p k c", c=128),
            in_=wvm_in.ap().rearrange("(k p) c -> p k c", p=128)), dG)
        n_wv = dG.n

        h2g = [rst.enter_context(nc.sbuf_tensor(f"p3_h2g{i}", [128, 8 * 512], BF16))
               for i in range(2)]
        qraw = [rst.enter_context(nc.sbuf_tensor(f"p3_qraw{i}", [128, 1024], BF16))
                for i in range(3)]
        sqb = [rst.enter_context(nc.sbuf_tensor(f"p3_sqb{i}", [128, 1024], BF16))
               for i in range(2)]
        nrm = rst.enter_context(nc.sbuf_tensor("p3_nrm", [2, 512], F32))
        nrm2 = rst.enter_context(nc.sbuf_tensor("p3_nrm2", [2, 512], F32))
        rq_s = [rst.enter_context(nc.sbuf_tensor(f"p3_rq{i}", [2, 512], BF16))
                for i in range(2)]
        bcs = rst.enter_context(nc.sbuf_tensor("p3_bcs", [128, 512], BF16))

        with nc.psum_tensor("p3_pq", [128, 512], F32) as pq, \
             nc.psum_tensor("p3_pk", [128, 512], F32) as pk, \
             nc.psum_tensor("p3_pv", [128, 512], F32) as pv, \
             nc.psum_tensor("p3_pn", [2, 512], F32) as pn, \
             nc.psum_tensor("p3_pbc", [128, 512], F32) as pbc:

            S.op("vector", lambda e: e.memset(
                vres.ap().rearrange("p (t h one) -> p t h one", h=2, one=65)
                [:, :, :, 64:65], 1.0), sem=dv)

            h2g2 = [sems["h2g0"], sems["h2g1"]]
            load_done = {}
            qk_evac = {}
            v_evac = {}
            cst = dict(n_n=0, n_rq=0, n_pbc=0, n_bcs=0, last_tt=0, n_pnread=0)
            chain_pn = {}
            chain_done = {}
            rqc = {}
            pendA, pendB = [], []
            for g in range(32):
                slot = g % 2
                sl3 = g % 3
                w = ([(cc, ccA_done), (dv, n_ro)] if g < 2
                     else [(pe, load_done[(g - 2, "pe")])])
                if g in (16, 17):
                    w.append((cc, ccB_done))
                for i in range(4):
                    tg = 4 * g + i
                    srcT = cc1a_out if tg // 8 < 8 else cc1b_out
                    S.dma("gpsimd", lambda e, tg=tg, i=i, slot=slot, srcT=srcT:
                          e.dma_start(
                        out=h2g[slot][:, i * 1024:(i + 1) * 1024],
                        in_=srcT.ap()[tg % 8, (tg // 8) % 8]),
                        h2g2[slot], waits=(w if i == 0 else []))
                n_l = h2g2[slot].n

                for (dst_ps, wsb) in ((pq, wq_ro), (pk, wk_ro)):
                    for c in range(8):
                        w2 = []
                        if c == 0:
                            w2 = [(h2g2[slot], n_l), (dv, n_ro)]
                            if g >= 1:
                                w2.append((ac, qk_evac.get(g - 1, (0, 0))[0]))
                        S.op("tensor", lambda e, c=c, dst_ps=dst_ps, wsb=wsb, slot=slot: e.matmul(
                            dst_ps[:, :], wsb[:, 128 * c:128 * c + 128],
                            h2g[slot].ap().rearrange("p (i cq) -> p i cq", cq=1024)
                            [:, :, c * 128:(c + 1) * 128],
                            start=(c == 0), stop=(c == 7)),
                            sem=(pe if c == 7 else None), waits=w2)
                n_qk = pe.n
                if _STAGE == "dbg_h2g" and g == 0:
                    S.wait("gpsimd", h2g2[0], h2g2[0].n)
                    for f in range(4):
                        S.dma("gpsimd", lambda e, f=f: e.dma_start(
                            out=consts["out"].ap().rearrange("(f p) c -> f p c",
                                                             p=128)[f],
                            in_=h2g[0][:, f * D:(f + 1) * D]), sems["dG"])
                    S.op("scalar", lambda e: e.activation(
                        qraw[0][:, 0:512], pq[:, :], AF.Copy), sem=ac,
                        waits=[(pe, n_qk)])
                    S.wait("gpsimd", ac, ac.n)
                    S.dma("gpsimd", lambda e: e.dma_start(
                        out=consts["out"].ap().rearrange("(f p) c -> f p c", p=128)
                        [4, :, 0:512],
                        in_=qraw[0][:, 0:512]), sems["dG"])
                    S.wait("gpsimd", sems["dG"], sems["dG"].n)
                    return
                for i in range(4):
                    for c in range(8):
                        w2 = []
                        if i == 0 and c == 0:
                            w2 = [(dG, n_wv)]
                            if g >= 1:
                                w2.append((ac, v_evac.get(g - 1, 0)))
                        S.op("tensor", lambda e, i=i, c=c, slot=slot: e.matmul(
                            pv[:, i * 128:(i + 1) * 128],
                            h2g[slot][:, i * 1024 + c * 128: i * 1024 + c * 128 + 128],
                            wv_bf[:, 128 * c:128 * c + 128],
                            start=(c == 0), stop=(c == 7)),
                            sem=(pe if (c == 7 and i == 3) else None), waits=w2)
                n_v = pe.n
                load_done[(g, "pe")] = n_v

                S.op("scalar", lambda e, g=g: e.activation(
                    vres.ap().rearrange("p (t h one) -> p t h one", h=2, one=65)
                    [:, 4 * g:4 * g + 4, :, 0:64],
                    pv.ap().rearrange("p (i h d) -> p i h d", i=4, h=2),
                    AF.Copy), sem=ac, waits=[(pe, n_v)])
                v_evac[g] = ac.n

                w_sq = [(pe, n_qk)]
                if (g - 2) in chain_pn:
                    w_sq.append((pe, chain_pn[g - 2]))
                if (g - 3) in chain_done:
                    w_sq.append((dv, chain_done[g - 3]))
                for (side, ps_in) in ((0, pq), (1, pk)):
                    S.op("scalar", lambda e, ps_in=ps_in, side=side, slot=slot:
                         e.activation(sqb[slot][:, side * 512:side * 512 + 512],
                                      ps_in[:, :], AF.Square),
                         sem=ac, waits=(w_sq if side == 0 else []))
                    S.op("scalar", lambda e, ps_in=ps_in, side=side, sl3=sl3:
                         e.activation(qraw[sl3][:, side * 512:side * 512 + 512],
                                      ps_in[:, :], AF.Copy), sem=ac)
                n_sq_done = ac.n
                qk_evac[g] = (n_sq_done, 0)

                def stageA(g=g, slot=slot, sl3=sl3, n_sq_done=n_sq_done):
                    for (side, is_q) in ((0, True), (1, False)):
                        off = side * 512
                        w_pn = [(ac, n_sq_done)]
                        if cst["n_pnread"]:
                            w_pn.append((dv, cst["n_pnread"]))
                        S.op("tensor", lambda e, slot=slot, off=off: e.matmul(
                            pn[:, :], ind_sq_sb[:, :], sqb[slot][:, off:off + 512],
                            start=True, stop=True), sem=pe, waits=w_pn)
                        n_pn = pe.n
                        if side == 1:
                            chain_pn[g] = n_pn
                        w_n = [(pe, n_pn)]
                        if cst["n_n"]:
                            w_n.append((ac, cst["n_n"]))
                        if cst["n_rq"]:
                            w_n.append((dv, cst["n_rq"]))
                        S.op("vector", lambda e: e.tensor_scalar_max(
                            nrm[:, :], pn[:, :], 1e-24), sem=dv, waits=w_n)
                        cst["n_pnread"] = dv.n
                        S.op("scalar", lambda e: e.activation(
                            nrm2[:, :], nrm[:, :], AF.Ln), sem=ac,
                            waits=[(dv, cst["n_pnread"])])
                        w_rq = [(pe, cst["n_pbc"])] if cst["n_pbc"] else []
                        if is_q:
                            S.op("scalar", lambda e: e.activation(
                                nrm[:, :], nrm2[:, :], AF.Exp, scale=-0.5),
                                sem=ac)
                            cst["n_n"] = ac.n
                            S.op("vector", lambda e, side=side: e.tensor_single_scalar(
                                rq_s[side][:, :], nrm[:, :], cols[0:2, 6:7],
                                ALU.mult), sem=dv,
                                waits=w_rq + [(ac, cst["n_n"])])
                        else:
                            S.op("scalar", lambda e, side=side: e.activation(
                                rq_s[side][:, :], nrm2[:, :], AF.Exp, scale=-0.5),
                                sem=ac, waits=w_rq)
                            cst["n_n"] = ac.n
                            S.op("vector", lambda e: e.engine_nop(), sem=dv,
                                 waits=[(ac, cst["n_n"])])
                        rqc[(g, side)] = dv.n
                        cst["n_rq"] = dv.n

                    def stageB(g=g, sl3=sl3):
                        for (side, dst) in ((0, qnf), (1, knf)):
                            off = side * 512
                            w_pbc = [(dv, rqc[(g, side)])]
                            if cst["n_bcs"]:
                                w_pbc.append((ac, cst["n_bcs"]))
                            S.op("tensor", lambda e, side=side: e.matmul(
                                pbc[:, :], ind_bc_sb[:, :], rq_s[side][:, :],
                                start=True, stop=True), sem=pe, waits=w_pbc)
                            n_pbc = pe.n
                            cst["n_pbc"] = n_pbc
                            w_bcs = [(pe, n_pbc)]
                            if cst["last_tt"]:
                                w_bcs.append((dv, cst["last_tt"]))
                            S.op("scalar", lambda e: e.activation(
                                bcs[:, :], pbc[:, :], AF.Copy), sem=ac, waits=w_bcs)
                            cst["n_bcs"] = ac.n
                            S.op("vector", lambda e, dst=dst, g=g, sl3=sl3,
                                 off=off: e.tensor_mul(
                                dst[:, 512 * g:512 * g + 512],
                                qraw[sl3][:, off:off + 512], bcs[:, :]),
                                sem=dv, waits=[(ac, cst["n_bcs"])])
                            cst["last_tt"] = dv.n
                        chain_done[g] = cst["last_tt"]
                    return stageB

                if pendB:
                    pendB.pop(0)()
                if pendA:
                    pendB.append(pendA.pop()())
                pendA.append(stageA)
            if pendB:
                pendB.pop(0)()
            if pendA:
                pendB.append(pendA.pop()())
            while pendB:
                pendB.pop(0)()



def _p4_attn(nc, S, astack, sems, consts, qnf, knf, vres, cc2h):
    dS, pe, ac, dv = (sems[n] for n in ("dS", "pe", "ac", "dv"))
    cc = sems["cc"]
    cols, triu_sb, ones64_sb = consts["cols"], consts["triu"], consts["ones64"]

    pa_sb = [astack.enter_context(nc.sbuf_tensor(f"p4_pa{i}", [128, 1024], BF16))
             for i in range(2)]
    pb_sb = [astack.enter_context(nc.sbuf_tensor(f"p4_pb{i}", [128, 1024], BF16))
             for i in range(2)]
    rs_sb = astack.enter_context(nc.sbuf_tensor("p4_rs", [1, 1024], F32))
    bca_sb = astack.enter_context(nc.sbuf_tensor("p4_bca", [64, 512], F32))
    bcb_sb = astack.enter_context(nc.sbuf_tensor("p4_bcb", [64, 512], F32))
    of_sb = [astack.enter_context(nc.sbuf_tensor(f"p4_of{i}", [128, 512], BF16))
             for i in range(2)]
    orawA = [astack.enter_context(nc.sbuf_tensor(f"p4_oa{i}", [65, 512], F32))
             for i in range(2)]
    orawB = [astack.enter_context(nc.sbuf_tensor(f"p4_ob{i}", [65, 512], F32))
             for i in range(2)]

    with nc.psum_tensor("p4_psA", [128, 1024], F32) as psA, \
         nc.psum_tensor("p4_psB", [128, 1024], F32) as psB, \
         nc.psum_tensor("p4_oA", [65, 512], F32) as oA, \
         nc.psum_tensor("p4_oB", [65, 512], F32) as oB, \
         nc.psum_tensor("p4_nA", [64, 512], F32) as nA, \
         nc.psum_tensor("p4_nB", [64, 512], F32) as nB:

        of_dma = []
        onorm = []
        pending_norm = []
        p4lim = int(os.environ.get("P4LIM", "999"))
        ngroups = 0
        for s in range(NSEQ):
            for qg in range(8):
                if ngroups >= p4lim:
                    continue
                ngroups += 1
                gi = s * 8 + qg
                if gi == 17:
                    S.op("gpsimd", lambda e: e.collective_compute(
                        "AllGather", ALU.bypass,
                        replica_groups=[list(range(NCORE))],
                        ins=[cc2h["a_in"].ap().opt()],
                        outs=[cc2h["a_out"].ap().opt()]),
                        sem=cc, waits=[(sems["of0"], sems["of0"].n),
                                       (sems["of1"], sems["of1"].n)])
                    cc2h["ccA_done"] = cc.n
                qc0 = s * SEQ + qg * 512
                nkt = 4 * qg + 4
                npair = nkt // 2
                exp_done = {}
                exp_done_h = {}
                diag_done = {}
                av_done = {}
                prev = onorm[-1] if onorm else None
                for pair in range(npair):
                    for kk in (2 * pair, 2 * pair + 1):
                        half = (kk % 2) * 512
                        gk = 32 * s + kk
                        w = []
                        if pair >= 1:
                            w.append((ac, exp_done_h[(pair - 1, kk % 2)]))
                        elif prev:
                            w.append((ac, prev["exp_last"]))
                        S.op("tensor", lambda e, gk=gk, half=half, qc0=qc0: e.matmul(
                            psA[:, half:half + 512],
                            knf[0:64, 128 * gk:128 * gk + 128],
                            qnf[0:64, qc0:qc0 + 512],
                            start=True, stop=True, tile_position=(0, 0)),
                            sem=pe, waits=w)
                        S.op("tensor", lambda e, gk=gk, half=half, qc0=qc0: e.matmul(
                            psB[:, half:half + 512],
                            knf[64:128, 128 * gk:128 * gk + 128],
                            qnf[64:128, qc0:qc0 + 512],
                            start=True, stop=True, tile_position=(64, 0)),
                            sem=pe)
                        n_sc = pe.n
                        w_exp = [(pe, n_sc)]
                        if pair >= 2:
                            w_exp.append((pe, av_done[pair - 2]))
                        elif prev:
                            w_exp.append((pe, prev["n_av"]))
                        S.op("scalar", lambda e, pair=pair, half=half: e.activation(
                            pa_sb[pair % 2][:, half:half + 512],
                            psA[:, half:half + 512], AF.Exp,
                            bias=cols[:, 7:8]), sem=ac, waits=w_exp)
                        S.op("scalar", lambda e, pair=pair, half=half: e.activation(
                            pb_sb[pair % 2][:, half:half + 512],
                            psB[:, half:half + 512], AF.Exp,
                            bias=cols[:, 7:8]), sem=ac)
                        exp_done_h[(pair, kk % 2)] = ac.n
                    exp_done[pair] = ac.n
                    for kk in (2 * pair, 2 * pair + 1):
                        if kk >= 4 * qg:
                            i0 = kk - 4 * qg
                            off = (kk % 2) * 512 + i0 * 128
                            for psb in (pa_sb, pb_sb):
                                S.op("vector", lambda e, psb=psb, pair=pair, off=off: e.tensor_mul(
                                    psb[pair % 2][:, off:off + 128],
                                    psb[pair % 2][:, off:off + 128],
                                    triu_sb[:, :]),
                                    sem=dv, waits=[(ac, exp_done[pair])])
                            diag_done[kk] = dv.n
                    for kk in (2 * pair, 2 * pair + 1):
                        gk2 = 32 * s + kk
                        i0 = max(0, kk - 4 * qg)
                        off = (kk % 2) * 512 + i0 * 128
                        nq = 512 - i0 * 128
                        wav = [(ac, exp_done_h[(pair, kk % 2)])]
                        if kk in diag_done:
                            wav.append((dv, diag_done[kk]))
                        if kk == 0 and prev:
                            wav.append((dv, prev["n_evac"]))
                        S.op("tensor", lambda e, gk2=gk2, off=off, nq=nq, i0=i0, kk=kk, pair=pair: e.matmul(
                            oA[:, i0 * 128: i0 * 128 + nq],
                            vres[:, 130 * gk2:130 * gk2 + 65],
                            pa_sb[pair % 2][:, off:off + nq],
                            start=(kk == 0), stop=(kk == nkt - 1)),
                            sem=pe, waits=wav)
                        S.op("tensor", lambda e, gk2=gk2, off=off, nq=nq, i0=i0, kk=kk, pair=pair: e.matmul(
                            oB[:, i0 * 128: i0 * 128 + nq],
                            vres[:, 130 * gk2 + 65:130 * gk2 + 130],
                            pb_sb[pair % 2][:, off:off + nq],
                            start=(kk == 0), stop=(kk == nkt - 1)),
                            sem=pe)
                    av_done[pair] = pe.n
                    if pair == 1 and pending_norm:
                        pending_norm.pop()()
                    S.maybe_bar(sems["bar"], 2)
                    if _STAGE == "dbg_p0" and s == 0 and qg == 0 and pair == 0:
                        S.wait("gpsimd", dv, dv.n)
                        S.wait("gpsimd", ac, ac.n)
                        S.dma("gpsimd", lambda e: e.dma_start(
                            out=consts["out"].ap()[0:128, :],
                            in_=pa_sb[0][:, :]), sems["dG"])
                        S.dma("gpsimd", lambda e: e.dma_start(
                            out=consts["out"].ap()[128:256, :],
                            in_=pb_sb[0][:, :]), sems["dG"])
                        S.wait("gpsimd", sems["dG"], sems["dG"].n)
                        S.flush()
                        raise StopIteration
                n_av = pe.n

                slot = gi % 2
                w_ev = [(pe, n_av)]
                if len(onorm) >= 2:
                    w_ev.append((dv, onorm[-2]["n_of"]))
                S.op("vector", lambda e, slot=slot: e.tensor_copy(
                    orawA[slot][:, :], oA[:, :]), sem=dv, waits=w_ev)
                S.op("vector", lambda e, slot=slot: e.tensor_copy(
                    orawB[slot][:, :], oB[:, :]), sem=dv)
                n_evac = dv.n
                entry = dict(n_evac=n_evac, n_av=n_av,
                             exp_last=exp_done[npair - 1])
                onorm.append(entry)

                def emit_norm(entry=entry, slot=slot, pidx=len(onorm) - 2,
                              s=s, qg=qg):
                    pv = onorm[pidx] if pidx >= 0 else None
                    w_rs = []
                    if pv:
                        w_rs.append((pe, pv["n_bc"]))
                    S.op("vector", lambda e, slot=slot: e.reciprocal(
                        rs_sb[0:1, 0:512], orawA[slot][64:65, :]), sem=dv,
                        waits=w_rs)
                    S.op("vector", lambda e, slot=slot: e.reciprocal(
                        rs_sb[0:1, 512:1024], orawB[slot][64:65, :]), sem=dv)
                    n_rs = dv.n
                    w_bc = [(dv, n_rs)]
                    if pv:
                        w_bc.append((ac, pv["n_bcs"]))
                    S.op("tensor", lambda e: e.matmul(
                        nA[:, :], ones64_sb[:, :], rs_sb[0:1, 0:512], start=True,
                        stop=True), sem=pe, waits=w_bc)
                    S.op("tensor", lambda e: e.matmul(
                        nB[:, :], ones64_sb[:, :], rs_sb[0:1, 512:1024], start=True,
                        stop=True), sem=pe)
                    n_bc = pe.n
                    w_bcs = [(pe, n_bc)]
                    if pv:
                        w_bcs.append((dv, pv["n_of"]))
                    S.op("scalar", lambda e: e.activation(bca_sb[:, :], nA[:, :],
                                                          AF.Copy),
                         sem=ac, waits=w_bcs)
                    S.op("scalar", lambda e: e.activation(bcb_sb[:, :], nB[:, :],
                                                          AF.Copy), sem=ac)
                    n_bcs = ac.n
                    wof = [(ac, n_bcs)]
                    if len(of_dma) >= 2:
                        wof.append(of_dma[-2])
                    S.op("vector", lambda e, slot=slot: e.tensor_mul(
                        of_sb[slot][0:64, :], orawA[slot][0:64, :], bca_sb[:, :]),
                        sem=dv, waits=wof)
                    S.op("vector", lambda e, slot=slot: e.tensor_mul(
                        of_sb[slot][64:128, :], orawB[slot][0:64, :], bcb_sb[:, :]),
                        sem=dv)
                    n_of = dv.n
                    entry["n_of"] = n_of
                    entry["n_bcs"] = n_bcs
                    entry["n_bc"] = n_bc

                    tg0 = 32 * s + 4 * qg
                    recv0 = tg0 % 8
                    slot_idx = tg0 // 8
                    cct = cc2h["a_in"] if slot_idx < 8 else cc2h["b_in"]
                    si = slot_idx % 8
                    ofs = sems["of0"] if slot == 0 else sems["of1"]
                    S.dma("sync", lambda e, recv0=recv0, si=si, cct=cct,
                          slot=slot: e.dma_start(
                        out=cct.ap()[recv0:recv0 + 4, :,
                                     si * 128:(si + 1) * 128]
                        .rearrange("i p q -> p i q"),
                        in_=of_sb[slot].ap().rearrange("p (i q) -> p i q", q=128)),
                        ofs, waits=[(dv, n_of)])
                    of_dma.append((ofs, ofs.n))

                pending_norm.append(emit_norm)
                S.maybe_bar(sems["bar"], 1)
        if pending_norm:
            pending_norm.pop()()



def _p5_oproj(nc, S, sems, consts, cc2a_out, ccA_done, cc2b_out, ccB_done,
              wout_in):
    dS, dG, pe, ac, dv, cc = (sems[n] for n in ("dS", "dG", "pe", "ac", "dv", "cc"))
    resid, rank_sb = consts["resid"], consts["rank"]

    with ExitStack() as pst:
        ofall = pst.enter_context(nc.sbuf_tensor("p5_ofall", [128, NCORE * 2048], BF16))
        wout_bf = pst.enter_context(nc.sbuf_tensor("p5_wout", [128, 8 * D], BF16))

        S.dma("gpsimd", lambda e: e.dma_start(
            out=wout_bf.ap().rearrange("p (k n) -> p k n", n=D),
            in_=wout_in.ap().rearrange("(k p) n -> p k n", p=128)), dG)
        n_w = dG.n

        rk_holder = {}

        def load_of(e, j, src, hoff):
            if "rk" not in rk_holder:
                rk_holder["rk"] = nc.values_load(
                    rank_sb[:, :], engines=[SP], min_val=0, max_val=7,
                    skip_runtime_bounds_check=True)
            rk = rk_holder["rk"]
            return e.dma_start(
                out=ofall.ap()[:, j * 2048 + hoff:j * 2048 + hoff + 1024],
                in_=src.ap()[j, DynSlice(rk, 1)].rearrange("one p q -> p (one q)"))

        n_ofh = {}
        ev = []

        def emit_half(hi, src, ccdone, py):
            for j in range(NCORE):
                S.dma("sync", lambda e, j=j, src=src: load_of(
                    e, j, src, hi * 1024), dS,
                    waits=([(cc, ccdone)] if j == 0 else []))
            n_ofh[hi] = dS.n
            for t in range(8 * hi, 8 * hi + 8):
                ps = py[t % 2]
                w = [(dS, n_ofh[hi]), (dG, n_w)]
                if len(ev) >= 2:
                    w.append((dv, ev[-2]))
                for j in range(NCORE):
                    for half in range(2):
                        S.op("tensor", lambda e, t=t, j=j, half=half, ps=ps: e.matmul(
                            ps[:, half * 512:(half + 1) * 512],
                            ofall[:, j * 2048 + t * 128: j * 2048 + t * 128 + 128],
                            wout_bf[:, j * D + half * 512: j * D + half * 512 + 512],
                            start=(j == 0), stop=(j == NCORE - 1)),
                            sem=(pe if (j == NCORE - 1 and half == 1) else None),
                            waits=(w if (j == 0 and half == 0) else []))
                n_mm = pe.n
                for half in range(2):
                    S.op("vector", lambda e, t=t, ps=ps, half=half: e.tensor_add(
                        resid[:, t * D + 512 * half:t * D + 512 * half + 512],
                        resid[:, t * D + 512 * half:t * D + 512 * half + 512],
                        ps[:, 512 * half:512 * half + 512]),
                        sem=dv, waits=[(pe, n_mm)])
                ev.append(dv.n)

        with nc.psum_tensor("p5_py0", [128, 1024], F32) as py0, \
             nc.psum_tensor("p5_py1", [128, 1024], F32) as py1:
            py = [py0, py1]
            emit_half(0, cc2a_out, ccA_done, py)
            emit_half(1, cc2b_out, ccB_done, py)


def _store_out(nc, S, sems, resid, out):
    dS = sems["dS"]
    S.dma("sync", lambda e: e.dma_start(
        out=out.ap().rearrange("(t p) c -> p t c", p=128),
        in_=resid.ap().rearrange("p (t c) -> p t c", c=D)), dS)
    n_done = dS.n
    S.wait("gpsimd", dS, n_done)



def kernel(L0_attn_x, L0_attn_ln_gamma, L0_attn_ln_beta, L0_attn_w_qkv,
           L0_attn_w_out, L0_attn_qkscale, L0_ffn_w_gate, L0_ffn_w_up,
           L0_ffn_w_down):
    x = np.ascontiguousarray(np.asarray(L0_attn_x, dtype=np.float32)).reshape(128, 128, D)
    gamma = np.asarray(L0_attn_ln_gamma, dtype=np.float32)
    beta = np.asarray(L0_attn_ln_beta, dtype=np.float32)
    wqkv = np.asarray(L0_attn_w_qkv, dtype=np.float32)
    w_out = np.ascontiguousarray(np.asarray(L0_attn_w_out, dtype=np.float32))
    qks = np.asarray(L0_attn_qkscale, dtype=np.float32).reshape(1, 1)
    wgT = np.ascontiguousarray(np.asarray(L0_ffn_w_gate, dtype=np.float32).T)
    wuT = np.ascontiguousarray(np.asarray(L0_ffn_w_up, dtype=np.float32).T)
    wdT = np.ascontiguousarray(np.asarray(L0_ffn_w_down, dtype=np.float32).T)

    cos, sin_eff, partner = _rope_consts()
    g_t = np.ascontiguousarray(np.broadcast_to(gamma, (128, D)))
    b_t = np.ascontiguousarray(np.broadcast_to(beta, (128, D)))

    in_maps = []
    for c in range(NCORE):
        qcols = np.arange(128 * c, 128 * c + 128)
        in_maps.append(dict(
            x=np.ascontiguousarray(x[c::8].reshape(2048, D)),
            g_t=g_t, b_t=b_t,
            wq_m=np.ascontiguousarray(wqkv[:, qcols]),
            wq_r=np.ascontiguousarray(wqkv[:, partner[qcols]]),
            wk_m=np.ascontiguousarray(wqkv[:, D + qcols]),
            wk_r=np.ascontiguousarray(wqkv[:, D + partner[qcols]]),
            wv_m=np.ascontiguousarray(wqkv[:, 2 * D + qcols]),
            cos_q=np.ascontiguousarray(np.broadcast_to(cos[qcols], (128, 128))),
            sin_q=np.ascontiguousarray(np.broadcast_to(sin_eff[qcols], (128, 128))),
            cos_k=np.ascontiguousarray(np.broadcast_to(cos[qcols], (128, 128))),
            sin_k=np.ascontiguousarray(np.broadcast_to(sin_eff[qcols], (128, 128))),
            w_out=w_out, wgT=wgT, wuT=wuT, wdT=wdT, qks=qks,
            rank=np.array([[c]], dtype=np.int32),
        ))

    nc = build_program()
    res = run_bass_kernel_spmd(nc, in_maps, core_ids=list(range(NCORE)))
    y = np.empty((128, 128, D), dtype=np.float32)
    for c in range(NCORE):
        y[c::8] = np.asarray(res.results[c]["out"]).reshape(NT, 128, D)
    return y.reshape(NSEQ, SEQ, D)


# version 13
# speedup vs baseline: 1.0510x; 1.0203x over previous
import os
from contextlib import ExitStack

import numpy as np
import ml_dtypes

import concourse.bass as bass
import concourse.mybir as mybir
from concourse.bass_types import DynSlice
from concourse.bass_utils import run_bass_kernel_spmd

F32 = mybir.dt.float32
BF16 = mybir.dt.bfloat16
I8 = mybir.dt.int8
I32 = mybir.dt.int32
AF = mybir.ActivationFunctionType
ALU = mybir.AluOpType
AX = mybir.AxisListType.X
SP = mybir.EngineType.SP

BF = ml_dtypes.bfloat16

_STAGE = "full"
_PARANOID = 0

D = 1024
INNER = 4096
SEQ = 4096
NSEQ = 4
NCORE = 8
NT = 16
NB = 4
TPB = 4
LN_EPS = 1e-5

ENGS = ("sync", "gpsimd", "tensor", "vector", "scalar")


def _rope_consts():
    inv = np.float32(10000.0) ** (-(np.arange(0, D, 2, dtype=np.float32) / np.float32(D)))
    emb = np.repeat(inv, 2)
    ang = np.float32(SEQ) * emb
    sin = np.sin(ang).astype(np.float32)
    cos = np.cos(ang).astype(np.float32)
    idx = np.arange(D)
    partner = np.where(idx < D // 2, 2 * idx + 1, 2 * idx - D)
    sin_eff = np.where(idx < D // 2, -sin, sin).astype(np.float32)
    return cos, sin_eff, partner


class Sem:
    def __init__(self, nc, name):
        self.h = nc.alloc_semaphore(name)
        self.n = 0


class Sched:
    def __init__(self, nc, block, sems=None):
        self.nc = nc
        self.block = block
        self.q = {e: [] for e in ENGS}
        self.sems = sems

    def op(self, eng, fn, sem=None, k=1, waits=(), chain=True):
        waits = list(waits)
        if chain and self.sems is not None and eng in ("vector", "scalar"):
            own = self.sems["dv" if eng == "vector" else "ac"]
            if own.n > 0 and (sem is not own or True):
                waits.append((own, own.n))
            if sem is None:
                sem = own
        waits = tuple((s.h, int(c)) for (s, c) in waits if c)
        if sem is not None:
            sem.n += k
        h = sem.h if sem is not None else None

        def run(e, fn=fn, waits=waits, h=h, kk=k):
            for (sh, c) in waits:
                e.wait_ge(sh, c)
            inst = fn(e)
            if h is not None:
                inst.then_inc(h, kk)

        self.q[eng].append(run)
        return sem.n if sem is not None else None

    def wait(self, eng, sem, n):
        self.q[eng].append(lambda e, sh=sem.h, n=int(n): e.wait_ge(sh, n))

    def dma(self, eng, fn, sem, waits=()):
        return self.op(eng, fn, sem=sem, k=16, waits=waits)

    def flush(self):
        for eng in ENGS:
            lst = self.q[eng]
            if not lst:
                continue

            def body(e, lst=lst):
                for f in lst:
                    f(e)

            getattr(self.block, eng)(body)
        self.q = {e: [] for e in ENGS}

    def barrier(self, bar):
        self.op("sync", lambda e: e.nop(), sem=bar, k=1)
        self.op("gpsimd", lambda e: e.engine_nop(), sem=bar, k=1)
        self.op("vector", lambda e: e.engine_nop(), sem=bar, k=1)
        self.op("scalar", lambda e: e.activation(
            self.scratch11[:, :], self.scratch11[:, :], AF.Identity), sem=bar, k=1)
        self.op("tensor", lambda e: e.nop(), sem=bar, k=1)
        t = bar.n
        for eng in ENGS:
            self.wait(eng, bar, t)

    def maybe_bar(self, bar, level):
        if _PARANOID >= level:
            self.barrier(bar)



def build_program():
    nc = bass.Bass("TRN2", target_bir_lowering=False, debug=False)

    x_in = nc.dram_tensor("x", [2048, D], F32, kind="ExternalInput")
    g_in = nc.dram_tensor("g_t", [128, D], F32, kind="ExternalInput")
    b_in = nc.dram_tensor("b_t", [128, D], F32, kind="ExternalInput")
    wqm_in = nc.dram_tensor("wq_m", [D, 128], F32, kind="ExternalInput")
    wqr_in = nc.dram_tensor("wq_r", [D, 128], F32, kind="ExternalInput")
    wkm_in = nc.dram_tensor("wk_m", [D, 128], F32, kind="ExternalInput")
    wkr_in = nc.dram_tensor("wk_r", [D, 128], F32, kind="ExternalInput")
    wvm_in = nc.dram_tensor("wv_m", [D, 128], F32, kind="ExternalInput")
    cosq_in = nc.dram_tensor("cos_q", [128, 128], F32, kind="ExternalInput")
    sinq_in = nc.dram_tensor("sin_q", [128, 128], F32, kind="ExternalInput")
    cosk_in = nc.dram_tensor("cos_k", [128, 128], F32, kind="ExternalInput")
    sink_in = nc.dram_tensor("sin_k", [128, 128], F32, kind="ExternalInput")
    wout_in = nc.dram_tensor("w_out", [D, D], F32, kind="ExternalInput")
    wgT_in = nc.dram_tensor("wgT", [D, INNER], F32, kind="ExternalInput")
    wuT_in = nc.dram_tensor("wuT", [D, INNER], F32, kind="ExternalInput")
    wdT_in = nc.dram_tensor("wdT", [INNER, D], F32, kind="ExternalInput")
    qks_in = nc.dram_tensor("qks", [1, 1], F32, kind="ExternalInput")
    rank_in = nc.dram_tensor("rank", [1, 1], I32, kind="ExternalInput")
    out = nc.dram_tensor("out", [2048, D], F32, kind="ExternalOutput")

    wgq_d = nc.dram_tensor("wgq_d", [8, 128, 8, 512], BF16)
    wuq_d = nc.dram_tensor("wuq_d", [8, 128, 8, 512], BF16)
    wdq_d = nc.dram_tensor("wdq_d", [INNER, D], BF16)
    cc1a_in = nc.dram_tensor("cc1a_in", [8, 128, 1024], BF16)
    cc1b_in = nc.dram_tensor("cc1b_in", [8, 128, 1024], BF16)
    cc1a_out = nc.dram_tensor("cc1a_out", [NCORE, 8, 128, 1024], BF16)
    cc1b_out = nc.dram_tensor("cc1b_out", [NCORE, 8, 128, 1024], BF16)
    cc2a_in = nc.dram_tensor("cc2a_in", [NCORE, 128, 8 * 128], BF16)
    cc2b_in = nc.dram_tensor("cc2b_in", [NCORE, 128, 8 * 128], BF16)
    cc2a_out = nc.dram_tensor("cc2a_out", [NCORE, NCORE, 128, 8 * 128], BF16)
    cc2b_out = nc.dram_tensor("cc2b_out", [NCORE, NCORE, 128, 8 * 128], BF16)

    I128 = nc.inline_tensor(np.eye(128, dtype=BF), name="I128")
    triu_np = (np.arange(128)[:, None] <= np.arange(128)[None, :]).astype(BF)
    TRIU = nc.inline_tensor(triu_np, name="TRIU")
    ind_sq_np = np.zeros((128, 2), dtype=BF)
    ind_sq_np[0:64, 0] = 1
    ind_sq_np[64:128, 1] = 1
    IND_SQ = nc.inline_tensor(ind_sq_np, name="IND_SQ")
    ind_bc_np = np.zeros((2, 128), dtype=BF)
    ind_bc_np[0, 0:64] = 1
    ind_bc_np[1, 64:128] = 1
    IND_BC = nc.inline_tensor(ind_bc_np, name="IND_BC")
    ONES_1x64 = nc.inline_tensor(np.ones((1, 64), dtype=np.float32), name="ONES_1x64")
    ONES_R_F = nc.inline_tensor(np.ones((1, 128), dtype=np.float32), name="ONES_R_F")
    ONES_C_F = nc.inline_tensor(np.ones((128, 1), dtype=np.float32), name="ONES_C_F")

    semnames = ("dS", "dG", "pe", "ac", "dv", "cc", "bar",
                "wq0", "wq1", "wq2", "wq3", "ts0", "ts1",
                "wg0", "wg1", "wu0", "wu1",
                "wd0", "wd1", "h2f0", "h2f1", "h2g0", "h2g1", "of0", "of1")
    sems = {n: Sem(nc, n) for n in semnames}
    dS, dG, pe, ac, dv, cc, bar = (sems[n] for n in
                                   ("dS", "dG", "pe", "ac", "dv", "cc", "bar"))

    with ExitStack() as stack:
        block = stack.enter_context(nc.Block())
        S = Sched(nc, block, sems=sems)
        S.scratch11 = stack.enter_context(nc.sbuf_tensor("bar_scr", [1, 1], F32))

        def sb(name, shape, dt=F32):
            return stack.enter_context(nc.sbuf_tensor(name, shape, dt))

        resid = sb("resid", [128, NT * D])
        stats = sb("stats", [128, 160])
        cols = sb("cols", [128, 16])
        sc11 = sb("sc11", [1, 8])
        i128_sb = sb("i128_sb", [128, 128], BF16)
        triu_sb = sb("triu_sb", [128, 128], BF16)
        ind_sq_sb = sb("ind_sq_sb", [128, 2], BF16)
        ind_bc_sb = sb("ind_bc_sb", [2, 128], BF16)
        ones64_sb = sb("ones64_sb", [1, 64], F32)
        onesrf_sb = sb("onesrf_sb", [1, 128], F32)
        onescf_sb = sb("onescf_sb", [128, 1], F32)
        rank_sb = sb("rank_sb", [1, 1], I32)
        qks_sb = sb("qks_sb", [1, 1], F32)

        consts = dict(resid=resid, stats=stats, cols=cols, sc11=sc11,
                      i128=i128_sb, triu=triu_sb, ind_sq=ind_sq_sb,
                      ind_bc=ind_bc_sb, ones64=ones64_sb, onesrf=onesrf_sb,
                      onescf=onescf_sb, rank=rank_sb, qks=qks_sb, out=out)

        S.op("vector", lambda e: e.memset(cols[:, 8:9], LN_EPS), sem=sems["dv"])
        for t in range(NT):
            S.dma("sync", lambda e, t=t: e.dma_start(
                out=resid[:, t * D:(t + 1) * D],
                in_=x_in.ap().rearrange("(t p) c -> t p c", p=128)[t]), dS)
        for dst, src in ((i128_sb, I128), (triu_sb, TRIU), (ind_sq_sb, IND_SQ),
                         (ind_bc_sb, IND_BC), (ones64_sb, ONES_1x64),
                         (onesrf_sb, ONES_R_F), (onescf_sb, ONES_C_F),
                         (rank_sb, rank_in), (qks_sb, qks_in)):
            S.dma("sync", lambda e, d=dst, s=src: e.dma_start(out=d[:, :], in_=s[:, :]), dS)
        n_x = dS.n
        n_consts = dS.n

        with nc.psum_tensor("ps_misc", [128, 8], F32) as ps_misc:
            def serial0(eng, fn, sem, extra=()):
                S.op(eng, fn, sem=sem,
                     waits=list(extra) + [(pe, pe.n), (ac, ac.n), (dv, dv.n)])

            serial0("scalar", lambda e: e.activation(sc11[:, 0:1], qks_sb[:, :],
                                                     AF.Abs), ac,
                    extra=[(dS, n_consts)])
            serial0("tensor", lambda e: e.matmul(ps_misc[:, 0:1], onesrf_sb[:, :],
                                                 qks_sb[:, :], start=True, stop=True),
                    pe, extra=[(dS, n_consts)])
            serial0("tensor", lambda e: e.matmul(ps_misc[:, 1:2], onesrf_sb[:, :],
                                                 sc11[:, 0:1], start=True, stop=True),
                    pe)
            serial0("scalar", lambda e: e.activation(cols[:, 6:7], ps_misc[:, 0:1],
                                                     AF.Copy), ac)
            serial0("scalar", lambda e: e.activation(cols[:, 7:8], ps_misc[:, 1:2],
                                                     AF.Copy, scale=-1.0), ac)

            mats = ((wgT_in, wgq_d, 8, INNER, 0, 3, 1.0 / 127.0),
                    (wuT_in, wuq_d, 8, INNER, 1, 4, 1.0 / 127.0),
                    (wdT_in, wdq_d, 32, D, 2, 5, 0.5 / 127.0))
            with ExitStack() as wst:
                s0dump = wst.enter_context(
                    nc.sbuf_tensor("s0_dump", [128, D], F32))
                B = _WQBufs(wst, nc, sems)
                absc = [wst.enter_context(
                    nc.sbuf_tensor(f"s0_ac{m}", [128, mats[m][2]], F32))
                    for m in range(3)]
                csum = [wst.enter_context(
                    nc.sbuf_tensor(f"s0_cs{m}", [128, 1], F32)) for m in range(3)]

                n_sacc_dv, n_sacc_ac = _ln_stats_acc(nc, S, sems, consts, s0dump,
                                                     (dS, n_x))
                n_sc = []
                for (w_in, wq_d, nck, wid, cs_, ci_, im_), am, cm in zip(
                        mats, absc, csum):
                    _wq_pass1(nc, S, sems, B, w_in, nck, wid, am)
                    n_sc.append(_wq_chain(nc, S, sems, am, cm, nck, wid, cols,
                                          cs_, ci_, im_, sc11, onesrf_sb,
                                          onescf_sb, ps_misc))
                n_stats_f1 = _ln_stats_tail(nc, S, sems, consts, n_sacc_ac)
                for (w_in, wq_d, nck, wid, cs_, ci_, im_), ns in zip(mats, n_sc):
                    _wq_pass2(nc, S, sems, B, w_in, wq_d, nck, wid, cols, cs_, ns)
                S.wait("sync", sems["ts0"], sems["ts0"].n)
                S.wait("sync", sems["ts1"], sems["ts1"].n)
                S.barrier(bar)
                S.flush()

        if _STAGE == "dbg_wq":
            S.dma("gpsimd", lambda e: e.dma_start(
                out=out.ap()[0:512, :].rearrange("(r s) c -> r s c", s=4),
                in_=wgq_d.ap()[0:128, :].rearrange("r (s c) -> r s c", c=1024)),
                dG)
            S.dma("sync", lambda e: e.dma_start(
                out=out.ap()[512:640, 0:16], in_=cols[:, :]), dS)
            S.dma("sync", lambda e: e.dma_start(
                out=out.ap()[640:641, 0:8], in_=sc11[:, :]), dS)
            S.wait("gpsimd", dG, dG.n)
            S.wait("gpsimd", dS, dS.n)
            S.flush()
            return nc
        if _STAGE in ("dbg_stats", "dbg_red"):
            with nc.sbuf_tensor("dbg_dump", [128, D], F32) as ddump, \
                 nc.sbuf_tensor("dbg_snap", [128, 112], F32) as dsnap:
                consts["snap"] = dsnap
                _ln_stats(nc, S, sems, consts, ddump, (dS, n_x),
                          chain=(_STAGE == "dbg_stats"))
                S.barrier(bar)
                S.dma("sync", lambda e: e.dma_start(
                    out=out.ap()[0:128, 0:160], in_=consts["stats"][:, :]), dS)
                S.dma("sync", lambda e: e.dma_start(
                    out=out.ap()[128:256, 0:112], in_=dsnap[:, :]), dS)
                S.wait("gpsimd", dS, dS.n)
                S.flush()
                consts.pop("snap")
            return nc

        if _STAGE == "dbg_xq":
            with nc.sbuf_tensor("dbg_dump", [128, D], F32) as ddump, \
                 nc.sbuf_tensor("dbg_x8", [128, D], I8) as x8, \
                 nc.sbuf_tensor("dbg_xb", [128, D], BF16) as xb:
                n_st = _ln_stats(nc, S, sems, consts, ddump, (dS, n_x))
                stats = consts["stats"]
                for t in range(NT):
                    S.op("vector", lambda e, t=t: e.tensor_scalar(
                        x8[:, :], resid[:, t * D:(t + 1) * D],
                        stats[:, 0 + t:1 + t], stats[:, 16 + t:17 + t],
                        ALU.subtract, ALU.mult),
                        sem=sems["dv"], waits=[(sems["dv"], n_st)] if t == 0 else [])
                    S.op("vector", lambda e: e.tensor_copy(xb[:, :], x8[:, :]),
                         sem=sems["dv"])
                    n_q = sems["dv"].n
                    S.dma("gpsimd", lambda e, t=t: e.dma_start(
                        out=out.ap().rearrange("(t p) c -> t p c", p=128)[t],
                        in_=xb[:, :]), dG, waits=[(sems["dv"], n_q)])
                    S.wait("vector", dG, dG.n)
                S.wait("gpsimd", dG, dG.n)
                S.flush()
            return nc

        with ExitStack() as p2st:
            p2_accs, p2_emit, p2_state = _p2_setup(nc, S, sems, consts,
                                                    p2st, g_in, b_in,
                                                    cc1a_in, cc1b_in,
                                                    cc1a_out, cc1b_out)

            def post_block(b, ffnctx):
                p2_accs(b, ffnctx)
                if b == 1:
                    p2_emit(0, ffnctx)
                elif b == 3:
                    p2_emit(1, ffnctx)

            _ffn_phase(nc, S, sems, consts, wgq_d, wuq_d, wdq_d,
                       wait_x=(dS, n_x), tag="f1", stats_n=n_stats_f1,
                       post_block=post_block)
        ccA_done, ccB_done = p2_state["cc_done"]
        if _STAGE == "dbg_raw":
            S.flush()
            return nc
        S.barrier(bar)
        S.flush()

        if _STAGE == "ffn1":
            _store_out(nc, S, sems, resid, out)
            S.flush()
            return nc

        with ExitStack() as astack:
            def asb(name, shape, dt=F32):
                return astack.enter_context(nc.sbuf_tensor(name, shape, dt))

            qnf = asb("qnf", [128, NSEQ * SEQ], BF16)
            knf = asb("knf", [128, NSEQ * SEQ], BF16)
            vres = asb("vres", [128, 128 * 130], BF16)

            _p3_qkv(nc, S, astack, sems, consts,
                    (wqm_in, wqr_in, wkm_in, wkr_in, wvm_in),
                    (cosq_in, sinq_in, cosk_in, sink_in),
                    cc1a_out, ccA_done, cc1b_out, ccB_done, qnf, knf, vres)
            if _STAGE in ("dbg_wro", "dbg_h2g"):
                S.flush()
                return nc
            S.barrier(bar)
            S.flush()

            if _STAGE in ("dbg_qnf", "dbg_knf", "dbg_vres"):
                src_t = {"dbg_qnf": qnf, "dbg_knf": knf, "dbg_vres": vres}[_STAGE]
                ncol = min(src_t.shape[1], 16384)
                S.dma("gpsimd", lambda e: e.dma_start(
                    out=out.ap().rearrange("(t p) c -> p t c", p=128)
                    [:, 0:ncol // D, :],
                    in_=src_t.ap()[:, 0:(ncol // D) * D]
                    .rearrange("p (t c) -> p t c", c=D)), sems["dG"])
                S.wait("gpsimd", sems["dG"], sems["dG"].n)
                S.flush()
                return nc

            cc2h = dict(a_in=cc2a_in, b_in=cc2b_in, a_out=cc2a_out,
                        ccA_done=None)
            if _STAGE == "dbg_p0":
                try:
                    _p4_attn(nc, S, astack, sems, consts, qnf, knf, vres, cc2h)
                except StopIteration:
                    pass
                return nc
            _p4_attn(nc, S, astack, sems, consts, qnf, knf, vres, cc2h)
            if _STAGE == "dbg_of":
                S.wait("gpsimd", sems["of0"], sems["of0"].n)
                S.wait("gpsimd", sems["of1"], sems["of1"].n)
                for r in range(NCORE):
                    for sl in range(2):
                        S.dma("gpsimd", lambda e, r=r, sl=sl: e.dma_start(
                            out=out.ap().rearrange("(a p) c -> a p c", p=128)
                            [r * 2 + sl],
                            in_=(cc2a_in if sl == 0 else cc2b_in).ap()[r]),
                            sems["dG"])
                S.wait("gpsimd", sems["dG"], sems["dG"].n)
                S.flush()
                return nc
            S.op("gpsimd", lambda e: e.collective_compute(
                "AllGather", ALU.bypass, replica_groups=[list(range(NCORE))],
                ins=[cc2b_in.ap().opt()], outs=[cc2b_out.ap().opt()]),
                sem=cc, waits=[(sems["of0"], sems["of0"].n),
                               (sems["of1"], sems["of1"].n)])
            ccB_done = cc.n
            S.flush()

        _p5_oproj(nc, S, sems, consts, cc2a_out, cc2h["ccA_done"],
                  cc2b_out, ccB_done, wout_in)
        S.barrier(bar)
        S.flush()

        if _STAGE == "attn":
            _store_out(nc, S, sems, resid, out)
            S.flush()
            return nc

        _ffn_phase(nc, S, sems, consts, wgq_d, wuq_d, wdq_d,
                   wait_x=None, tag="f2")
        S.barrier(bar)
        S.flush()

        _store_out(nc, S, sems, resid, out)
        S.flush()

    return nc



class _WQBufs:

    def __init__(self, st, nc, sems):
        self.wbuf = [st.enter_context(nc.sbuf_tensor(f"s0wb{i}", [128, INNER], F32))
                     for i in range(4)]
        self.t8 = st.enter_context(nc.sbuf_tensor("s0t8", [128, INNER], I8))
        self.tern = [st.enter_context(nc.sbuf_tensor(f"s0tn{i}", [128, INNER], BF16))
                     for i in range(2)]
        self.q = [("sync", sems["wq0"]), ("gpsimd", sems["wq1"]),
                  ("sync", sems["wq2"]), ("gpsimd", sems["wq3"])]
        self.sq = [("sync", sems["ts0"]), ("gpsimd", sems["ts1"])]
        self.last_read = [None, None, None, None]
        self.last_store = [None, None]


def _wq_pass1(nc, S, sems, B, w_in, nchunk, width, abscols):
    dv = sems["dv"]
    wv = w_in.ap().rearrange("(k p) w -> k p w", p=128)
    for k in range(nchunk):
        s = k % 4
        eng, sem = B.q[s]
        w = [B.last_read[s]] if B.last_read[s] else []
        S.dma(eng, lambda e, k=k, s=s: e.dma_start(
            out=B.wbuf[s][:, 0:width], in_=wv[k]), sem, waits=w)
        n_load = sem.n
        S.op("vector", lambda e, k=k, s=s: e.reduce_sum(
            abscols[:, k:k + 1], B.wbuf[s][:, 0:width], AX,
            apply_absolute_value=True), sem=dv, waits=[(sem, n_load)])
        B.last_read[s] = (dv, dv.n)


def _wq_chain(nc, S, sems, abscols, colsum, nchunk, width, cols, col_s, col_inv,
              inv_mult, sc11, onesrf, onescf, ps_misc):
    pe, ac, dv = sems["pe"], sems["ac"], sems["dv"]

    def serial(eng, fn, sem):
        S.op(eng, fn, sem=sem,
             waits=[(pe, pe.n), (ac, ac.n), (dv, dv.n)])

    serial("vector", lambda e: e.reduce_sum(colsum[:, :], abscols[:, 0:nchunk],
                                            AX), dv)
    serial("tensor", lambda e: e.matmul(ps_misc[0:1, 2:3], colsum[:, :],
                                        onescf[:, :], start=True, stop=True), pe)
    serial("scalar", lambda e: e.activation(sc11[:, 1:2], ps_misc[0:1, 2:3],
                                            AF.Copy,
                                            scale=1.0 / (128 * nchunk * width)),
           ac)
    serial("vector", lambda e: e.tensor_scalar_max(sc11[:, 2:3], sc11[:, 1:2],
                                                   1e-5), dv)
    serial("vector", lambda e: e.reciprocal(sc11[:, 3:4], sc11[:, 2:3]), dv)
    serial("vector", lambda e: e.tensor_single_scalar(sc11[:, 4:5], sc11[:, 2:3],
                                                      inv_mult, ALU.mult), dv)
    serial("tensor", lambda e: e.matmul(ps_misc[:, 3:4], onesrf[:, :],
                                        sc11[:, 3:4], start=True, stop=True), pe)
    serial("tensor", lambda e: e.matmul(ps_misc[:, 4:5], onesrf[:, :],
                                        sc11[:, 4:5], start=True, stop=True), pe)
    serial("scalar", lambda e: e.activation(cols[:, col_s:col_s + 1],
                                            ps_misc[:, 3:4], AF.Copy), ac)
    serial("scalar", lambda e: e.activation(cols[:, col_inv:col_inv + 1],
                                            ps_misc[:, 4:5], AF.Copy), ac)
    return ac.n


def _wq_pass2(nc, S, sems, B, w_in, wq_d, nchunk, width, cols, col_s, n_scale):
    ac, dv = sems["ac"], sems["dv"]
    wv = w_in.ap().rearrange("(k p) w -> k p w", p=128)
    if len(wq_d.shape) == 4:
        wqv = [wq_d.ap()[:, :, k, :].rearrange("ic p j -> p ic j")
               for k in range(nchunk)]
    else:
        w4 = wq_d.ap().rearrange("(k p) w -> k p w", p=128)
        wqv = [w4[k] for k in range(nchunk)]

    for k in range(nchunk):
        s = k % 4
        ts_ = k % 2
        eng, sem = B.q[s]
        w = [B.last_read[s]] if B.last_read[s] else []
        S.dma(eng, lambda e, k=k, s=s: e.dma_start(
            out=B.wbuf[s][:, 0:width], in_=wv[k]), sem, waits=w)
        n_load = sem.n
        wts = [(sem, n_load)]
        if k < 2:
            wts.append((ac, n_scale))
        S.op("vector", lambda e, s=s: e.tensor_single_scalar(
            B.t8[:, 0:width], B.wbuf[s][:, 0:width], cols[:, col_s:col_s + 1],
            ALU.mult), sem=dv, waits=wts)
        B.last_read[s] = (dv, dv.n)
        w2 = [B.last_store[ts_]] if B.last_store[ts_] else []
        S.op("vector", lambda e, ts_=ts_: e.tensor_scalar(
            B.tern[ts_][:, 0:width], B.t8[:, 0:width], 1.0, -1.0, ALU.min,
            ALU.max), sem=dv, waits=w2)
        n_tern = dv.n
        seng, ssem = B.sq[ts_]
        if len(wq_d.shape) == 4:
            S.dma(seng, lambda e, k=k, ts_=ts_: e.dma_start(
                out=wqv[k],
                in_=B.tern[ts_].ap().rearrange("p (ic j) -> p ic j", j=512)),
                ssem, waits=[(dv, n_tern)])
        else:
            S.dma(seng, lambda e, k=k, ts_=ts_: e.dma_start(
                out=wqv[k], in_=B.tern[ts_][:, 0:width]), ssem,
                waits=[(dv, n_tern)])
        B.last_store[ts_] = (ssem, ssem.n)



def _ln_stats_acc(nc, S, sems, consts, dump, wait_x):
    dS, pe, ac, dv = (sems[n] for n in ("dS", "pe", "ac", "dv"))
    resid, stats, cols = consts["resid"], consts["stats"], consts["cols"]

    S.op("scalar", lambda e: e.preload_act_table(AF.Sqrt), sem=ac)

    sumc = stats[:, 96:112]
    sqc = stats[:, 112:128]
    rmx = stats[:, 128:144]
    rmn = stats[:, 144:160]
    for t in range(NT):
        w = [wait_x] if (wait_x and t == 0) else []
        S.op("scalar", lambda e, t=t: e.activation(
            dump[:, 0:D], resid[:, t * D:(t + 1) * D], AF.Copy,
            accum_out=sumc[:, t:t + 1]), sem=ac, waits=w)
        S.op("scalar", lambda e, t=t: e.activation(
            dump[:, 0:D], resid[:, t * D:(t + 1) * D], AF.Square,
            accum_out=sqc[:, t:t + 1]), sem=ac)
        S.op("vector", lambda e, t=t: e.reduce_max(
            rmx[:, t:t + 1], resid[:, t * D:(t + 1) * D], AX),
            sem=dv, waits=([wait_x] if (wait_x and t == 0) else []))
        S.op("vector", lambda e, t=t: e.tensor_reduce(
            rmn[:, t:t + 1], resid[:, t * D:(t + 1) * D], AX, ALU.min), sem=dv)
    return dv.n, ac.n


def _ln_stats(nc, S, sems, consts, dump, wait_x, chain=True):
    n_dv, n_acc = _ln_stats_acc(nc, S, sems, consts, dump, wait_x)
    if not chain:
        return n_dv
    return _ln_stats_tail(nc, S, sems, consts, n_acc)


def _ln_stats_tail(nc, S, sems, consts, n_acc):
    dS, pe, ac, dv = (sems[n] for n in ("dS", "pe", "ac", "dv"))
    resid, stats, cols = consts["resid"], consts["stats"], consts["cols"]
    sumc = stats[:, 96:112]
    sqc = stats[:, 112:128]
    rmx = stats[:, 128:144]
    rmn = stats[:, 144:160]

    mu = stats[:, 0:16]
    a_q = stats[:, 16:32]
    c_g = stats[:, 32:48]
    c_u = stats[:, 48:64]
    t1 = stats[:, 64:80]
    t2 = stats[:, 80:96]

    S.op("vector", lambda e: e.tensor_single_scalar(mu[:, :], sumc[:, :], 1.0 / D,
                                                    ALU.mult),
         sem=dv, waits=[(ac, n_acc)])
    S.op("vector", lambda e: e.tensor_single_scalar(sqc[:, :], sqc[:, :], 1.0 / D,
                                                    ALU.mult), sem=dv)
    S.op("vector", lambda e: e.tensor_mul(t1[:, :], mu[:, :], mu[:, :]), sem=dv)
    S.op("vector", lambda e: e.tensor_sub(sqc[:, :], sqc[:, :], t1[:, :]), sem=dv)
    n_var = dv.n
    S.op("scalar", lambda e: e.activation(t1[:, :], sqc[:, :], AF.Sqrt,
                                          bias=cols[:, 8:9]),
         sem=ac, waits=[(dv, n_var)])
    n_std = ac.n
    S.op("vector", lambda e: e.reciprocal(t1[:, :], t1[:, :]),
         sem=dv, waits=[(ac, n_std)])
    n_rstd = dv.n
    snap = consts.get("snap")

    def snp(j):
        if snap is not None:
            S.op("vector", lambda e, j=j: e.tensor_copy(
                snap[:, 16 * j:16 * j + 16],
                stats[:, 128:144] if j < 5 else (stats[:, 80:96] if j == 5
                                                 else stats[:, 16:32])), sem=dv)

    S.op("vector", lambda e: e.tensor_sub(rmx[:, :], rmx[:, :], mu[:, :]),
         sem=dv, waits=[(dv, n_rstd)])
    snp(0)
    S.op("vector", lambda e: e.tensor_sub(rmn[:, :], mu[:, :], rmn[:, :]), sem=dv)
    S.op("vector", lambda e: e.tensor_max(rmx[:, :], rmx[:, :], rmn[:, :]), sem=dv)
    snp(1)
    S.op("vector", lambda e: e.tensor_mul(rmx[:, :], rmx[:, :], t1[:, :]), sem=dv)
    snp(2)
    S.op("vector", lambda e: e.tensor_scalar_max(rmx[:, :], rmx[:, :], 1e-5), sem=dv)
    snp(3)
    snp(4)
    S.op("vector", lambda e: e.reciprocal(t2[:, :], rmx[:, :]), sem=dv)
    snp(5)
    S.op("vector", lambda e: e.scalar_tensor_tensor(
        a_q[:, :], t2[:, :], 127.0, t1[:, :], ALU.mult, ALU.mult), sem=dv)
    snp(6)
    S.op("vector", lambda e: e.tensor_single_scalar(
        c_g[:, :], rmx[:, :], cols[:, 3:4], ALU.mult), sem=dv)
    S.op("vector", lambda e: e.tensor_single_scalar(
        c_u[:, :], rmx[:, :], cols[:, 4:5], ALU.mult), sem=dv)
    return dv.n



def _ffn_phase(nc, S, sems, consts, wgq_d, wuq_d, wdq_d, wait_x, tag,
               stats_n=None, post_block=None):
    dS, pe, ac, dv = (sems[n] for n in ("dS", "pe", "ac", "dv"))
    resid, stats, cols, i128_sb = (consts["resid"], consts["stats"],
                                   consts["cols"], consts["i128"])
    mu = stats[:, 0:16]
    a_q = stats[:, 16:32]
    c_g = stats[:, 32:48]
    c_u = stats[:, 48:64]
    hcd = stats[:, 96:112]

    wgv = wgq_d.ap().rearrange("ic p kk j -> ic p (kk j)")
    wuv = wuq_d.ap().rearrange("ic p kk j -> ic p (kk j)")
    wdv = wdq_d.ap().rearrange("(jc p) (h w) -> jc h p w", p=128, w=512)

    with ExitStack() as fst:
        def fsb(name, shape, dt):
            return fst.enter_context(nc.sbuf_tensor(f"{tag}_{name}", shape, dt))

        wg_sb = [fsb(f"wg{i}", [128, 8 * 512], BF16) for i in range(2)]
        wu_sb = [fsb(f"wu{i}", [128, 8 * 512], BF16) for i in range(2)]
        wd_sb = [fsb(f"wd{i}", [128, 512], BF16) for i in range(2)]
        xq8 = fsb("xq8", [128, D], I8)
        xqb = fsb("xqb", [128, D], BF16)
        xqT = [fsb(f"xqT{i}", [128, D], BF16) for i in range(TPB)]
        silu_s = [fsb(f"silu{i}", [128, 512], F32) for i in range(2)]
        gq8 = fsb("gq8", [128, INNER], I8)
        mcol = fsb("mcol", [128, 8], F32)
        dump = fsb("dump", [128, D], F32)

        big = fsb("big", [128, TPB * 16384], mybir.dt.uint8)
        base = nc.lookup_mloc(big).addr
        raw_s = [nc.alloc_sbuf_tensor_at(f"{tag}_raw{i}", [128, INNER], F32,
                                         offset=base + i * 16384)
                 for i in range(TPB)]
        gqb_s = [nc.alloc_sbuf_tensor_at(f"{tag}_gqb{i}", [128, INNER], BF16,
                                         offset=base + i * 16384)
                 for i in range(TPB)]
        gqT_s = [nc.alloc_sbuf_tensor_at(f"{tag}_gqT{i}", [128, INNER], BF16,
                                         offset=base + i * 16384 + 8192)
                 for i in range(TPB)]

        if stats_n is None:
            n_stats = _ln_stats(nc, S, sems, consts, dump, wait_x)
        else:
            n_stats = stats_n
        S.maybe_bar(sems["bar"], 1)
        S.op("scalar", lambda e: e.preload_act_table(AF.Silu), sem=ac)

        with nc.psum_tensor(f"{tag}_ptr", [128, 1024], F32) as ptr, \
             nc.psum_tensor(f"{tag}_pg0", [128, 512], F32) as pg0, \
             nc.psum_tensor(f"{tag}_pg1", [128, 512], F32) as pg1, \
             nc.psum_tensor(f"{tag}_pu0", [128, 512], F32) as pu0, \
             nc.psum_tensor(f"{tag}_pu1", [128, 512], F32) as pu1, \
             nc.psum_tensor(f"{tag}_pd0", [128, 512], F32) as pd0, \
             nc.psum_tensor(f"{tag}_pd1", [128, 512], F32) as pd1:
            pg, pu = [pg0, pg1], [pu0, pu1]
            pd4 = [pg0, pg1, pd0, pd1]

            strip_loads = {}
            strip_free = {}
            tr_pe = {}
            ptr_evac = []
            n_gu_last = 0
            n_down_evac = 0
            n_down_mm = 0

            wg2 = [sems["wg0"], sems["wg1"]]
            wu2 = [sems["wu0"], sems["wu1"]]
            wd2 = [sems["wd0"], sems["wd1"]]

            def load_strips(b, ic):
                slot = ic % 2
                prevkey = (b, ic - 2) if ic >= 2 else (b - 1, ic + 6)
                prev = strip_free.get(prevkey)
                w = [(pe, prev)] if prev else []
                S.dma("sync", lambda e, ic=ic, slot=slot: e.dma_start(
                    out=wg_sb[slot][:, :], in_=wgv[ic]), wg2[slot], waits=w)
                S.dma("sync", lambda e, ic=ic, slot=slot: e.dma_start(
                    out=wu_sb[slot][:, :], in_=wuv[ic]), wu2[slot])
                strip_loads[(b, ic)] = (wg2[slot].n, wu2[slot].n)

            def emit_xq(b, n_gu_prev):
                xq_evac = {}
                for i, t in enumerate([b * TPB + j for j in range(TPB)]):
                    wq = [(dv, n_stats)] if (b == 0 and i == 0) else []
                    if i >= 1:
                        wq.append((pe, xq_evac[i - 1][1]))
                    S.op("vector", lambda e, t=t: e.tensor_scalar(
                        xq8[:, :], resid[:, t * D:(t + 1) * D],
                        mu[:, t:t + 1], a_q[:, t:t + 1], ALU.subtract, ALU.mult),
                        sem=dv, waits=wq)
                    S.op("vector", lambda e: e.tensor_copy(xqb[:, :], xq8[:, :]),
                         sem=dv)
                    n_xq = dv.n
                    for c in range(8):
                        w = []
                        if c == 0:
                            w = [(dv, n_xq)]
                            if ptr_evac:
                                w.append((ac, ptr_evac[-1]))
                        S.op("tensor", lambda e, c=c: e.matmul(
                            ptr[:, c * 128:(c + 1) * 128],
                            xqb[:, 128 * c:128 * c + 128], i128_sb[:, :],
                            start=True, stop=True),
                            sem=(pe if c == 7 else None), waits=w)
                    n_tr = pe.n
                    wev = [(pe, n_tr)]
                    if b > 0:
                        wev.append((pe, n_gu_prev))
                    S.op("scalar", lambda e, i=i: e.activation(
                        xqT[i][:, 0:512], ptr[:, 0:512], AF.Copy),
                        sem=ac, waits=wev)
                    S.op("scalar", lambda e, i=i: e.activation(
                        xqT[i][:, 512:1024], ptr[:, 512:1024], AF.Copy), sem=ac)
                    ptr_evac.append(ac.n)
                    xq_evac[i] = (ac.n, n_tr)
                return ac.n

            xq_all = {0: emit_xq(0, 0)}
            for b in range(NB):
                tiles = [b * TPB + i for i in range(TPB)]
                n_xq_all = xq_all[b]

                if (b, 0) not in strip_loads:
                    load_strips(b, 0)
                if (b, 1) not in strip_loads:
                    load_strips(b, 1)
                evac_g, evac_u = {}, {}
                gqb_done = {}
                for ic in range(8):
                    n_wg, n_wu = strip_loads[(b, ic)]
                    for i, t in enumerate(tiles):
                        ps = i % 2
                        w_mm = [(wg2[ic % 2], n_wg), (wu2[ic % 2], n_wu),
                                (ac, n_xq_all)]
                        if b > 0 and ic == 0 and i < 2:
                            w_mm.append((dv, n_down_evac))
                        prev = (ic, i - 2) if i >= 2 else ((ic - 1, i + 2) if ic >= 1 else None)
                        if prev in evac_g:
                            w_mm.append((ac, evac_g[prev]))
                        if prev in evac_u:
                            w_mm.append((dv, evac_u[prev]))
                        for kk in range(8):
                            S.op("tensor", lambda e, i=i, kk=kk, ps=ps, ic=ic: e.matmul(
                                pg[ps][:, :], xqT[i][:, 128 * kk:128 * kk + 128],
                                wg_sb[ic % 2][:, 512 * kk:512 * kk + 512],
                                start=(kk == 0), stop=(kk == 7)),
                                sem=(pe if kk == 7 else None),
                                waits=(w_mm if kk == 0 else []))
                        n_g = pe.n
                        for kk in range(8):
                            S.op("tensor", lambda e, i=i, kk=kk, ps=ps, ic=ic: e.matmul(
                                pu[ps][:, :], xqT[i][:, 128 * kk:128 * kk + 128],
                                wu_sb[ic % 2][:, 512 * kk:512 * kk + 512],
                                start=(kk == 0), stop=(kk == 7)),
                                sem=(pe if kk == 7 else None))
                        n_u = pe.n
                        strip_free[(b, ic)] = n_u
                        w_silu = [(pe, n_g)]
                        if prev in evac_u:
                            w_silu.append((dv, evac_u[prev]))
                        S.op("scalar", lambda e, t=t, ps=ps: e.activation(
                            silu_s[ps][:, :], pg[ps][:, :], AF.Silu,
                            scale=c_g[:, t:t + 1]),
                            sem=ac, waits=w_silu)
                        evac_g[(ic, i)] = ac.n
                        w_raw = [(ac, evac_g[(ic, i)]), (pe, n_u)]
                        if b > 0 and ic == 0:
                            w_raw.append((pe, n_down_mm))
                        S.op("vector", lambda e, i=i, ic=ic, ps=ps: e.tensor_mul(
                            raw_s[i][:, 512 * ic:512 * ic + 512], silu_s[ps][:, :],
                            pu[ps][:, :]),
                            sem=dv, waits=w_raw)
                        evac_u[(ic, i)] = dv.n
                        if ic == 7:
                            S.op("vector", lambda e, i=i: e.reduce_max(
                                mcol[:, 0:1], raw_s[i][:, :], AX,
                                apply_absolute_value=True), sem=dv)
                            S.op("vector", lambda e, t=t: e.tensor_single_scalar(
                                mcol[:, 1:2], mcol[:, 0:1], c_u[:, t:t + 1],
                                ALU.mult), sem=dv)
                            S.op("vector", lambda e: e.tensor_scalar_max(
                                mcol[:, 1:2], mcol[:, 1:2], 1e-5), sem=dv)
                            S.op("vector", lambda e: e.reciprocal(
                                mcol[:, 2:3], mcol[:, 1:2]), sem=dv)
                            S.op("vector", lambda e, t=t: e.scalar_tensor_tensor(
                                mcol[:, 3:4], mcol[:, 2:3], 127.0,
                                c_u[:, t:t + 1], ALU.mult, ALU.mult), sem=dv)
                            S.op("vector", lambda e, t=t: e.tensor_single_scalar(
                                hcd[:, t:t + 1], mcol[:, 1:2], cols[:, 5:6],
                                ALU.mult), sem=dv)
                            S.op("vector", lambda e, i=i: e.tensor_single_scalar(
                                gq8[:, :], raw_s[i][:, :], mcol[:, 3:4],
                                ALU.mult), sem=dv)
                            S.op("vector", lambda e, i=i: e.tensor_copy(
                                gqb_s[i][:, :], gq8[:, :]), sem=dv)
                            gqb_done[i] = dv.n
                    S.maybe_bar(sems["bar"], 2)
                    if ic + 2 < 8:
                        load_strips(b, ic + 2)
                    elif b + 1 < NB:
                        load_strips(b + 1, ic + 2 - 8)
                n_gu_last = pe.n

                if _STAGE == "dbg_raw" and b == 0:
                    for i in range(TPB):
                        S.dma("gpsimd", lambda e, i=i: e.dma_start(
                            out=consts["out"].ap().rearrange(
                                "(f p) c -> f p c", p=128)[4 * i:4 * i + 4],
                            in_=raw_s[i].ap().rearrange("p (f c) -> f p c", c=D)),
                            sems["dG"], waits=[(dv, evac_u[(7, i)])])
                    S.wait("gpsimd", sems["dG"], sems["dG"].n)
                    return

                for i, t in enumerate(tiles):
                    n_gqb = gqb_done[i]
                    for grp in range(8):
                        half = (grp % 2) * 512
                        for c in range(4):
                            jc = grp * 4 + c
                            w = []
                            if c == 0:
                                if grp == 0:
                                    w.append((dv, n_gqb))
                                if ptr_evac:
                                    w.append((ac, ptr_evac[-1] if grp < 2 else ptr_evac[-2]))
                            S.op("tensor", lambda e, i=i, jc=jc, half=half, c=c: e.matmul(
                                ptr[:, half + c * 128: half + c * 128 + 128],
                                gqb_s[i][:, 128 * jc:128 * jc + 128], i128_sb[:, :],
                                start=True, stop=True),
                                sem=(pe if c == 3 else None), waits=w)
                        n_tr = pe.n
                        S.op("scalar", lambda e, i=i, grp=grp, half=half: e.activation(
                            gqT_s[i][:, grp * 512: grp * 512 + 512],
                            ptr[:, half: half + 512], AF.Copy),
                            sem=ac, waits=[(pe, n_tr)])
                        ptr_evac.append(ac.n)
                    tr_pe[i] = pe.n
                    S.maybe_bar(sems["bar"], 2)
                n_gqT_ready = ac.n

                ev_prev = n_down_evac
                wd_slot_free = getattr(_ffn_phase, "_wd_free", {0: 0, 1: 0})
                for half in range(2):
                    first_mm_wait = [(ac, n_gqT_ready)]
                    if half == 0:
                        first_mm_wait.append((ac, evac_g[(7, 3)]))
                        first_mm_wait.append((dv, evac_u[(7, 3)]))
                        if b > 0:
                            first_mm_wait.append((dv, ev_prev))
                    else:
                        first_mm_wait.append((dv, n_down_evac))
                    for jc in range(32):
                        slot = jc % 2
                        wl = [(pe, wd_slot_free[slot])] if wd_slot_free[slot] else []
                        S.dma("sync", lambda e, jc=jc, half=half, slot=slot: e.dma_start(
                            out=wd_sb[slot][:, :], in_=wdv[jc, half]), wd2[slot],
                            waits=wl)
                        n_wd = wd2[slot].n
                        for i, t in enumerate(tiles):
                            w = []
                            if jc == 0:
                                w = list(first_mm_wait) if i == 0 else []
                            if i == 0:
                                w.append((wd2[slot], n_wd))
                            S.op("tensor", lambda e, i=i, jc=jc, slot=slot: e.matmul(
                                pd4[i][:, :], gqT_s[i][:, 128 * jc:128 * jc + 128],
                                wd_sb[slot][:, :],
                                start=(jc == 0), stop=(jc == 31)),
                                sem=(pe if i == TPB - 1 else None), waits=w)
                        wd_slot_free[slot] = pe.n
                    n_dmm = pe.n
                    for i, t in enumerate(tiles):
                        S.op("vector", lambda e, t=t, i=i, half=half: e.scalar_tensor_tensor(
                            resid[:, t * D + half * 512: t * D + half * 512 + 512],
                            pd4[i][:, :], hcd[:, t:t + 1],
                            resid[:, t * D + half * 512: t * D + half * 512 + 512],
                            ALU.mult, ALU.add),
                            sem=dv, waits=[(pe, n_dmm)])
                    n_down_evac = dv.n
                    if half == 0 and b + 1 < NB:
                        xq_all[b + 1] = emit_xq(b + 1, n_gu_last)
                    S.maybe_bar(sems["bar"], 2)
                n_down_mm = pe.n
                if post_block is not None:
                    post_block(b, dict(ptr=ptr, ptr_evac=ptr_evac,
                                       n_down_evac=n_down_evac, dump=dump))
                S.maybe_bar(sems["bar"], 1)



def _p2_setup(nc, S, sems, consts, st, g_in, b_in, cc1a_in, cc1b_in,
              cc1a_out, cc1b_out):
    dS, pe, ac, dv, cc = (sems[n] for n in ("dS", "pe", "ac", "dv", "cc"))
    resid, stats, i128_sb = consts["resid"], consts["stats"], consts["i128"]

    def psb(name, shape, dt=F32):
        return st.enter_context(nc.sbuf_tensor(name, shape, dt))

    g_t = psb("p2_g", [128, D])
    b_t = psb("p2_b", [128, D])
    nbuf = psb("p2_n", [128, D])
    h2b = psb("p2_h2b", [128, D], BF16)
    h2f = [psb(f"p2_h2f{i}", [128, D], BF16) for i in range(2)]

    S.dma("sync", lambda e: e.dma_start(out=g_t[:, :], in_=g_in[:, :]), dS)
    S.dma("sync", lambda e: e.dma_start(out=b_t[:, :], in_=b_in[:, :]), dS)
    n_gb = dS.n

    sumc = stats[:, 96:112]
    sqc = stats[:, 112:128]
    rstd2 = stats[:, 64:80]
    b2 = stats[:, 80:96]
    state = dict(prev_tt=0, prev_ev={}, prev_dma={}, cc_done=[None, None])

    def emit_accs(b, ffnctx):
        dump = ffnctx["dump"]
        for k, t in enumerate(range(4 * b, 4 * b + 4)):
            w = [(dv, ffnctx["n_down_evac"])] if k == 0 else []
            S.op("scalar", lambda e, t=t: e.activation(
                dump[:, :], resid[:, t * D:(t + 1) * D], AF.Copy,
                accum_out=sumc[:, t:t + 1]), sem=ac, waits=w)
            S.op("scalar", lambda e, t=t: e.activation(
                dump[:, :], resid[:, t * D:(t + 1) * D], AF.Square,
                accum_out=sqc[:, t:t + 1]), sem=ac)
        state["acc_n"] = ac.n

    def emit_half(h, ffnctx):
        dump = ffnctx["dump"]
        t0 = 8 * h
        hs = slice(t0, t0 + 8)
        n_acc = state["acc_n"]
        S.op("vector", lambda e: e.tensor_single_scalar(
            sumc[:, hs], sumc[:, hs], 1.0 / D, ALU.mult),
            sem=dv, waits=[(ac, n_acc)])
        S.op("vector", lambda e: e.tensor_single_scalar(
            sqc[:, hs], sqc[:, hs], 1.0 / D, ALU.mult), sem=dv)
        S.op("vector", lambda e: e.tensor_mul(b2[:, hs], sumc[:, hs],
                                              sumc[:, hs]), sem=dv)
        S.op("vector", lambda e: e.tensor_sub(sqc[:, hs], sqc[:, hs],
                                              b2[:, hs]), sem=dv)
        n_var = dv.n
        S.op("scalar", lambda e: e.preload_act_table(AF.Sqrt), sem=ac)
        S.op("scalar", lambda e: e.activation(rstd2[:, hs], sqc[:, hs], AF.Sqrt,
                                              bias=consts["cols"][:, 8:9]),
             sem=ac, waits=[(dv, n_var)])
        n_std2 = ac.n
        if h == 0:
            S.op("scalar", lambda e: e.preload_act_table(AF.Silu), sem=ac)
        S.op("vector", lambda e: e.reciprocal(rstd2[:, hs], rstd2[:, hs]),
             sem=dv, waits=[(ac, n_std2)])
        n_rstd = dv.n

        ptr2 = ffnctx["ptr"]
        ptr_evac = ffnctx["ptr_evac"]
        for t in range(t0, t0 + 8):
            slot = t % 2
            w_n = [(dv, n_rstd), (dS, n_gb)]
            if state["prev_tt"]:
                w_n.append((dv, state["prev_tt"]))
            S.op("vector", lambda e, t=t: e.tensor_scalar(
                nbuf[:, :], resid[:, t * D:(t + 1) * D],
                sumc[:, t:t + 1], rstd2[:, t:t + 1], ALU.subtract, ALU.mult),
                sem=dv, waits=w_n)
            w_m = []
            if (t - 1) in state["prev_ev"]:
                w_m.append((pe, state["prev_ev"][(t - 1, "pe")]))
            S.op("vector", lambda e: e.tensor_mul(nbuf[:, :], nbuf[:, :],
                                                  g_t[:, :]),
                 sem=dv, waits=w_m)
            S.op("vector", lambda e: e.tensor_add(h2b[:, :], nbuf[:, :],
                                                  b_t[:, :]), sem=dv)
            state["prev_tt"] = dv.n
            n_h2b = dv.n
            for c in range(8):
                w2 = []
                if c == 0:
                    w2 = [(dv, n_h2b)]
                    if ptr_evac:
                        w2.append((ac, ptr_evac[-1]))
                S.op("tensor", lambda e, c=c: e.matmul(
                    ptr2[:, c * 128:(c + 1) * 128],
                    h2b[:, 128 * c:128 * c + 128], i128_sb[:, :],
                    start=True, stop=True),
                    sem=(pe if c == 7 else None), waits=w2)
            n_tr = pe.n
            wev = [(pe, n_tr)]
            if (t - 2) in state["prev_dma"]:
                wev.append(state["prev_dma"][t - 2])
            S.op("scalar", lambda e, slot=slot: e.activation(
                h2f[slot][:, 0:512], ptr2[:, 0:512], AF.Copy),
                sem=ac, waits=wev)
            S.op("scalar", lambda e, slot=slot: e.activation(
                h2f[slot][:, 512:1024], ptr2[:, 512:1024], AF.Copy), sem=ac)
            state["prev_ev"][t] = ac.n
            state["prev_ev"][(t, "pe")] = n_tr
            ptr_evac.append(ac.n)
            h2fs = sems["h2f0"] if slot == 0 else sems["h2f1"]
            cct = cc1a_in if h == 0 else cc1b_in
            S.dma("sync", lambda e, t=t, slot=slot, cct=cct: e.dma_start(
                out=cct.ap()[t % 8], in_=h2f[slot][:, :]),
                h2fs, waits=[(ac, state["prev_ev"][t])])
            state["prev_dma"][t] = (h2fs, h2fs.n)

        src_t, dst_t = ((cc1a_in, cc1a_out) if h == 0 else (cc1b_in, cc1b_out))
        S.op("gpsimd", lambda e, src_t=src_t, dst_t=dst_t: e.collective_compute(
            "AllGather", ALU.bypass, replica_groups=[list(range(NCORE))],
            ins=[src_t.ap().opt()], outs=[dst_t.ap().opt()]),
            sem=cc, waits=[(sems["h2f0"], sems["h2f0"].n),
                           (sems["h2f1"], sems["h2f1"].n)])
        state["cc_done"][h] = cc.n

    return emit_accs, emit_half, state



def _p3_qkv(nc, S, astack, sems, consts, wslices, ropes, cc1a_out, ccA_done,
            cc1b_out, ccB_done, qnf, knf, vres):
    dS, dG, pe, ac, dv, cc = (sems[n] for n in ("dS", "dG", "pe", "ac", "dv", "cc"))
    cols, i128_sb = consts["cols"], consts["i128"]
    ind_sq_sb, ind_bc_sb = consts["ind_sq"], consts["ind_bc"]
    wqm_in, wqr_in, wkm_in, wkr_in, wvm_in = wslices
    cosq_in, sinq_in, cosk_in, sink_in = ropes

    S.op("scalar", lambda e: e.add_instruction(mybir.InstLoadActFuncSet(
        name=nc.get_next_instruction_name(), act_func_set_id=6,
        ins=[], outs=[])), sem=sems["ac"])
    wq_ro = astack.enter_context(nc.sbuf_tensor("wq_ro", [128, D], BF16))
    wk_ro = astack.enter_context(nc.sbuf_tensor("wk_ro", [128, D], BF16))
    wv_bf = astack.enter_context(nc.sbuf_tensor("wv_bf", [128, D], BF16))

    def _rope_fold():
        with ExitStack() as ropest:
            wm = ropest.enter_context(nc.sbuf_tensor("p3_wm", [128, D], F32))
            wr = ropest.enter_context(nc.sbuf_tensor("p3_wr", [128, D], F32))
            cosb = ropest.enter_context(nc.sbuf_tensor("p3_cos", [128, 128], F32))
            sinb = ropest.enter_context(nc.sbuf_tensor("p3_sin", [128, 128], F32))
            tmp = ropest.enter_context(nc.sbuf_tensor("p3_tmp", [128, 128], F32))
            tmp2 = ropest.enter_context(nc.sbuf_tensor("p3_tmp2", [128, 128], F32))

            for (m_in, r_in, c_in, s_in, dst) in (
                    (wqm_in, wqr_in, cosq_in, sinq_in, wq_ro),
                    (wkm_in, wkr_in, cosk_in, sink_in, wk_ro)):
                n_prev_use = dv.n
                S.dma("gpsimd", lambda e, m=m_in: e.dma_start(
                    out=wm.ap().rearrange("p (k c) -> p k c", c=128),
                    in_=m.ap().rearrange("(k p) c -> p k c", p=128)),
                    dG, waits=[(dv, n_prev_use)])
                S.dma("gpsimd", lambda e, r=r_in: e.dma_start(
                    out=wr.ap().rearrange("p (k c) -> p k c", c=128),
                    in_=r.ap().rearrange("(k p) c -> p k c", p=128)), dG)
                S.dma("gpsimd", lambda e, c=c_in: e.dma_start(
                    out=cosb[:, :], in_=c[:, :]), dG)
                S.dma("gpsimd", lambda e, s=s_in: e.dma_start(
                    out=sinb[:, :], in_=s[:, :]), dG)
                n_l = dG.n
                for k in range(8):
                    S.op("vector", lambda e, k=k: e.tensor_mul(
                        tmp[:, :], wr[:, 128 * k:128 * k + 128], sinb[:, :]),
                        sem=dv, waits=[(dG, n_l)] if k == 0 else [])
                    S.op("vector", lambda e, k=k: e.tensor_mul(
                        tmp2[:, :], wm[:, 128 * k:128 * k + 128], cosb[:, :]),
                        sem=dv)
                    S.op("vector", lambda e, k=k, dst=dst: e.tensor_add(
                        dst[:, 128 * k:128 * k + 128], tmp2[:, :], tmp[:, :]),
                        sem=dv)
        return dv.n

    with ExitStack() as rst:
        n_ro = _rope_fold()
        if _STAGE == "dbg_wro":
            S.wait("gpsimd", dv, n_ro)
            S.dma("gpsimd", lambda e: e.dma_start(
                out=consts["out"].ap()[0:128, :], in_=wq_ro[:, :]), dG)
            S.dma("gpsimd", lambda e: e.dma_start(
                out=consts["out"].ap()[128:256, :], in_=wk_ro[:, :]), dG)
            S.wait("gpsimd", dG, dG.n)
            S.flush()
            return
        S.dma("gpsimd", lambda e: e.dma_start(
            out=wv_bf.ap().rearrange("p (k c) -> p k c", c=128),
            in_=wvm_in.ap().rearrange("(k p) c -> p k c", p=128)), dG)
        n_wv = dG.n

        h2g = [rst.enter_context(nc.sbuf_tensor(f"p3_h2g{i}", [128, 8 * 512], BF16))
               for i in range(2)]
        qraw = [rst.enter_context(nc.sbuf_tensor(f"p3_qraw{i}", [128, 1024], BF16))
                for i in range(3)]
        sqb = [rst.enter_context(nc.sbuf_tensor(f"p3_sqb{i}", [128, 1024], BF16))
               for i in range(2)]
        nrm = rst.enter_context(nc.sbuf_tensor("p3_nrm", [2, 512], F32))
        nrm2 = rst.enter_context(nc.sbuf_tensor("p3_nrm2", [2, 512], F32))
        rq_s = [rst.enter_context(nc.sbuf_tensor(f"p3_rq{i}", [2, 512], BF16))
                for i in range(2)]
        bcs = rst.enter_context(nc.sbuf_tensor("p3_bcs", [128, 512], BF16))

        with nc.psum_tensor("p3_pq", [128, 512], F32) as pq, \
             nc.psum_tensor("p3_pk", [128, 512], F32) as pk, \
             nc.psum_tensor("p3_pv", [128, 512], F32) as pv, \
             nc.psum_tensor("p3_pn", [2, 512], F32) as pn, \
             nc.psum_tensor("p3_pbc", [128, 512], F32) as pbc:

            S.op("vector", lambda e: e.memset(
                vres.ap().rearrange("p (t h one) -> p t h one", h=2, one=65)
                [:, :, :, 64:65], 1.0), sem=dv)

            h2g2 = [sems["h2g0"], sems["h2g1"]]
            load_done = {}
            qk_evac = {}
            v_evac = {}
            cst = dict(n_n=0, n_rq=0, n_pbc=0, n_bcs=0, last_tt=0, n_pnread=0)
            chain_pn = {}
            chain_done = {}
            rqc = {}
            pendA, pendB = [], []
            for g in range(32):
                slot = g % 2
                sl3 = g % 3
                w = ([(cc, ccA_done), (dv, n_ro)] if g < 2
                     else [(pe, load_done[(g - 2, "pe")])])
                if g in (16, 17):
                    w.append((cc, ccB_done))
                for i in range(4):
                    tg = 4 * g + i
                    srcT = cc1a_out if tg // 8 < 8 else cc1b_out
                    S.dma("gpsimd", lambda e, tg=tg, i=i, slot=slot, srcT=srcT:
                          e.dma_start(
                        out=h2g[slot][:, i * 1024:(i + 1) * 1024],
                        in_=srcT.ap()[tg % 8, (tg // 8) % 8]),
                        h2g2[slot], waits=(w if i == 0 else []))
                n_l = h2g2[slot].n

                for (dst_ps, wsb) in ((pq, wq_ro), (pk, wk_ro)):
                    for c in range(8):
                        w2 = []
                        if c == 0:
                            w2 = [(h2g2[slot], n_l), (dv, n_ro)]
                            if g >= 1:
                                w2.append((ac, qk_evac.get(g - 1, (0, 0))[0]))
                        S.op("tensor", lambda e, c=c, dst_ps=dst_ps, wsb=wsb, slot=slot: e.matmul(
                            dst_ps[:, :], wsb[:, 128 * c:128 * c + 128],
                            h2g[slot].ap().rearrange("p (i cq) -> p i cq", cq=1024)
                            [:, :, c * 128:(c + 1) * 128],
                            start=(c == 0), stop=(c == 7)),
                            sem=(pe if c == 7 else None), waits=w2)
                n_qk = pe.n
                if _STAGE == "dbg_h2g" and g == 0:
                    S.wait("gpsimd", h2g2[0], h2g2[0].n)
                    for f in range(4):
                        S.dma("gpsimd", lambda e, f=f: e.dma_start(
                            out=consts["out"].ap().rearrange("(f p) c -> f p c",
                                                             p=128)[f],
                            in_=h2g[0][:, f * D:(f + 1) * D]), sems["dG"])
                    S.op("scalar", lambda e: e.activation(
                        qraw[0][:, 0:512], pq[:, :], AF.Copy), sem=ac,
                        waits=[(pe, n_qk)])
                    S.wait("gpsimd", ac, ac.n)
                    S.dma("gpsimd", lambda e: e.dma_start(
                        out=consts["out"].ap().rearrange("(f p) c -> f p c", p=128)
                        [4, :, 0:512],
                        in_=qraw[0][:, 0:512]), sems["dG"])
                    S.wait("gpsimd", sems["dG"], sems["dG"].n)
                    return
                for i in range(4):
                    for c in range(8):
                        w2 = []
                        if i == 0 and c == 0:
                            w2 = [(dG, n_wv)]
                            if g >= 1:
                                w2.append((ac, v_evac.get(g - 1, 0)))
                        S.op("tensor", lambda e, i=i, c=c, slot=slot: e.matmul(
                            pv[:, i * 128:(i + 1) * 128],
                            h2g[slot][:, i * 1024 + c * 128: i * 1024 + c * 128 + 128],
                            wv_bf[:, 128 * c:128 * c + 128],
                            start=(c == 0), stop=(c == 7)),
                            sem=(pe if (c == 7 and i == 3) else None), waits=w2)
                n_v = pe.n
                load_done[(g, "pe")] = n_v

                S.op("scalar", lambda e, g=g: e.activation(
                    vres.ap().rearrange("p (t h one) -> p t h one", h=2, one=65)
                    [:, 4 * g:4 * g + 4, :, 0:64],
                    pv.ap().rearrange("p (i h d) -> p i h d", i=4, h=2),
                    AF.Copy), sem=ac, waits=[(pe, n_v)])
                v_evac[g] = ac.n

                w_sq = [(pe, n_qk)]
                if (g - 2) in chain_pn:
                    w_sq.append((pe, chain_pn[g - 2]))
                if (g - 3) in chain_done:
                    w_sq.append((dv, chain_done[g - 3]))
                for (side, ps_in) in ((0, pq), (1, pk)):
                    S.op("scalar", lambda e, ps_in=ps_in, side=side, slot=slot:
                         e.activation(sqb[slot][:, side * 512:side * 512 + 512],
                                      ps_in[:, :], AF.Square),
                         sem=ac, waits=(w_sq if side == 0 else []))
                    S.op("scalar", lambda e, ps_in=ps_in, side=side, sl3=sl3:
                         e.activation(qraw[sl3][:, side * 512:side * 512 + 512],
                                      ps_in[:, :], AF.Copy), sem=ac)
                n_sq_done = ac.n
                qk_evac[g] = (n_sq_done, 0)

                def stageA(g=g, slot=slot, sl3=sl3, n_sq_done=n_sq_done):
                    for (side, is_q) in ((0, True), (1, False)):
                        off = side * 512
                        w_pn = [(ac, n_sq_done)]
                        if cst["n_pnread"]:
                            w_pn.append((dv, cst["n_pnread"]))
                        S.op("tensor", lambda e, slot=slot, off=off: e.matmul(
                            pn[:, :], ind_sq_sb[:, :], sqb[slot][:, off:off + 512],
                            start=True, stop=True), sem=pe, waits=w_pn)
                        n_pn = pe.n
                        if side == 1:
                            chain_pn[g] = n_pn
                        w_n = [(pe, n_pn)]
                        if cst["n_n"]:
                            w_n.append((ac, cst["n_n"]))
                        if cst["n_rq"]:
                            w_n.append((dv, cst["n_rq"]))
                        S.op("vector", lambda e: e.tensor_scalar_max(
                            nrm[:, :], pn[:, :], 1e-24), sem=dv, waits=w_n)
                        cst["n_pnread"] = dv.n
                        S.op("scalar", lambda e: e.activation(
                            nrm2[:, :], nrm[:, :], AF.Ln), sem=ac,
                            waits=[(dv, cst["n_pnread"])])
                        w_rq = [(pe, cst["n_pbc"])] if cst["n_pbc"] else []
                        if is_q:
                            S.op("scalar", lambda e: e.activation(
                                nrm[:, :], nrm2[:, :], AF.Exp, scale=-0.5),
                                sem=ac)
                            cst["n_n"] = ac.n
                            S.op("vector", lambda e, side=side: e.tensor_single_scalar(
                                rq_s[side][:, :], nrm[:, :], cols[0:2, 6:7],
                                ALU.mult), sem=dv,
                                waits=w_rq + [(ac, cst["n_n"])])
                        else:
                            S.op("scalar", lambda e, side=side: e.activation(
                                rq_s[side][:, :], nrm2[:, :], AF.Exp, scale=-0.5),
                                sem=ac, waits=w_rq)
                            cst["n_n"] = ac.n
                            S.op("vector", lambda e: e.engine_nop(), sem=dv,
                                 waits=[(ac, cst["n_n"])])
                        rqc[(g, side)] = dv.n
                        cst["n_rq"] = dv.n

                    def stageB(g=g, sl3=sl3):
                        for (side, dst) in ((0, qnf), (1, knf)):
                            off = side * 512
                            w_pbc = [(dv, rqc[(g, side)])]
                            if cst["n_bcs"]:
                                w_pbc.append((ac, cst["n_bcs"]))
                            S.op("tensor", lambda e, side=side: e.matmul(
                                pbc[:, :], ind_bc_sb[:, :], rq_s[side][:, :],
                                start=True, stop=True), sem=pe, waits=w_pbc)
                            n_pbc = pe.n
                            cst["n_pbc"] = n_pbc
                            w_bcs = [(pe, n_pbc)]
                            if cst["last_tt"]:
                                w_bcs.append((dv, cst["last_tt"]))
                            S.op("scalar", lambda e: e.activation(
                                bcs[:, :], pbc[:, :], AF.Copy), sem=ac, waits=w_bcs)
                            cst["n_bcs"] = ac.n
                            S.op("vector", lambda e, dst=dst, g=g, sl3=sl3,
                                 off=off: e.tensor_mul(
                                dst[:, 512 * g:512 * g + 512],
                                qraw[sl3][:, off:off + 512], bcs[:, :]),
                                sem=dv, waits=[(ac, cst["n_bcs"])])
                            cst["last_tt"] = dv.n
                        chain_done[g] = cst["last_tt"]
                    return stageB

                if pendB:
                    pendB.pop(0)()
                if pendA:
                    pendB.append(pendA.pop()())
                pendA.append(stageA)
            if pendB:
                pendB.pop(0)()
            if pendA:
                pendB.append(pendA.pop()())
            while pendB:
                pendB.pop(0)()



def _p4_attn(nc, S, astack, sems, consts, qnf, knf, vres, cc2h):
    dS, pe, ac, dv = (sems[n] for n in ("dS", "pe", "ac", "dv"))
    cc = sems["cc"]
    cols, triu_sb, ones64_sb = consts["cols"], consts["triu"], consts["ones64"]

    pa_sb = [astack.enter_context(nc.sbuf_tensor(f"p4_pa{i}", [128, 1024], BF16))
             for i in range(2)]
    pb_sb = [astack.enter_context(nc.sbuf_tensor(f"p4_pb{i}", [128, 1024], BF16))
             for i in range(2)]
    rs_sb = astack.enter_context(nc.sbuf_tensor("p4_rs", [1, 1024], F32))
    bca_sb = astack.enter_context(nc.sbuf_tensor("p4_bca", [64, 512], F32))
    bcb_sb = astack.enter_context(nc.sbuf_tensor("p4_bcb", [64, 512], F32))
    of_sb = [astack.enter_context(nc.sbuf_tensor(f"p4_of{i}", [128, 512], BF16))
             for i in range(2)]
    orawA = [astack.enter_context(nc.sbuf_tensor(f"p4_oa{i}", [65, 512], F32))
             for i in range(2)]
    orawB = [astack.enter_context(nc.sbuf_tensor(f"p4_ob{i}", [65, 512], F32))
             for i in range(2)]

    with nc.psum_tensor("p4_psA", [128, 1024], F32) as psA, \
         nc.psum_tensor("p4_psB", [128, 1024], F32) as psB, \
         nc.psum_tensor("p4_oA", [65, 512], F32) as oA, \
         nc.psum_tensor("p4_oB", [65, 512], F32) as oB, \
         nc.psum_tensor("p4_nA", [64, 512], F32) as nA, \
         nc.psum_tensor("p4_nB", [64, 512], F32) as nB:

        of_dma = []
        onorm = []
        pending_norm = []
        p4lim = int(os.environ.get("P4LIM", "999"))
        ngroups = 0
        for s in range(NSEQ):
            for qg in range(8):
                if ngroups >= p4lim:
                    continue
                ngroups += 1
                gi = s * 8 + qg
                if gi == 17:
                    S.op("gpsimd", lambda e: e.collective_compute(
                        "AllGather", ALU.bypass,
                        replica_groups=[list(range(NCORE))],
                        ins=[cc2h["a_in"].ap().opt()],
                        outs=[cc2h["a_out"].ap().opt()]),
                        sem=cc, waits=[(sems["of0"], sems["of0"].n),
                                       (sems["of1"], sems["of1"].n)])
                    cc2h["ccA_done"] = cc.n
                qc0 = s * SEQ + qg * 512
                nkt = 4 * qg + 4
                npair = nkt // 2
                exp_done = {}
                exp_done_h = {}
                diag_done = {}
                av_done = {}
                prev = onorm[-1] if onorm else None
                for pair in range(npair):
                    for kk in (2 * pair, 2 * pair + 1):
                        half = (kk % 2) * 512
                        gk = 32 * s + kk
                        w = []
                        if pair >= 1:
                            w.append((ac, exp_done_h[(pair - 1, kk % 2)]))
                        elif prev:
                            w.append((ac, prev["exp_last"]))
                        S.op("tensor", lambda e, gk=gk, half=half, qc0=qc0: e.matmul(
                            psA[:, half:half + 512],
                            knf[0:64, 128 * gk:128 * gk + 128],
                            qnf[0:64, qc0:qc0 + 512],
                            start=True, stop=True, tile_position=(0, 0)),
                            sem=pe, waits=w)
                        S.op("tensor", lambda e, gk=gk, half=half, qc0=qc0: e.matmul(
                            psB[:, half:half + 512],
                            knf[64:128, 128 * gk:128 * gk + 128],
                            qnf[64:128, qc0:qc0 + 512],
                            start=True, stop=True, tile_position=(64, 0)),
                            sem=pe)
                        n_sc = pe.n
                        w_exp = [(pe, n_sc)]
                        if pair >= 2:
                            w_exp.append((pe, av_done[pair - 2]))
                        elif prev:
                            w_exp.append((pe, prev["n_av"]))
                        S.op("scalar", lambda e, pair=pair, half=half: e.activation(
                            pa_sb[pair % 2][:, half:half + 512],
                            psA[:, half:half + 512], AF.Exp), sem=ac,
                            waits=w_exp)
                        S.op("scalar", lambda e, pair=pair, half=half: e.activation(
                            pb_sb[pair % 2][:, half:half + 512],
                            psB[:, half:half + 512], AF.Exp), sem=ac)
                        exp_done_h[(pair, kk % 2)] = ac.n
                    exp_done[pair] = ac.n
                    for kk in (2 * pair, 2 * pair + 1):
                        if kk >= 4 * qg:
                            i0 = kk - 4 * qg
                            off = (kk % 2) * 512 + i0 * 128
                            for psb in (pa_sb, pb_sb):
                                S.op("vector", lambda e, psb=psb, pair=pair, off=off: e.tensor_mul(
                                    psb[pair % 2][:, off:off + 128],
                                    psb[pair % 2][:, off:off + 128],
                                    triu_sb[:, :]),
                                    sem=dv, waits=[(ac, exp_done[pair])])
                            diag_done[kk] = dv.n
                    for kk in (2 * pair, 2 * pair + 1):
                        gk2 = 32 * s + kk
                        i0 = max(0, kk - 4 * qg)
                        off = (kk % 2) * 512 + i0 * 128
                        nq = 512 - i0 * 128
                        wav = [(ac, exp_done_h[(pair, kk % 2)])]
                        if kk in diag_done:
                            wav.append((dv, diag_done[kk]))
                        if kk == 0 and prev:
                            wav.append((dv, prev["n_evac"]))
                        S.op("tensor", lambda e, gk2=gk2, off=off, nq=nq, i0=i0, kk=kk, pair=pair: e.matmul(
                            oA[:, i0 * 128: i0 * 128 + nq],
                            vres[:, 130 * gk2:130 * gk2 + 65],
                            pa_sb[pair % 2][:, off:off + nq],
                            start=(kk == 0), stop=(kk == nkt - 1)),
                            sem=pe, waits=wav)
                        S.op("tensor", lambda e, gk2=gk2, off=off, nq=nq, i0=i0, kk=kk, pair=pair: e.matmul(
                            oB[:, i0 * 128: i0 * 128 + nq],
                            vres[:, 130 * gk2 + 65:130 * gk2 + 130],
                            pb_sb[pair % 2][:, off:off + nq],
                            start=(kk == 0), stop=(kk == nkt - 1)),
                            sem=pe)
                    av_done[pair] = pe.n
                    if pair == 1 and pending_norm:
                        pending_norm.pop()()
                    S.maybe_bar(sems["bar"], 2)
                    if _STAGE == "dbg_p0" and s == 0 and qg == 0 and pair == 0:
                        S.wait("gpsimd", dv, dv.n)
                        S.wait("gpsimd", ac, ac.n)
                        S.dma("gpsimd", lambda e: e.dma_start(
                            out=consts["out"].ap()[0:128, :],
                            in_=pa_sb[0][:, :]), sems["dG"])
                        S.dma("gpsimd", lambda e: e.dma_start(
                            out=consts["out"].ap()[128:256, :],
                            in_=pb_sb[0][:, :]), sems["dG"])
                        S.wait("gpsimd", sems["dG"], sems["dG"].n)
                        S.flush()
                        raise StopIteration
                n_av = pe.n

                slot = gi % 2
                w_ev = [(pe, n_av)]
                if len(onorm) >= 2:
                    w_ev.append((dv, onorm[-2]["n_of"]))
                S.op("vector", lambda e, slot=slot: e.tensor_copy(
                    orawA[slot][:, :], oA[:, :]), sem=dv, waits=w_ev)
                S.op("vector", lambda e, slot=slot: e.tensor_copy(
                    orawB[slot][:, :], oB[:, :]), sem=dv)
                n_evac = dv.n
                entry = dict(n_evac=n_evac, n_av=n_av,
                             exp_last=exp_done[npair - 1])
                onorm.append(entry)

                def emit_norm(entry=entry, slot=slot, pidx=len(onorm) - 2,
                              s=s, qg=qg):
                    pv = onorm[pidx] if pidx >= 0 else None
                    w_rs = []
                    if pv:
                        w_rs.append((pe, pv["n_bc"]))
                    S.op("vector", lambda e, slot=slot: e.reciprocal(
                        rs_sb[0:1, 0:512], orawA[slot][64:65, :]), sem=dv,
                        waits=w_rs)
                    S.op("vector", lambda e, slot=slot: e.reciprocal(
                        rs_sb[0:1, 512:1024], orawB[slot][64:65, :]), sem=dv)
                    n_rs = dv.n
                    w_bc = [(dv, n_rs)]
                    if pv:
                        w_bc.append((ac, pv["n_bcs"]))
                    S.op("tensor", lambda e: e.matmul(
                        nA[:, :], ones64_sb[:, :], rs_sb[0:1, 0:512], start=True,
                        stop=True), sem=pe, waits=w_bc)
                    S.op("tensor", lambda e: e.matmul(
                        nB[:, :], ones64_sb[:, :], rs_sb[0:1, 512:1024], start=True,
                        stop=True), sem=pe)
                    n_bc = pe.n
                    w_bcs = [(pe, n_bc)]
                    if pv:
                        w_bcs.append((dv, pv["n_of"]))
                    S.op("scalar", lambda e: e.activation(bca_sb[:, :], nA[:, :],
                                                          AF.Copy),
                         sem=ac, waits=w_bcs)
                    S.op("scalar", lambda e: e.activation(bcb_sb[:, :], nB[:, :],
                                                          AF.Copy), sem=ac)
                    n_bcs = ac.n
                    wof = [(ac, n_bcs)]
                    if len(of_dma) >= 2:
                        wof.append(of_dma[-2])
                    S.op("vector", lambda e, slot=slot: e.tensor_mul(
                        of_sb[slot][0:64, :], orawA[slot][0:64, :], bca_sb[:, :]),
                        sem=dv, waits=wof)
                    S.op("vector", lambda e, slot=slot: e.tensor_mul(
                        of_sb[slot][64:128, :], orawB[slot][0:64, :], bcb_sb[:, :]),
                        sem=dv)
                    n_of = dv.n
                    entry["n_of"] = n_of
                    entry["n_bcs"] = n_bcs
                    entry["n_bc"] = n_bc

                    tg0 = 32 * s + 4 * qg
                    recv0 = tg0 % 8
                    slot_idx = tg0 // 8
                    cct = cc2h["a_in"] if slot_idx < 8 else cc2h["b_in"]
                    si = slot_idx % 8
                    ofs = sems["of0"] if slot == 0 else sems["of1"]
                    S.dma("sync", lambda e, recv0=recv0, si=si, cct=cct,
                          slot=slot: e.dma_start(
                        out=cct.ap()[recv0:recv0 + 4, :,
                                     si * 128:(si + 1) * 128]
                        .rearrange("i p q -> p i q"),
                        in_=of_sb[slot].ap().rearrange("p (i q) -> p i q", q=128)),
                        ofs, waits=[(dv, n_of)])
                    of_dma.append((ofs, ofs.n))

                pending_norm.append(emit_norm)
                S.maybe_bar(sems["bar"], 1)
        if pending_norm:
            pending_norm.pop()()



def _p5_oproj(nc, S, sems, consts, cc2a_out, ccA_done, cc2b_out, ccB_done,
              wout_in):
    dS, dG, pe, ac, dv, cc = (sems[n] for n in ("dS", "dG", "pe", "ac", "dv", "cc"))
    resid, rank_sb = consts["resid"], consts["rank"]

    with ExitStack() as pst:
        ofall = pst.enter_context(nc.sbuf_tensor("p5_ofall", [128, NCORE * 2048], BF16))
        wout_bf = pst.enter_context(nc.sbuf_tensor("p5_wout", [128, 8 * D], BF16))

        S.dma("gpsimd", lambda e: e.dma_start(
            out=wout_bf.ap().rearrange("p (k n) -> p k n", n=D),
            in_=wout_in.ap().rearrange("(k p) n -> p k n", p=128)), dG)
        n_w = dG.n

        rk_holder = {}

        def load_of(e, j, src, hoff):
            if "rk" not in rk_holder:
                rk_holder["rk"] = nc.values_load(
                    rank_sb[:, :], engines=[SP], min_val=0, max_val=7,
                    skip_runtime_bounds_check=True)
            rk = rk_holder["rk"]
            return e.dma_start(
                out=ofall.ap()[:, j * 2048 + hoff:j * 2048 + hoff + 1024],
                in_=src.ap()[j, DynSlice(rk, 1)].rearrange("one p q -> p (one q)"))

        n_ofh = {}
        ev = []

        def emit_half(hi, src, ccdone, py):
            for j in range(NCORE):
                S.dma("sync", lambda e, j=j, src=src: load_of(
                    e, j, src, hi * 1024), dS,
                    waits=([(cc, ccdone)] if j == 0 else []))
            n_ofh[hi] = dS.n
            for t in range(8 * hi, 8 * hi + 8):
                ps = py[t % 2]
                w = [(dS, n_ofh[hi]), (dG, n_w)]
                if len(ev) >= 2:
                    w.append((dv, ev[-2]))
                for j in range(NCORE):
                    for half in range(2):
                        S.op("tensor", lambda e, t=t, j=j, half=half, ps=ps: e.matmul(
                            ps[:, half * 512:(half + 1) * 512],
                            ofall[:, j * 2048 + t * 128: j * 2048 + t * 128 + 128],
                            wout_bf[:, j * D + half * 512: j * D + half * 512 + 512],
                            start=(j == 0), stop=(j == NCORE - 1)),
                            sem=(pe if (j == NCORE - 1 and half == 1) else None),
                            waits=(w if (j == 0 and half == 0) else []))
                n_mm = pe.n
                for half in range(2):
                    S.op("vector", lambda e, t=t, ps=ps, half=half: e.tensor_add(
                        resid[:, t * D + 512 * half:t * D + 512 * half + 512],
                        resid[:, t * D + 512 * half:t * D + 512 * half + 512],
                        ps[:, 512 * half:512 * half + 512]),
                        sem=dv, waits=[(pe, n_mm)])
                ev.append(dv.n)

        with nc.psum_tensor("p5_py0", [128, 1024], F32) as py0, \
             nc.psum_tensor("p5_py1", [128, 1024], F32) as py1:
            py = [py0, py1]
            emit_half(0, cc2a_out, ccA_done, py)
            emit_half(1, cc2b_out, ccB_done, py)


def _store_out(nc, S, sems, resid, out):
    dS = sems["dS"]
    S.dma("sync", lambda e: e.dma_start(
        out=out.ap().rearrange("(t p) c -> p t c", p=128),
        in_=resid.ap().rearrange("p (t c) -> p t c", c=D)), dS)
    n_done = dS.n
    S.wait("gpsimd", dS, n_done)



def kernel(L0_attn_x, L0_attn_ln_gamma, L0_attn_ln_beta, L0_attn_w_qkv,
           L0_attn_w_out, L0_attn_qkscale, L0_ffn_w_gate, L0_ffn_w_up,
           L0_ffn_w_down):
    x = np.ascontiguousarray(np.asarray(L0_attn_x, dtype=np.float32)).reshape(128, 128, D)
    gamma = np.asarray(L0_attn_ln_gamma, dtype=np.float32)
    beta = np.asarray(L0_attn_ln_beta, dtype=np.float32)
    wqkv = np.asarray(L0_attn_w_qkv, dtype=np.float32)
    w_out = np.ascontiguousarray(np.asarray(L0_attn_w_out, dtype=np.float32))
    qks = np.asarray(L0_attn_qkscale, dtype=np.float32).reshape(1, 1)
    wgT = np.ascontiguousarray(np.asarray(L0_ffn_w_gate, dtype=np.float32).T)
    wuT = np.ascontiguousarray(np.asarray(L0_ffn_w_up, dtype=np.float32).T)
    wdT = np.ascontiguousarray(np.asarray(L0_ffn_w_down, dtype=np.float32).T)

    cos, sin_eff, partner = _rope_consts()
    g_t = np.ascontiguousarray(np.broadcast_to(gamma, (128, D)))
    b_t = np.ascontiguousarray(np.broadcast_to(beta, (128, D)))

    in_maps = []
    for c in range(NCORE):
        qcols = np.arange(128 * c, 128 * c + 128)
        in_maps.append(dict(
            x=np.ascontiguousarray(x[c::8].reshape(2048, D)),
            g_t=g_t, b_t=b_t,
            wq_m=np.ascontiguousarray(wqkv[:, qcols]),
            wq_r=np.ascontiguousarray(wqkv[:, partner[qcols]]),
            wk_m=np.ascontiguousarray(wqkv[:, D + qcols]),
            wk_r=np.ascontiguousarray(wqkv[:, D + partner[qcols]]),
            wv_m=np.ascontiguousarray(wqkv[:, 2 * D + qcols]),
            cos_q=np.ascontiguousarray(np.broadcast_to(cos[qcols], (128, 128))),
            sin_q=np.ascontiguousarray(np.broadcast_to(sin_eff[qcols], (128, 128))),
            cos_k=np.ascontiguousarray(np.broadcast_to(cos[qcols], (128, 128))),
            sin_k=np.ascontiguousarray(np.broadcast_to(sin_eff[qcols], (128, 128))),
            w_out=w_out, wgT=wgT, wuT=wuT, wdT=wdT, qks=qks,
            rank=np.array([[c]], dtype=np.int32),
        ))

    nc = build_program()
    res = run_bass_kernel_spmd(nc, in_maps, core_ids=list(range(NCORE)))
    y = np.empty((128, 128, D), dtype=np.float32)
    for c in range(NCORE):
        y[c::8] = np.asarray(res.results[c]["out"]).reshape(NT, 128, D)
    return y.reshape(NSEQ, SEQ, D)
